# version 02
# baseline (speedup 1.0000x reference)
import contextlib
import os
os.environ.setdefault("NEURON_SCRATCHPAD_PAGE_SIZE", "640")
import numpy as np
import ml_dtypes
import concourse.bass as bass
import concourse.mybir as mybir
from concourse.bass_utils import run_bass_kernel_spmd
from concourse.bass import IndirectOffsetOnAxis

F32 = mybir.dt.float32
BF16 = mybir.dt.bfloat16
I32 = mybir.dt.int32
AF = mybir.ActivationFunctionType
ALU = mybir.AluOpType
BF = ml_dtypes.bfloat16

NCORES = 8
B = 2
T = 8192
D = 2048
NTOK = B * T
TC = NTOK // NCORES
D_INNER = 4096
D_STATE = 128
NHEADS = 64
HEADDIM = 64
CONV_DIM = D_INNER + 2 * D_STATE
D_IN_PROJ = 2 * D_INNER + 2 * D_STATE + NHEADS
CHUNK = 128
LN_EPS = 1e-5

_PASSTHRU = {"dma_start", "dma_start_transpose", "indirect_dma_start",
             "collective_compute", "wait_ge", "nop", "engine_nop"}


class Eng:

    def __init__(self, ctx, nc, eng, name):
        self._e = eng
        self.h = ctx.enter_context(nc.semaphore(name))
        self.n = 0

    def wait(self, other, n=None):
        self._e.wait_ge(other.h, other.n if n is None else n)

    def raw(self):
        return self._e

    def __getattr__(self, name):
        f = getattr(self._e, name)
        if name in _PASSTHRU or name.startswith("_"):
            return f

        def wrap(*a, **k):
            if self.n:
                self._e.wait_ge(self.h, self.n)
            inst = f(*a, **k)
            self.n += 1
            inst.then_inc(self.h, 1)
            return inst
        return wrap


class Sem:

    def __init__(self, ctx, nc, name):
        self.h = ctx.enter_context(nc.semaphore(name))
        self.n = 0
        self.base = 0


class SemView:

    def __init__(self, core):
        self._c = core
        self.base = core.n

    @property
    def h(self):
        return self._c.h

    @property
    def n(self):
        return self._c.n

    @n.setter
    def n(self, v):
        self._c.n = v


class Dma:

    def __init__(self, ctx, nc, eng, name):
        self._e = eng
        self._ctx = ctx
        self._nc = nc
        self.q = Sem(ctx, nc, name)

    def wait(self, other, n=None):
        self._e.wait_ge(other.h, other.n if n is None else n)

    def start(self, out, in_, sem=None, **kw):
        sem = sem or self.q
        inst = self._e.dma_start(out=out, in_=in_, **kw)
        sem.n += 16
        inst.then_inc(sem.h, 16)
        return inst

    def indirect(self, out, out_offset, in_, in_offset, sem=None, **kw):
        sem = sem or self.q
        inst = self._e.indirect_dma_start(out=out, out_offset=out_offset,
                                          in_=in_, in_offset=in_offset, **kw)
        sem.n += 16
        inst.then_inc(sem.h, 16)
        return inst

    def transpose(self, out, in_, sem=None):
        sem = sem or self.q
        inst = self._e.dma_start(out=out, in_=in_, transpose=True)
        sem.n += 16
        inst.then_inc(sem.h, 16)
        return inst

    def collective(self, kind, op, replica_groups, ins, outs, sem=None):
        sem = sem or self.q
        inst = self._e.collective_compute(kind, op, replica_groups=replica_groups,
                                          ins=ins, outs=outs)
        sem.n += 1
        inst.then_inc(sem.h, 1)
        return inst


class Pe:

    def __init__(self, ctx, nc, name="pe"):
        self._e = nc.tensor
        self.h = ctx.enter_context(nc.semaphore(name))
        self.n = 0

    def wait(self, other, n=None):
        self._e.wait_ge(other.h, other.n if n is None else n)

    def matmul(self, *a, **k):
        return self._e.matmul(*a, **k)

    def transpose(self, *a, **k):
        return self._e.transpose(*a, **k)

    def done(self, inst):
        self.n += 1
        inst.then_inc(self.h, 1)
        return self.n


def _bf(x):
    return np.ascontiguousarray(np.asarray(x).astype(BF))


def _f32(x):
    return np.ascontiguousarray(np.asarray(x).astype(np.float32))





class Phase:
    def __init__(self, nc, ctx, prefix=""):
        self.nc = nc
        self.ctx = ctx
        self.prefix = prefix
        self.block = None
        self.E = None
        self.sems = {}
        self.dram = {}
        self.nbar = 0

    def nm(self, name):
        return self.prefix + name

    def sem(self, name):
        if name not in self.sems:
            self.sems[name] = Sem(self.ctx, self.nc, "S_" + name)
        return SemView(self.sems[name])

    def dio(self, name, shape, dt, kind):
        if name in self.dram:
            return self.dram[name]
        kindmap = {"in": "ExternalInput", "out": "ExternalOutput", "int": "Internal"}
        nm = name if kind != "int" else self.prefix + name
        ap = self.nc.dram_tensor(nm, list(shape), dt, kind=kindmap[kind]).ap()
        self.dram[name] = ap
        return ap

    def engines(self):
        if self.E is not None:
            return self.E
        self.block = self.ctx.enter_context(self.nc.Block())
        block = self.block

        class NS:
            pass
        E = NS()

        @block.vector
        def _(v):
            E.ve = Eng(self.ctx, self.nc, v, "E_ve")

        @block.scalar
        def _(s):
            E.se = Eng(self.ctx, self.nc, s, "E_se")

        @block.tensor
        def _(t):
            E.pe = Pe(self.ctx, self.nc, "E_pe")

        @block.gpsimd
        def _(g):
            E.gq = Dma(self.ctx, self.nc, g, "E_gq")
            E.ge = Eng(self.ctx, self.nc, g, "E_ge")

        @block.sync
        def _(s):
            E.sq = Dma(self.ctx, self.nc, s, "E_sq")
        self.E = E
        return E

    def barrier(self):
        E = self.E
        block = self.block
        self.nbar += 1
        engs = [E.ve, E.se, E.pe, E.ge]
        dsems = [s for s in self.sems.values()] + [E.gq.q, E.sq.q]
        targets = [(e.h, e.n) for e in engs] + [(s.h, s.n) for s in dsems]

        def emit(eng):
            for h, n in targets:
                if n:
                    eng.wait_ge(h, n)

        @block.vector
        def _(v):
            emit(v)

        @block.scalar
        def _(s):
            emit(s)

        @block.tensor
        def _(t):
            emit(t)

        @block.gpsimd
        def _(g):
            emit(g)

        @block.sync
        def _(s):
            emit(s)



def phase_mamba_front(P):
    nc = P.nc
    TE = TC + 3
    xt = P.dio("xt", [D, TE], F32, "in")
    wT = P.dio("wT", [D, D_IN_PROJ], F32, "in")
    cw = P.dio("cw", [128, 34 * 4], F32, "in")
    cb = P.dio("cb", [128, 34], F32, "in")
    dtb = P.dio("dtb", [NHEADS, 1], F32, "in")
    alog = P.dio("alog", [NHEADS, 1], F32, "in")
    ones_bf = P.dio("ones_bf", [128, 128], BF16, "in")

    zs_o = P.dio("zs", [D_INNER, TC], BF16, "out")
    xc_o = P.dio("xc", [D_INNER, TC], BF16, "out")
    bf_o = P.dio("Bf", [D_STATE, TC], BF16, "out")
    cf_o = P.dio("Cf", [D_STATE, TC], BF16, "out")
    dtv_o = P.dio("dtv", [NHEADS, TC], F32, "out")
    cuma_o = P.dio("cumA", [NHEADS, TC], F32, "out")
    xn_o = P.dio("xn", [D, TE], BF16, "out")

    NG = 5
    GOFF = [0, 512, 1024, 1536, 2048]
    GW = [512, 512, 512, 512, 3]
    NE = 67

    with contextlib.ExitStack() as ctx:
        sx = ctx.enter_context(nc.sbuf_tensor(P.nm("sx"), [128, 16 * TE], BF16))
        sx2 = ctx.enter_context(nc.sbuf_tensor(P.nm("sx2"), [128, 16 * 512], BF16))
        sm = ctx.enter_context(nc.sbuf_tensor(P.nm("sm"), [128, TE], F32))
        srs = ctx.enter_context(nc.sbuf_tensor(P.nm("srs"), [128, TE], F32))
        svar = ctx.enter_context(nc.sbuf_tensor(P.nm("svar"), [128, 512], F32))
        sw = ctx.enter_context(nc.sbuf_tensor(P.nm("sw"), [128, 2 * 2048], BF16))
        sasm = ctx.enter_context(nc.sbuf_tensor(P.nm("sasm"), [128, 2 * TE], F32))
        stmp = ctx.enter_context(nc.sbuf_tensor(P.nm("stmp"), [128, 2 * TC], F32))
        sconv = ctx.enter_context(nc.sbuf_tensor(P.nm("sconv"), [128, 2 * TC], BF16))
        sones = ctx.enter_context(nc.sbuf_tensor(P.nm("sones"), [128, 128], BF16))
        scw = ctx.enter_context(nc.sbuf_tensor(P.nm("scw"), [128, 34 * 4], F32))
        scb = ctx.enter_context(nc.sbuf_tensor(P.nm("scb"), [128, 34], F32))
        sdtb = ctx.enter_context(nc.sbuf_tensor(P.nm("sdtb"), [64, 1], F32))
        salog = ctx.enter_context(nc.sbuf_tensor(P.nm("salog"), [64, 1], F32))
        sdta = ctx.enter_context(nc.sbuf_tensor(P.nm("sdta"), [64, TE], F32))
        sdtb2 = ctx.enter_context(nc.sbuf_tensor(P.nm("sdtb2"), [64, TE], F32))
        sdtv = ctx.enter_context(nc.sbuf_tensor(P.nm("sdtv"), [64, TE], F32))
        sneg = ctx.enter_context(nc.sbuf_tensor(P.nm("sneg"), [64, 1], F32))
        scum = ctx.enter_context(nc.sbuf_tensor(P.nm("scum"), [64, TC], F32))
        szero = ctx.enter_context(nc.sbuf_tensor(P.nm("szero"), [64, 128], F32))

        pS1 = ctx.enter_context(nc.psum_tensor(P.nm("pS1"), [128, 512], F32))
        pS2 = ctx.enter_context(nc.psum_tensor(P.nm("pS2"), [128, 512], F32))
        pmm = ctx.enter_context(nc.psum_tensor(P.nm("pmm"), [128, 2 * 512], F32))

        E = P.engines()
        block = P.block
        ve, se, pe, gq, sq = E.ve, E.se, E.pe, E.gq, E.sq
        ldx = P.sem("ldx")
        ldwb = [P.sem("ldwb0"), P.sem("ldwb1")]
        ldc = P.sem("ldc")
        stxn = P.sem("stxn")
        stcb = [P.sem("stcb0"), P.sem("stcb1")]
        stfin = P.sem("stfin")

        @block.gpsimd
        def _(g):
            for k in range(16):
                gq.start(sx[:, k * TE:(k + 1) * TE], xt[k * 128:(k + 1) * 128, :], sem=ldx)
            gq.start(sw[:, 0:2048], wT[:, 0:128].rearrange("(k p) e -> p k e", p=128),
                     sem=ldwb[0])

        @block.sync
        def _(s):
            sq.start(scw[:, :], cw, sem=ldc)
            sq.start(scb[:, :], cb, sem=ldc)
            sq.start(sdtb[:], dtb, sem=ldc)
            sq.start(salog[:], alog, sem=ldc)
            sq.start(sones[:], ones_bf, sem=ldc)
        ldc_done = ldc.n

        v_xn_done = None
        for g in range(NG):
            o, w = GOFF[g], GW[g]

            @block.vector
            def _(v, g=g, o=o, w=w):
                if g == 0:
                    ve.wait(ldx)
                for k in range(16):
                    ve.tensor_mul(sx2[:, k * 512:k * 512 + w],
                                  sx[:, k * TE + o:k * TE + o + w],
                                  sx[:, k * TE + o:k * TE + o + w])
            vx2 = ve.n

            @block.tensor
            def _(t, g=g, o=o, w=w, vx2=vx2):
                pe.wait(ldc, ldc_done)
                pe.wait(ve, vx2)
                for k in range(16):
                    pe.matmul(pS1[:, 0:w], sones[:], sx[:, k * TE + o:k * TE + o + w],
                              start=(k == 0), stop=(k == 15))
                for k in range(16):
                    i = pe.matmul(pS2[:, 0:w], sones[:], sx2[:, k * 512:k * 512 + w],
                                  start=(k == 0), stop=(k == 15))
                pe.done(i)
            pest = pe.n

            @block.vector
            def _(v, g=g, o=o, w=w, pest=pest):
                ve.wait(pe, pest)
                ve.tensor_scalar_mul(sm[:, o:o + w], pS1[:, 0:w], 1.0 / D)
                ve.tensor_mul(svar[:, 0:w], sm[:, o:o + w], sm[:, o:o + w])
                ve.scalar_tensor_tensor(svar[:, 0:w], pS2[:, 0:w], 1.0 / D,
                                        svar[:, 0:w], ALU.mult, ALU.subtract)
                ve.tensor_scalar_add(svar[:, 0:w], svar[:, 0:w], LN_EPS)
            vnorm = ve.n

            @block.scalar
            def _(s, g=g, o=o, w=w, vnorm=vnorm):
                se.wait(ve, vnorm)
                se.activation(srs[:, o:o + w], svar[:, 0:w], AF.Ln)
                se.activation(srs[:, o:o + w], srs[:, o:o + w], AF.Exp, scale=-0.5)
            snorm = se.n

            @block.vector
            def _(v, g=g, o=o, w=w, snorm=snorm):
                ve.wait(se, snorm)
                for k in range(16):
                    sl = slice(k * TE + o, k * TE + o + w)
                    ve.tensor_sub(sx[:, sl], sx[:, sl], sm[:, o:o + w])
                    ve.tensor_mul(sx[:, sl], sx[:, sl], srs[:, o:o + w])
            v_xn_done = ve.n

        @block.sync
        def _(s):
            sq.wait(ve, v_xn_done)
            for k in range(16):
                sq.start(xn_o[k * 128:(k + 1) * 128, :], sx[:, k * TE:(k + 1) * TE], sem=stxn)

        pemm = {}
        sev = {}
        sev_tile = {}
        vcv = {}
        scv = {}

        for e in range(NE):
            rows = 64 if e == 66 else 128
            pp = e % 2
            ppn = (e + 1) % 2

            @block.gpsimd
            def _(g, e=e, ppn=ppn):
                if e + 1 < NE:
                    nrows = 64 if e + 1 == 66 else 128
                    if e >= 1:
                        gq.wait(pe, pemm[(e - 1, NG - 1)])
                    gq.start(sw[:, ppn * 2048:ppn * 2048 + 16 * nrows],
                             wT[:, (e + 1) * 128:(e + 1) * 128 + nrows]
                             .rearrange("(k p) e -> p k e", p=128), sem=ldwb[ppn])

            for g in range(NG):
                o, w = GOFF[g], GW[g]
                idx = e * NG + g
                bank = idx % 2

                @block.tensor
                def _(t, e=e, rows=rows, pp=pp, g=g, o=o, w=w, idx=idx, bank=bank):
                    if g == 0:
                        pe.wait(ldwb[e % 2], ldwb[e % 2].base + 16 * (e // 2 + 1))
                        if e == 0:
                            pe.wait(ve, v_xn_done)
                    if idx >= 2:
                        pidx = idx - 2
                        pe.wait(se, sev[(pidx // NG, pidx % NG)])
                    for k in range(16):
                        i = pe.matmul(pmm[0:rows, bank * 512:bank * 512 + w],
                                      sw[:, pp * 2048 + k * rows: pp * 2048 + (k + 1) * rows],
                                      sx[:, k * TE + o:k * TE + o + w],
                                      start=(k == 0), stop=(k == 15))
                    pemm[(e, g)] = pe.done(i)

                @block.scalar
                def _(s, e=e, rows=rows, pp=pp, g=g, o=o, w=w, idx=idx, bank=bank):
                    if g == 0:
                        if e < 32 and e >= 2:
                            se.wait(stcb[e % 2], stcb[e % 2].base + 16 * (e // 2))
                        if 34 <= e < 66:
                            se.wait(ve, vcv[e - 2])
                    se.wait(pe, pemm[(e, g)])
                    if e < 32:
                        lo = max(o, 3) - 3
                        hi = o + w - 3
                        se.activation(sconv[:, pp * TC + lo:pp * TC + hi],
                                      pmm[:, bank * 512 + (lo + 3 - o):bank * 512 + (hi + 3 - o)],
                                      AF.Silu)
                    elif e == 66:
                        se.copy(sdta[0:64, o:o + w], pmm[0:64, bank * 512:bank * 512 + w])
                    else:
                        se.copy(sasm[:, pp * TE + o:pp * TE + o + w],
                                pmm[:, bank * 512:bank * 512 + w])
                    sev[(e, g)] = se.n
            sev_tile[e] = se.n

            if e < 32:
                @block.sync
                def _(s, e=e, pp=pp):
                    sq.wait(se, sev_tile[e])
                    sq.start(zs_o[e * 128:(e + 1) * 128, :], sconv[:, pp * TC:(pp + 1) * TC],
                             sem=stcb[pp])
            elif e < 66:
                ct = e - 32

                @block.vector
                def _(v, e=e, pp=pp, ct=ct):
                    ve.wait(se, sev_tile[e])
                    if e >= 34:
                        ve.wait(se, scv[e - 2])
                    a = sasm[:, pp * TE:(pp + 1) * TE]
                    tb = stmp[:, pp * TC:(pp + 1) * TC]
                    ve.tensor_scalar(tb, a[:, 0:TC], scw[:, 4 * ct:4 * ct + 1],
                                     scb[:, ct:ct + 1], ALU.mult, ALU.add)
                    for k in range(1, 4):
                        ve.scalar_tensor_tensor(tb, a[:, k:TC + k], scw[:, 4 * ct + k:4 * ct + k + 1],
                                                tb, ALU.mult, ALU.add)
                    vcv[e] = ve.n

                @block.scalar
                def _(s, e=e, pp=pp):
                    se.wait(ve, vcv[e])
                    if e >= 2:
                        se.wait(stcb[e % 2], stcb[e % 2].base + 16 * (e // 2))
                    se.activation(sconv[:, pp * TC:(pp + 1) * TC],
                                  stmp[:, pp * TC:(pp + 1) * TC], AF.Silu)
                    scv[e] = se.n

                @block.sync
                def _(s, e=e, pp=pp):
                    sq.wait(se, scv[e])
                    if e < 64:
                        dst = xc_o[(e - 32) * 128:(e - 31) * 128, :]
                    elif e == 64:
                        dst = bf_o
                    else:
                        dst = cf_o
                    sq.start(dst, sconv[:, pp * TC:(pp + 1) * TC], sem=stcb[pp])

        @block.scalar
        def _(s):
            se.wait(ldc, ldc_done)
            se.activation(sdtb2[0:64, :], sdta[0:64, :], AF.Abs, bias=sdtb[:])
            se.activation(sdtb2[0:64, :], sdtb2[0:64, :], AF.Exp, scale=-1.0)
            se.activation(sdtb2[0:64, :], sdtb2[0:64, :], AF.Ln, bias=1.0)
            se.activation(sdta[0:64, :], sdta[0:64, :], AF.Relu, bias=sdtb[:])
            se.activation(sneg[:], salog[:], AF.Exp)
        s_dt = se.n

        @block.vector
        def _(v):
            ve.wait(se, s_dt)
            ve.tensor_add(sdtv[0:64, :], sdta[0:64, :], sdtb2[0:64, :])
            ve.tensor_scalar_mul(sneg[:], sneg[:], -1.0)
            ve.memset(szero[:], 0.0)
            ve.tensor_scalar_mul(sdtb2[0:64, 0:TC], sdtv[0:64, 3:3 + TC], sneg[:])
            for c in range(TC // CHUNK):
                ve.tensor_tensor_scan(scum[:, c * CHUNK:(c + 1) * CHUNK],
                                      sdtb2[0:64, c * CHUNK:(c + 1) * CHUNK],
                                      szero[:], 0.0, ALU.add, ALU.add)
        v_dt = ve.n

        @block.sync
        def _(s):
            sq.wait(ve, v_dt)
            sq.start(dtv_o, sdtv[0:64, 3:3 + TC], sem=stfin)
            sq.start(cuma_o, scum[0:64, :], sem=stfin)
            for sem in (stfin, stxn, stcb[0], stcb[1], ldc):
                s.wait_ge(sem.h, sem.n)

        @block.gpsimd
        def _(g):
            for sem in (ldx, ldwb[0], ldwb[1]):
                g.wait_ge(sem.h, sem.n)



def host_mamba_front(u_flat, in_proj_w, conv_w, conv_b, dt_bias, A_log):
    wT = _f32(in_proj_w.T)
    ones_bf = np.ones((128, 128), dtype=BF)
    cw_t = _f32(conv_w.reshape(34, 128, 4).transpose(1, 0, 2).reshape(128, 136))
    cb_t = _f32(conv_b.reshape(34, 128).T)
    in_maps = []
    for c in range(NCORES):
        s0 = c * TC
        bstart = (s0 // T) * T
        nh = min(3, s0 - bstart)
        ext = np.zeros((TC + 3, D), np.float32)
        if nh:
            ext[3 - nh:3] = u_flat[s0 - nh:s0]
        ext[3:] = u_flat[s0:s0 + TC]
        in_maps.append({
            "xt": _f32(ext.T),
            "wT": wT,
            "cw": cw_t,
            "cb": cb_t,
            "dtb": _f32(dt_bias.reshape(-1, 1)),
            "alog": _f32(A_log.reshape(-1, 1)),
            "ones_bf": ones_bf,
        })
    return in_maps




def phase_ssd(P):
    nc = P.nc
    NCH = TC // CHUNK
    HP = D_INNER
    xc_i = P.dio("xc", [D_INNER, TC], BF16, "in")
    b_fm = P.dio("Bf", [D_STATE, TC], BF16, "in")
    c_fm = P.dio("Cf", [D_STATE, TC], BF16, "in")
    dtv_i = P.dio("dtv", [NHEADS, TC], F32, "in")
    cuma_i = P.dio("cumA", [NHEADS, TC], F32, "in")
    idnf = P.dio("idnf", [128, 128], F32, "in")
    drow = P.dio("drow", [1, 64], F32, "in")
    trilm = P.dio("trilm", [128, 128], F32, "in")
    onecol = P.dio("onecol", [1, 128], F32, "in")
    bmask = P.dio("bmask", [8, 8], F32, "in")
    pmask = P.dio("pmask", [1, 8], F32, "in")

    y_o = P.dio("y_pm", [TC, HP], BF16, "out")

    cc_S = P.dio("cc_S", [D_STATE, HP], F32, "int")
    cc_sl = P.dio("cc_sl", [1, 64], F32, "int")
    ag_S = P.dio("ag_S", [NCORES * D_STATE, HP], F32, "int")
    ag_sl = P.dio("ag_sl", [NCORES, 64], F32, "int")

    with contextlib.ExitStack() as ctx:
        sB_pm = ctx.enter_context(nc.sbuf_tensor(P.nm("sB_pm"), [128, NCH * 128], BF16))
        sB_fm = ctx.enter_context(nc.sbuf_tensor(P.nm("sB_fm"), [128, TC], BF16))
        sC_fm = ctx.enter_context(nc.sbuf_tensor(P.nm("sC_fm"), [128, TC], BF16))
        sdtv = ctx.enter_context(nc.sbuf_tensor(P.nm("sdtv"), [128, NCH * 64], F32))
        scuma = ctx.enter_context(nc.sbuf_tensor(P.nm("scuma"), [128, NCH * 64], F32))
        sculc = ctx.enter_context(nc.sbuf_tensor(P.nm("sculc"), [1, NCH * 64], F32))
        sculh = ctx.enter_context(nc.sbuf_tensor(P.nm("sculh"), [1, 64 * NCH], F32))
        sdtvf = ctx.enter_context(nc.sbuf_tensor(P.nm("sdtvf"), [64, TC], F32))
        scumaf = ctx.enter_context(nc.sbuf_tensor(P.nm("scumaf"), [64, TC], F32))
        sidnf = ctx.enter_context(nc.sbuf_tensor(P.nm("sidnf"), [128, 128], F32))
        srowp = ctx.enter_context(nc.sbuf_tensor(P.nm("srowp"), [1, 64 * 128], F32))
        sdrow = ctx.enter_context(nc.sbuf_tensor(P.nm("sdrow"), [1, 64], F32))
        stril = ctx.enter_context(nc.sbuf_tensor(P.nm("stril"), [128, 128], F32))
        sonec = ctx.enter_context(nc.sbuf_tensor(P.nm("sonec"), [1, 128], F32))
        sbm = ctx.enter_context(nc.sbuf_tensor(P.nm("sbm"), [8, 8], F32))
        spm = ctx.enter_context(nc.sbuf_tensor(P.nm("spm"), [1, 8], F32))
        sx = ctx.enter_context(nc.sbuf_tensor(P.nm("sx"), [128, 2 * HP], BF16))
        sxdt = ctx.enter_context(nc.sbuf_tensor(P.nm("sxdt"), [128, HP], BF16))
        sxdt2 = ctx.enter_context(nc.sbuf_tensor(P.nm("sxdt2"), [128, HP], BF16))
        sS = ctx.enter_context(nc.sbuf_tensor(P.nm("sS"), [128, HP], F32))
        sSb = ctx.enter_context(nc.sbuf_tensor(P.nm("sSb"), [128, HP], BF16))
        sdec = ctx.enter_context(nc.sbuf_tensor(P.nm("sdec"), [128, 64], F32))
        sds = ctx.enter_context(nc.sbuf_tensor(P.nm("sds"), [128, 64], F32))
        sdd = ctx.enter_context(nc.sbuf_tensor(P.nm("sdd"), [128, 64], F32))
        sDbc = ctx.enter_context(nc.sbuf_tensor(P.nm("sDbc"), [128, 64], F32))
        seA = ctx.enter_context(nc.sbuf_tensor(P.nm("seA"), [128, 64], F32))
        sGtm = ctx.enter_context(nc.sbuf_tensor(P.nm("sGtm"), [128, 128], BF16))
        sLt = ctx.enter_context(nc.sbuf_tensor(P.nm("sLt"), [128, 1024], F32))
        sMt = ctx.enter_context(nc.sbuf_tensor(P.nm("sMt"), [128, 1024], BF16))
        syo = ctx.enter_context(nc.sbuf_tensor(P.nm("syo"), [128, 512], F32))
        sy = ctx.enter_context(nc.sbuf_tensor(P.nm("sy"), [128, 3 * 512], BF16))
        ssum = ctx.enter_context(nc.sbuf_tensor(P.nm("ssum"), [1, 64], F32))
        srow = ctx.enter_context(nc.sbuf_tensor(P.nm("srow"), [1, 64], F32))
        sSin = ctx.enter_context(nc.sbuf_tensor(P.nm("sSin"), [128, HP], F32))
        scoef = ctx.enter_context(nc.sbuf_tensor(P.nm("scoef"), [128, 64], F32))
        sfac = ctx.enter_context(nc.sbuf_tensor(P.nm("sfac"), [128, 64], F32))
        sagsl = ctx.enter_context(nc.sbuf_tensor(P.nm("sagsl"), [8, 64], F32))
        scexp = ctx.enter_context(nc.sbuf_tensor(P.nm("scexp"), [128, 64], F32))
        sagS = sx[:, 0:2 * HP].bitcast(F32)
        sSinb = sxdt[:, 0:HP]
        stmp = ctx.enter_context(nc.sbuf_tensor(P.nm("stmp"), [128, 2048], F32))

        pst = ctx.enter_context(nc.psum_tensor(P.nm("pst"), [128, 1024], F32))
        pLt = ctx.enter_context(nc.psum_tensor(P.nm("pLt"), [128, 1024], F32))
        pYd = ctx.enter_context(nc.psum_tensor(P.nm("pYd"), [128, 512], F32))
        pYo = ctx.enter_context(nc.psum_tensor(P.nm("pYo"), [128, 512], F32))
        pbcC = ctx.enter_context(nc.psum_tensor(P.nm("pbcC"), [128, 512], F32))
        pGt = ctx.enter_context(nc.psum_tensor(P.nm("pGt"), [128, 512], F32))

        E = P.engines()
        block = P.block
        ve, se, pe, gq, ge, sq = E.ve, E.se, E.pe, E.gq, E.ge, E.sq
        K = {}
        ldc = P.sem("ldc2")
        ldxb = [P.sem("ldxb0"), P.sem("ldxb1")]
        styp = [P.sem("styp0"), P.sem("styp1")]
        stS = P.sem("stS")
        agc = P.sem("agc")
        agl = P.sem("agl")
        ags = P.sem("ags")
        sty2 = [P.sem("sty2a"), P.sem("sty2b")]
        ldrow = P.sem("ldrow")

        @block.sync
        def _(s):
            for k in range(NCH):
                sq.transpose(sB_pm[:, k * 128:(k + 1) * 128],
                             b_fm[:, k * 128:(k + 1) * 128], sem=ldc)
            sq.start(sB_fm[:, :], b_fm, sem=ldc)
            sq.start(sC_fm[:, :], c_fm, sem=ldc)
            sq.start(sdtvf[:, :], dtv_i, sem=ldc)
            sq.start(scumaf[:, :], cuma_i, sem=ldc)
            sq.start(sidnf[:, :], idnf, sem=ldc)
            for k in range(NCH):
                sq.start(sculc[0:1, k * 64:(k + 1) * 64].unsqueeze(2),
                         cuma_i[:, k * 128 + 127:k * 128 + 128].rearrange("h q -> () h q"),
                         sem=ldc, allow_slow_non_contiguous=True)
                sq.start(sculh[0:1, :].rearrange("o (h c) -> o h c", c=NCH)[:, :, k:k + 1],
                         cuma_i[:, k * 128 + 127:k * 128 + 128].rearrange("h q -> () h q"),
                         sem=ldc, allow_slow_non_contiguous=True)
            sq.start(sdrow[:, :], drow, sem=ldc)
            sq.start(stril[:, :], trilm, sem=ldc)
            sq.start(sonec[:, :], onecol, sem=ldc)
            sq.start(sbm[:, :], bmask, sem=ldc)
            sq.start(spm[:, :], pmask, sem=ldc)
        ld_all = ldc.n

        for k in range(NCH):
            @block.tensor
            def _(t, k=k):
                if k == 0:
                    pe.wait(ldc, ld_all)
                else:
                    pe.wait(ve, K[("cvt", k - 1)])
                i = pe.transpose(pGt[:, 256:320], sdtvf[:, k * 128:(k + 1) * 128],
                                 sidnf[0:64, 0:64])
                pe.done(i)
                i = pe.transpose(pGt[:, 320:384], scumaf[:, k * 128:(k + 1) * 128],
                                 sidnf[0:64, 0:64])
                K[("tcma", k)] = pe.done(i)

            @block.vector
            def _(v, k=k):
                ve.wait(pe, K[("tcma", k)])
                ve.tensor_copy(sdtv[:, k * 64:(k + 1) * 64], pGt[:, 256:320])
                ve.tensor_copy(scuma[:, k * 64:(k + 1) * 64], pGt[:, 320:384])
            K[("cvt", k)] = ve.n
        cvt_done = ve.n

        @block.sync
        def _(s):
            sq.transpose(sx[:, 0:HP], xc_i[:, 0:128], sem=ldxb[0])

        @block.tensor
        def _(t):
            pe.wait(ldc, ld_all)
            pe.wait(ve, cvt_done)
            i = pe.matmul(pGt[:, 0:64], sonec[:, :], sdrow[:, :], start=True, stop=True)
            pe.done(i)
        pe_dbc = pe.n

        @block.vector
        def _(v):
            ve.wait(pe, pe_dbc)
            ve.tensor_copy(sDbc[:, :], pGt[:, 0:64])
            ve.memset(sS[:, :], 0.0)
        v_init = ve.n

        for c in range(NCH):
            pp = c % 2
            ppn = (c + 1) % 2
            cg = slice(c * 64, (c + 1) * 64)

            @block.sync
            def _(s, c=c, ppn=ppn):
                if c + 1 < NCH:
                    if c >= 1:
                        sq.wait(ve, K[(c - 1, "xdt")])
                        sq.wait(ge, K[(c - 1, 7, "gd")])
                    sq.transpose(sx[:, ppn * HP:(ppn + 1) * HP],
                                 xc_i[:, (c + 1) * 128:(c + 2) * 128], sem=ldxb[ppn])

            @block.gpsimd
            def _(g, c=c, ppn=ppn):
                if True:
                    pass
                if c > 0:
                    gq.wait(pe, K[(c - 1, 7, "lt")])
                gq.start(srowp[0:1, :].rearrange("o (h q) -> o h q", q=128),
                         cuma_i[:, (c) * 128:(c + 1) * 128].rearrange("h q -> () h q"),
                         sem=ldrow)

            @block.tensor
            def _(t, c=c):
                if c == 0:
                    pe.wait(ve, v_init)
                else:
                    pe.wait(se, K[(c - 1, "dec")])
                    pe.wait(ve, K[(c - 1, "ds")])
                i = pe.matmul(pbcC[:, 0:64], sonec[:, :],
                              sculc[:, c * 64:(c + 1) * 64], start=True, stop=True)
                K[(c, "bc")] = pe.done(i)

            @block.vector
            def _(v, c=c, cg=cg):
                ve.wait(pe, K[(c, "bc")])
                if c == 0:
                    ve.wait(ve, cvt_done) if False else None
                ve.tensor_sub(sds[:, :], pbcC[:, 0:64], scuma[:, cg])
            K[(c, "ds")] = ve.n

            @block.scalar
            def _(s, c=c, cg=cg):
                se.wait(ve, K[(c, "ds")])
                se.activation(sdec[:, :], pbcC[:, 0:64], AF.Exp)
                se.activation(sds[:, :], sds[:, :], AF.Exp)
                se.activation(seA[:, :], scuma[:, cg], AF.Exp)
            K[(c, "dec")] = se.n

            @block.vector
            def _(v, c=c, cg=cg, pp=pp):
                ve.wait(se, K[(c, "dec")])
                ve.tensor_mul(sdd[:, :], sds[:, :], sdtv[:, cg])
                if c == 0:
                    ve.wait(ldxb[0], 16)
                else:
                    ve.wait(ldxb[pp], ldxb[pp].base + 16 * (c // 2 + 1))
                xin = sx[:, pp * HP:(pp + 1) * HP].rearrange("p (h q) -> p h q", h=64)
                ve.tensor_tensor(sxdt.rearrange("p (h q) -> p h q", h=64), xin,
                                 sdtv[:, cg].unsqueeze(2).broadcast_to((128, 64, 64)), ALU.mult)
                ve.tensor_tensor(sxdt2.rearrange("p (h q) -> p h q", h=64), xin,
                                 sdd[:, :].unsqueeze(2).broadcast_to((128, 64, 64)), ALU.mult)
            K[(c, "xdt")] = ve.n

            @block.scalar
            def _(s, c=c):
                se.wait(ve, K[(c, "xdt")])
                se.copy(sSb[:, :], sS[:, :])
            K[(c, "sbf")] = se.n

            @block.tensor
            def _(t, c=c):
                pe.wait(se, K[(c, "sbf")])
                if c == 0:
                    pe.wait(ve, cvt_done)
                if c > 0:
                    pe.wait(ve, K[(c - 1, "gtm")])
                i = pe.matmul(pGt[:, 128:256], sB_fm[:, c * 128:(c + 1) * 128],
                              sC_fm[:, c * 128:(c + 1) * 128], start=True, stop=True)
                K[(c, "gt")] = pe.done(i)

            @block.vector
            def _(v, c=c):
                ve.wait(pe, K[(c, "gt")])
                ve.tensor_mul(sGtm[:, :], pGt[:, 128:256], stril[:, :])
            K[(c, "gtm")] = ve.n

            for hg in range(8):
                idx = c * 8 + hg
                par = idx % 2
                hsl = slice(hg * 512, (hg + 1) * 512)

                @block.tensor
                def _(t, c=c, hg=hg):
                    if hg == 0:
                        pe.wait(ve, K[(c, "gtm")])
                        pe.wait(ldrow, ldrow.base + 16 * (c + 1))
                    prev = K.get((c, hg - 1, "ltsub") if hg else (c - 1, 7, "ltsub"))
                    if prev:
                        pe.wait(ve, prev)
                    for j in range(2):
                        i = pe.matmul(pLt[:, j * 512:(j + 1) * 512], sonec[:, :],
                                      srowp[0:1, hg * 1024 + j * 512:hg * 1024 + (j + 1) * 512],
                                      start=True, stop=True)
                    K[(c, hg, "lt")] = pe.done(i)

                @block.vector
                def _(v, c=c, hg=hg):
                    ve.wait(pe, K[(c, hg, "lt")])
                    ve.tensor_tensor(sLt.rearrange("p (h q) -> p h q", h=8),
                                     scuma[:, c * 64 + hg * 8:c * 64 + (hg + 1) * 8]
                                     .unsqueeze(2).broadcast_to((128, 8, 128)),
                                     pLt.rearrange("p (h q) -> p h q", h=8),
                                     ALU.subtract)
                K[(c, hg, "ltsub")] = ve.n

                @block.scalar
                def _(s, c=c, hg=hg):
                    se.wait(ve, K[(c, hg, "ltsub")])
                    if K.get((c, hg - 1, "mt") if hg else (c - 1, 7, "mt")):
                        se.wait(ve, K[(c, hg - 1, "mt") if hg else (c - 1, 7, "mt")])
                    se.activation(sLt[:, :], sLt[:, :], AF.Relu)
                    se.activation(sMt[:, :], sLt[:, :], AF.Exp, scale=-1.0)
                K[(c, hg, "srelu")] = se.n

                @block.vector
                def _(v, c=c, hg=hg):
                    ve.wait(se, K[(c, hg, "srelu")])
                    ve.tensor_tensor(sMt.rearrange("p (h q) -> p h q", h=8),
                                     sMt.rearrange("p (h q) -> p h q", h=8),
                                     sGtm[:, :].unsqueeze(1).broadcast_to((128, 8, 128)),
                                     ALU.mult)
                K[(c, hg, "mt")] = ve.n

                @block.tensor
                def _(t, c=c, hg=hg, hsl=hsl):
                    pe.wait(ve, K[(c, hg, "mt")])
                    prev_vy = K.get((c, hg - 1, "vy") if hg else (c - 1, 7, "vy"))
                    if prev_vy:
                        pe.wait(ve, prev_vy)
                    for h in range(8):
                        i = pe.matmul(pYd[:, h * 64:(h + 1) * 64], sMt[:, h * 128:(h + 1) * 128],
                                      sxdt[:, hg * 512 + h * 64:hg * 512 + (h + 1) * 64],
                                      start=True, stop=True)
                    pe.done(i)
                    i = pe.matmul(pYo[:, :], sC_fm[:, c * 128:(c + 1) * 128],
                                  sSb[:, hsl], start=True, stop=True)
                    K[(c, hg, "yo")] = pe.done(i)

                @block.gpsimd
                def _(g, c=c, hg=hg, pp=pp):
                    ge.wait(ve, K[(c, "xdt")])
                    prev_vy = K.get((c, hg - 1, "vy") if hg else (c - 1, 7, "vy"))
                    if prev_vy:
                        ge.wait(ve, prev_vy)
                    ge.tensor_tensor(sy[:, 1024:1536].rearrange("p (h q) -> p h q", h=8),
                                     sx[:, pp * HP + hg * 512:pp * HP + (hg + 1) * 512]
                                     .rearrange("p (h q) -> p h q", h=8),
                                     sDbc[:, hg * 8:(hg + 1) * 8].unsqueeze(2).broadcast_to((128, 8, 64)),
                                     ALU.mult)
                K[(c, hg, "gd")] = ge.n

                @block.vector
                def _(v, c=c, hg=hg, par=par, idx=idx):
                    ve.wait(pe, K[(c, hg, "yo")])
                    ve.wait(ge, K[(c, hg, "gd")])
                    if idx >= 2:
                        ve.wait(styp[par], styp[par].base + 16 * (idx // 2))
                    ve.tensor_tensor(syo.rearrange("p (h q) -> p h q", h=8),
                                     pYo[:, :].rearrange("p (h q) -> p h q", h=8),
                                     seA[:, hg * 8:(hg + 1) * 8].unsqueeze(2).broadcast_to((128, 8, 64)),
                                     ALU.mult)
                    ve.tensor_add(syo[:, :], syo[:, :], pYd[:, :])
                    ve.tensor_add(sy[:, par * 512:(par + 1) * 512], syo[:, :], sy[:, 1024:1536])
                K[(c, hg, "vy")] = ve.n

                @block.sync
                def _(s, c=c, hg=hg, par=par):
                    sq.wait(ve, K[(c, hg, "vy")])
                    sq.start(y_o[c * 128:(c + 1) * 128, hg * 512:(hg + 1) * 512],
                             sy[:, par * 512:(par + 1) * 512], sem=styp[par])

            for qt in range(4):
                hs = slice(qt * 1024, (qt + 1) * 1024)

                @block.tensor
                def _(t, c=c, qt=qt, hs=hs):
                    if qt == 0:
                        pe.wait(ve, K[(c, 7, "vy")])
                    else:
                        pe.wait(ve, K[(c, qt - 1, "supd")])
                    for j in range(2):
                        i = pe.matmul(pst[:, j * 512:(j + 1) * 512],
                                      sB_pm[:, c * 128:(c + 1) * 128],
                                      sxdt2[:, qt * 1024 + j * 512:qt * 1024 + (j + 1) * 512],
                                      start=True, stop=True)
                    K[(c, qt, "st")] = pe.done(i)

                @block.gpsimd
                def _(g, c=c, qt=qt, hs=hs):
                    ge.wait(se, K[(c, "sbf")])
                    if K.get((c, qt - 1, "supd") if qt else (c - 1, 3, "supd")):
                        ge.wait(ve, K[(c, qt - 1, "supd") if qt else (c - 1, 3, "supd")])
                    ge.tensor_tensor(stmp[:, 0:1024].rearrange("p (h q) -> p h q", h=16),
                                     sS[:, hs].rearrange("p (h q) -> p h q", h=16),
                                     sdec[:, qt * 16:(qt + 1) * 16].unsqueeze(2)
                                     .broadcast_to((128, 16, 64)), ALU.mult)
                K[(c, qt, "gs")] = ge.n

                @block.vector
                def _(v, c=c, qt=qt, hs=hs):
                    ve.wait(pe, K[(c, qt, "st")])
                    ve.wait(ge, K[(c, qt, "gs")])
                    ve.tensor_add(sS[:, hs], stmp[:, 0:1024], pst[:, :])
                K[(c, qt, "supd")] = ve.n

        @block.vector
        def _(v):
            ve.tensor_reduce(ssum[0:1, :], sculh[:, :].rearrange("p (h c) -> p h c", h=64),
                             mybir.AxisListType.X, ALU.add)
        v_sl = ve.n

        @block.sync
        def _(s):
            sq.wait(ve, v_sl)
            sq.wait(ve, K[(NCH - 1, 3, "supd")])
            sq.start(cc_S, sS[:, :], sem=stS)
            sq.start(cc_sl, ssum[0:1, :], sem=stS)

        @block.gpsimd
        def _(g):
            g.wait_ge(stS.h, stS.n)
            gq.collective("AllGather", ALU.bypass, [list(range(NCORES))],
                          [cc_S.opt()], [ag_S.opt()], sem=agc)
            gq.collective("AllGather", ALU.bypass, [list(range(NCORES))],
                          [cc_sl.opt()], [ag_sl.opt()], sem=agc)

        @block.sync
        def _(s):
            s.wait_ge(agc.h, agc.n)
            sq.start(sagsl[:, :], ag_sl, sem=agl)

        @block.vector
        def _(v):
            ve.wait(agl, agl.n)
            ve.memset(sSin[:, :], 0.0)
        v_sin0 = ve.n

        R = {}
        for j in range(NCORES):
            @block.tensor
            def _(t, j=j):
                if j == 0:
                    pe.wait(ve, v_sin0)
                else:
                    pe.wait(ve, R[(j - 1, "acc")])
                i = pe.matmul(pbcC[0:1, 0:64], sbm[:, j:j + 1], sagsl[:, :],
                              start=True, stop=True)
                R[(j, "log")] = pe.done(i)

            @block.vector
            def _(v, j=j):
                ve.wait(pe, R[(j, "log")])
                ve.tensor_copy(srow[0:1, :], pbcC[0:1, 0:64])
            R[(j, "vrow")] = ve.n

            @block.scalar
            def _(s, j=j):
                se.wait(ve, R[(j, "vrow")])
                se.activation(srow[0:1, :], srow[0:1, :], AF.Exp)
            R[(j, "sexp")] = se.n

            @block.vector
            def _(v, j=j):
                ve.wait(se, R[(j, "sexp")])
                ve.tensor_scalar_mul(srow[0:1, :], srow[0:1, :], spm[0:1, j:j + 1])
            R[(j, "vpm")] = ve.n

            @block.tensor
            def _(t, j=j):
                pe.wait(ve, R[(j, "vpm")])
                i = pe.matmul(pbcC[:, 64:128], sonec[:, :], srow[0:1, :], start=True, stop=True)
                R[(j, "bc")] = pe.done(i)

            @block.vector
            def _(v, j=j):
                ve.wait(pe, R[(j, "bc")])
                ve.tensor_copy(scexp[:, :], pbcC[:, 64:128])
            R[(j, "cexp")] = ve.n

            @block.sync
            def _(s, j=j):
                sq.wait(ve, R[(j, "cexp")])
                sq.start(sagS, ag_S[j * 128:(j + 1) * 128, :], sem=ags)
            R[(j, "lds")] = ags.n

            @block.vector
            def _(v, j=j):
                ve.wait(ags, R[(j, "lds")])
                ve.tensor_tensor(sagS.rearrange("p (h q) -> p h q", h=64),
                                 sagS.rearrange("p (h q) -> p h q", h=64),
                                 scexp[:, :].unsqueeze(2).broadcast_to((128, 64, 64)),
                                 ALU.mult)
                ve.tensor_add(sSin[:, :], sSin[:, :], sagS)
            R[(j, "acc")] = ve.n

        @block.scalar
        def _(s):
            se.wait(ve, R[(NCORES - 1, "acc")])
            se.copy(sSinb, sSin[:, :])
        s_sinb = se.n

        @block.vector
        def _(v):
            ve.wait(se, s_sinb)
            ve.memset(scoef[:, :], 1.0)
        v_coef0 = ve.n

        P = {}
        for c in range(NCH):
            @block.tensor
            def _(t, c=c):
                if c == 0:
                    pe.wait(ve, v_coef0)
                else:
                    pe.wait(ve, P[(c - 1, "coef")])
                    pe.wait(se, P[(c - 1, "dec")])
                i = pe.matmul(pbcC[:, 0:64], sonec[:, :],
                              sculc[:, c * 64:(c + 1) * 64], start=True, stop=True)
                P[(c, "bc")] = pe.done(i)

            @block.scalar
            def _(s, c=c):
                se.wait(pe, P[(c, "bc")])
                se.activation(sdec[:, :], pbcC[:, 0:64], AF.Exp)
                se.activation(seA[:, :], scuma[:, c * 64:(c + 1) * 64], AF.Exp)
            P[(c, "dec")] = se.n

            @block.vector
            def _(v, c=c):
                ve.wait(se, P[(c, "dec")])
                ve.tensor_mul(sfac[:, :], seA[:, :], scoef[:, :])
            P[(c, "fac")] = ve.n

            for hg in range(8):
                idx = c * 8 + hg
                par = idx % 2

                @block.tensor
                def _(t, c=c, hg=hg):
                    if hg == 0:
                        pe.wait(ve, P[(c, "fac")])
                    prev = P.get((c, hg - 1, "vy") if hg else (c - 1, 7, "vy"))
                    if prev:
                        pe.wait(ve, prev)
                    i = pe.matmul(pYo[:, :], sC_fm[:, c * 128:(c + 1) * 128],
                                  sSinb[:, hg * 512:(hg + 1) * 512], start=True, stop=True)
                    P[(c, hg, "mm")] = pe.done(i)

                @block.vector
                def _(v, c=c, hg=hg, par=par, idx=idx):
                    ve.wait(pe, P[(c, hg, "mm")])
                    if idx >= 2:
                        ve.wait(sty2[par], sty2[par].base + 16 * (idx // 2))
                    ve.tensor_tensor(sy[:, par * 512:(par + 1) * 512].rearrange("p (h q) -> p h q", h=8),
                                     pYo[:, :].rearrange("p (h q) -> p h q", h=8),
                                     sfac[:, hg * 8:(hg + 1) * 8].unsqueeze(2)
                                     .broadcast_to((128, 8, 64)), ALU.mult)
                P[(c, hg, "vy")] = ve.n

                @block.gpsimd
                def _(g, c=c, hg=hg, par=par):
                    ge.wait(ve, P[(c, hg, "vy")])
                    gq.start(y_o[c * 128:(c + 1) * 128, hg * 512:(hg + 1) * 512],
                             sy[:, par * 512:(par + 1) * 512], sem=sty2[par],
                             accum_op=ALU.add)

            @block.vector
            def _(v, c=c):
                ve.wait(se, P[(c, "dec")])
                ve.tensor_mul(scoef[:, :], scoef[:, :], sdec[:, :])
            P[(c, "coef")] = ve.n

        @block.sync
        def _(s):
            for sem in (styp[0], styp[1], stS, ldc, agl, ags):
                s.wait_ge(sem.h, sem.n)

        @block.gpsimd
        def _(g):
            for sem in (sty2[0], sty2[1], agc, ldxb[0], ldxb[1]):
                g.wait_ge(sem.h, sem.n)



def host_ssd(xc_all, bf_all, cf_all, dtv_all, cuma_all, Dvec):
    tril = (np.arange(128)[None, :] >= np.arange(128)[:, None]).astype(np.float32)
    onec = np.ones((1, 128), np.float32)
    drow = _f32(Dvec.reshape(1, 64))
    idnf = np.eye(128, dtype=np.float32)
    in_maps = []
    for c in range(NCORES):
        p = c % 4
        b0 = (c // 4) * 4
        bm = np.zeros((8, 8), np.float32)
        pm = np.zeros((1, 8), np.float32)
        for s in range(8):
            if b0 <= s < b0 + p:
                pm[0, s] = 1.0
                for k in range(s + 1, b0 + p):
                    bm[k, s] = 1.0
        in_maps.append({
            "xc": _bf(xc_all[c]),
            "Bf": _bf(bf_all[c]),
            "Cf": _bf(cf_all[c]),
            "dtv": _f32(dtv_all[c]),
            "cumA": _f32(cuma_all[c]),
            "idnf": idnf,
            "drow": drow,
            "trilm": tril, "onecol": onec,
            "bmask": bm, "pmask": pm,
        })
    return in_maps


def phase_mgate(P):
    nc = P.nc
    y_pm = P.dio("y_pm", [TC, D_INNER], BF16, "in")
    zs = P.dio("zs", [D_INNER, TC], BF16, "in")
    u_fm = P.dio("u_fm", [D, TC], F32, "in")
    woT = P.dio("woT", [D_INNER, D], F32, "in")
    normw = P.dio("normw", [128, 32], F32, "in")
    ones_bf = P.dio("ones_bf", [128, 128], BF16, "in")
    x1_o = P.dio("x1_fm", [D, TC], F32, "out")

    NG = 4
    GWD = 512
    NK = 32
    NO = 16

    with contextlib.ExitStack() as ctx:
        sg = ctx.enter_context(nc.sbuf_tensor(P.nm("sg"), [128, NK * GWD], BF16))
        sy = ctx.enter_context(nc.sbuf_tensor(P.nm("syb"), [128, 2 * GWD], BF16))
        sz = ctx.enter_context(nc.sbuf_tensor(P.nm("szb"), [128, 2 * GWD], BF16))
        sg2 = ctx.enter_context(nc.sbuf_tensor(P.nm("sg2"), [128, 2 * GWD], BF16))
        srms = ctx.enter_context(nc.sbuf_tensor(P.nm("srms"), [128, GWD], F32))
        snw = ctx.enter_context(nc.sbuf_tensor(P.nm("snw"), [128, 32], F32))
        sones = ctx.enter_context(nc.sbuf_tensor(P.nm("sones"), [128, 128], BF16))
        sw = ctx.enter_context(nc.sbuf_tensor(P.nm("sw"), [128, 2 * NK * 128], BF16))
        su = ctx.enter_context(nc.sbuf_tensor(P.nm("su"), [128, 2 * GWD], F32))
        sx1 = ctx.enter_context(nc.sbuf_tensor(P.nm("sx1"), [128, 2 * GWD], F32))

        pS = ctx.enter_context(nc.psum_tensor(P.nm("pS"), [128, 512], F32))
        pmm = ctx.enter_context(nc.psum_tensor(P.nm("pmm"), [128, 2 * 512], F32))

        E = P.engines()
        block = P.block
        ve, se, pe, gq, sq = E.ve, E.se, E.pe, E.gq, E.sq
        ldc = P.sem("ldc")
        ldyz = [P.sem("ldyz0"), P.sem("ldyz1")]
        ldw = [P.sem("ldw0"), P.sem("ldw1")]
        ldu = [P.sem("ldu0"), P.sem("ldu1")]
        stx = [P.sem("stx0"), P.sem("stx1")]

        @block.sync
        def _(s):
            sq.start(snw[:, :], normw, sem=ldc)
            sq.start(sones[:, :], ones_bf, sem=ldc)
        ldc_done = ldc.n

        K = {}
        nload = [0, 0]
        nwload = [0, 0]
        nuload = [0, 0]
        nxst = [0, 0]
        for grp in range(NG):
            go = grp * GWD
            for k in range(NK):
                pp = k % 2

                @block.sync
                def _(s, grp=grp, go=go, k=k, pp=pp):
                    if K.get(("g", grp, k - 2)) is not None:
                        sq.wait(ve, K[("g2", grp, k - 2)])
                    sq.transpose(sy[:, pp * GWD:(pp + 1) * GWD],
                                 y_pm[go:go + GWD, k * 128:(k + 1) * 128], sem=ldyz[pp])
                    sq.start(sz[:, pp * GWD:(pp + 1) * GWD],
                             zs[k * 128:(k + 1) * 128, go:go + GWD], sem=ldyz[pp])
                    nload[pp] += 32

                @block.vector
                def _(v, grp=grp, k=k, pp=pp):
                    ve.wait(ldyz[pp], ldyz[pp].base + 32 * ((grp * NK + k) // 2 + 1))
                    ve.tensor_mul(sg[:, k * GWD:(k + 1) * GWD],
                                  sy[:, pp * GWD:(pp + 1) * GWD],
                                  sz[:, pp * GWD:(pp + 1) * GWD])
                    K[("g", grp, k)] = ve.n
                    if k >= 2:
                        ve.wait(pe, K[("st", grp, k - 2)])
                    ve.tensor_mul(sg2[:, pp * GWD:(pp + 1) * GWD],
                                  sg[:, k * GWD:(k + 1) * GWD],
                                  sg[:, k * GWD:(k + 1) * GWD])
                    K[("g2", grp, k)] = ve.n

                @block.tensor
                def _(t, grp=grp, k=k, pp=pp):
                    pe.wait(ldc, ldc_done)
                    pe.wait(ve, K[("g2", grp, k)])
                    if grp > 0 and k == 0:
                        pe.wait(se, K[("rms", grp - 1)])
                    i = pe.matmul(pS[:, :], sones[:], sg2[:, pp * GWD:(pp + 1) * GWD],
                                  start=(k == 0), stop=(k == NK - 1))
                    K[("st", grp, k)] = pe.done(i)

            @block.vector
            def _(v, grp=grp):
                ve.wait(pe, K[("st", grp, NK - 1)])
                ve.tensor_scalar(srms[:, :], pS[:, :], 1.0 / D_INNER, LN_EPS,
                                 ALU.mult, ALU.add)
            K[("vrms", grp)] = ve.n

            @block.scalar
            def _(s, grp=grp):
                se.wait(ve, K[("vrms", grp)])
                se.activation(srms[:, :], srms[:, :], AF.Ln)
                se.activation(srms[:, :], srms[:, :], AF.Exp, scale=-0.5)
            K[("rms", grp)] = se.n

            for k in range(NK):
                @block.vector
                def _(v, grp=grp, k=k):
                    if k == 0:
                        ve.wait(se, K[("rms", grp)])
                    sl = sg[:, k * GWD:(k + 1) * GWD]
                    ve.tensor_scalar_mul(sl, sl, snw[:, k:k + 1])
                    ve.tensor_mul(sl, sl, srms[:, :])
                K[("gn", grp)] = ve.n

            for ot in range(NO):
                pp = ot % 2
                gi = grp * NO + ot

                @block.gpsimd
                def _(g, grp=grp, ot=ot, pp=pp, gi=gi):
                    if gi >= 2:
                        gq.wait(pe, K[("mm", gi - 2)])
                    gq.start(sw[:, pp * NK * 128:(pp + 1) * NK * 128],
                             woT[:, ot * 128:(ot + 1) * 128]
                             .rearrange("(k p) e -> p k e", p=128), sem=ldw[pp])
                    nwload[pp] += 16

                @block.sync
                def _(s, grp=grp, go=go, ot=ot, pp=pp, gi=gi):
                    if gi >= 2:
                        sq.wait(ve, K[("x1", gi - 2)])
                    sq.start(su[:, pp * GWD:(pp + 1) * GWD],
                             u_fm[ot * 128:(ot + 1) * 128, go:go + GWD], sem=ldu[pp])
                    nuload[pp] += 16

                @block.tensor
                def _(t, grp=grp, ot=ot, pp=pp, gi=gi):
                    pe.wait(ldw[pp], ldw[pp].base + 16 * (gi // 2 + 1))
                    if ot == 0:
                        pe.wait(ve, K[("gn", grp)])
                    if gi >= 2:
                        pe.wait(ve, K[("x1", gi - 2)])
                    for k in range(NK):
                        i = pe.matmul(pmm[:, pp * 512:pp * 512 + GWD],
                                      sw[:, pp * NK * 128 + k * 128:pp * NK * 128 + (k + 1) * 128],
                                      sg[:, k * GWD:(k + 1) * GWD],
                                      start=(k == 0), stop=(k == NK - 1))
                    K[("mm", gi)] = pe.done(i)

                @block.vector
                def _(v, grp=grp, ot=ot, pp=pp, gi=gi):
                    ve.wait(pe, K[("mm", gi)])
                    ve.wait(ldu[pp], ldu[pp].base + 16 * (gi // 2 + 1))
                    if gi >= 2:
                        ve.wait(stx[pp], stx[pp].base + 16 * (gi // 2))
                    ve.tensor_add(sx1[:, pp * GWD:(pp + 1) * GWD],
                                  pmm[:, pp * 512:pp * 512 + GWD],
                                  su[:, pp * GWD:(pp + 1) * GWD])
                K[("x1", gi)] = ve.n

                @block.sync
                def _(s, grp=grp, go=go, ot=ot, pp=pp, gi=gi):
                    sq.wait(ve, K[("x1", gi)])
                    sq.start(x1_o[ot * 128:(ot + 1) * 128, go:go + GWD],
                             sx1[:, pp * GWD:(pp + 1) * GWD], sem=stx[pp])
                    nxst[pp] += 16

        @block.sync
        def _(s):
            for sem in (stx[0], stx[1], ldc, ldu[0], ldu[1]):
                s.wait_ge(sem.h, sem.n)

        @block.gpsimd
        def _(g):
            for sem in (ldw[0], ldw[1]):
                g.wait_ge(sem.h, sem.n)



def host_mgate(y_all, zs_all, u_flat, out_proj_w, norm_w):
    woT = _f32(out_proj_w.T)
    nw = _f32(norm_w.reshape(32, 128).T)
    ones_bf = np.ones((128, 128), dtype=BF)
    in_maps = []
    for c in range(NCORES):
        s0 = c * TC
        in_maps.append({
            "y_pm": _bf(y_all[c]),
            "zs": zs_all[c] if zs_all[c].dtype == BF else _bf(zs_all[c]),
            "u_fm": _f32(u_flat[s0:s0 + TC].T),
            "woT": woT,
            "normw": nw,
            "ones_bf": ones_bf,
        })
    return in_maps

EMOE = 64
CAPM = 1024
TOPK = 6
ROWS_E = CAPM + 1


def phase_moe_route(P):
    nc = P.nc
    x1 = P.dio("x1", [D, TC], F32, "in")
    wg = P.dio("wg", [D, EMOE], F32, "in")
    wsu = P.dio("wsu", [D, 1024], F32, "in")
    wsd = P.dio("wsd", [128, 64 * 128], F32, "in")
    iot = P.dio("iot", [128, EMOE], F32, "in")
    trils = P.dio("trils", [128, 128], F32, "in")
    onek1 = P.dio("onek1", [1, 128], F32, "in")
    onecl = P.dio("onecl", [128, 1], F32, "in")
    rankm = P.dio("rankm", [8, 1], F32, "in")
    tokbase = P.dio("tokbase", [128, 1], F32, "in")
    ident_bf = P.dio("ident_bf", [128, 128], BF16, "in")
    ones_bf = P.dio("ones_bf", [128, 128], BF16, "in")

    xnf_o = P.dio("xnf", [D, TC], BF16, "out")
    xnt_o = P.dio("xnt", [TC, D], BF16, "out")
    sht_o = P.dio("sht", [TC, D], BF16, "out")
    slot_o = P.dio("slot", [TC, TOPK], F32, "out")
    wv_o = P.dio("wv", [TC, TOPK], F32, "out")
    cnt_o = P.dio("cnt", [1, EMOE], F32, "out")
    tbl_o = P.dio("tbl", [EMOE * ROWS_E, 1], F32, "out")

    cc_cnt = P.dio("cc_cnt", [1, EMOE], F32, "int")
    ag_cnt = P.dio("ag_cnt", [NCORES, EMOE], F32, "int")

    NG = 4
    GWD = 512
    NT = 16

    with contextlib.ExitStack() as ctx:
        sx = ctx.enter_context(nc.sbuf_tensor(P.nm("sx"), [128, 16 * GWD], F32))
        sxn = ctx.enter_context(nc.sbuf_tensor(P.nm("sxn"), [128, 16 * TC], BF16))
        sx2 = ctx.enter_context(nc.sbuf_tensor(P.nm("sx2"), [128, 2 * GWD], BF16))
        sm = ctx.enter_context(nc.sbuf_tensor(P.nm("smn"), [128, GWD], F32))
        srs = ctx.enter_context(nc.sbuf_tensor(P.nm("srs"), [128, GWD], F32))
        svar = ctx.enter_context(nc.sbuf_tensor(P.nm("svar"), [128, GWD], F32))
        swg = ctx.enter_context(nc.sbuf_tensor(P.nm("swg"), [128, 16 * EMOE], BF16))
        swgf = ctx.enter_context(nc.sbuf_tensor(P.nm("swgf"), [128, 16 * EMOE], F32))
        swsu = ctx.enter_context(nc.sbuf_tensor(P.nm("swsu"), [128, 2 * 16 * 128], BF16))
        swsd2 = ctx.enter_context(nc.sbuf_tensor(P.nm("swsd2"), [128, 64 * 128], BF16))
        ssc = ctx.enter_context(nc.sbuf_tensor(P.nm("ssc"), [128, NT * EMOE], F32))
        smax = ctx.enter_context(nc.sbuf_tensor(P.nm("smax"), [128, 8], F32))
        srem = ctx.enter_context(nc.sbuf_tensor(P.nm("srem"), [128, EMOE], F32))
        smask = ctx.enter_context(nc.sbuf_tensor(P.nm("smask"), [128, NT * EMOE], F32))
        spos = ctx.enter_context(nc.sbuf_tensor(P.nm("spos"), [128, NT * EMOE], F32))
        siota = ctx.enter_context(nc.sbuf_tensor(P.nm("siota"), [128, EMOE], F32))
        stril = ctx.enter_context(nc.sbuf_tensor(P.nm("strils"), [128, 128], F32))
        sone1 = ctx.enter_context(nc.sbuf_tensor(P.nm("sone1"), [1, 128], F32))
        sonec = ctx.enter_context(nc.sbuf_tensor(P.nm("sonecl"), [128, 1], F32))
        srkm = ctx.enter_context(nc.sbuf_tensor(P.nm("srkm"), [8, 1], F32))
        stkb = ctx.enter_context(nc.sbuf_tensor(P.nm("stkb"), [128, 1], F32))
        soff = ctx.enter_context(nc.sbuf_tensor(P.nm("soff"), [1, EMOE], F32))
        sgoff = ctx.enter_context(nc.sbuf_tensor(P.nm("sgoff"), [128, EMOE], F32))
        szer = ctx.enter_context(nc.sbuf_tensor(P.nm("szer"), [128, EMOE], F32))
        scum = ctx.enter_context(nc.sbuf_tensor(P.nm("scum"), [128, EMOE], F32))
        soh = ctx.enter_context(nc.sbuf_tensor(P.nm("soh"), [128, EMOE], F32))
        sd1 = ctx.enter_context(nc.sbuf_tensor(P.nm("sd1"), [128, EMOE], F32))
        scol = ctx.enter_context(nc.sbuf_tensor(P.nm("scol"), [128, 16], F32))
        sslot = ctx.enter_context(nc.sbuf_tensor(P.nm("sslot"), [128, TOPK], F32))
        swv = ctx.enter_context(nc.sbuf_tensor(P.nm("swv"), [128, TOPK], F32))
        svald = ctx.enter_context(nc.sbuf_tensor(P.nm("svald"), [128, TOPK], F32))
        sslo32 = ctx.enter_context(nc.sbuf_tensor(P.nm("sslo32"), [128, 1], I32))
        stok = ctx.enter_context(nc.sbuf_tensor(P.nm("stok"), [128, 1], F32))
        sidn = ctx.enter_context(nc.sbuf_tensor(P.nm("sidnb"), [128, 128], BF16))
        sonesb = ctx.enter_context(nc.sbuf_tensor(P.nm("sonesb"), [128, 128], BF16))
        sonef = ctx.enter_context(nc.sbuf_tensor(P.nm("sonef"), [128, 128], F32))
        ssu = ctx.enter_context(nc.sbuf_tensor(P.nm("ssu"), [128, 8 * GWD], BF16))
        sshf = ctx.enter_context(nc.sbuf_tensor(P.nm("sshf"), [128, 16 * GWD], BF16))
        sstg = ctx.enter_context(nc.sbuf_tensor(P.nm("sstg"), [128, 2 * 256], BF16))

        pS1 = ctx.enter_context(nc.psum_tensor(P.nm("pS1"), [128, 512], F32))
        pS2 = ctx.enter_context(nc.psum_tensor(P.nm("pS2"), [128, 512], F32))
        psc = ctx.enter_context(nc.psum_tensor(P.nm("psc"), [128, 512], F32))
        ppos = ctx.enter_context(nc.psum_tensor(P.nm("ppos"), [128, 512], F32))
        pmm = ctx.enter_context(nc.psum_tensor(P.nm("pmm"), [128, 2 * 512], F32))
        ptr = ctx.enter_context(nc.psum_tensor(P.nm("ptr"), [128, 512], BF16))

        E = P.engines()
        block = P.block
        ve, se, pe, gq, sq = E.ve, E.se, E.pe, E.gq, E.sq
        ldc = P.sem("ldc")
        ldx = P.sem("ldx")
        ldsu = [P.sem("ldsu0"), P.sem("ldsu1")]
        stf = P.sem("stf")
        agc = P.sem("agc")
        agl = P.sem("agl")
        sct = P.sem("sct")
        stsw = P.sem("stsw")
        stt_ = [P.sem("stt0"), P.sem("stt1")]

        @block.sync
        def _(s):
            sq.start(swgf[:, :].rearrange("p (k e) -> p k e", k=16),
                     wg.rearrange("(k p) e -> p k e", p=128), sem=ldc)
            sq.start(siota[:, :], iot, sem=ldc)
            sq.start(stril[:, :], trils, sem=ldc)
            sq.start(sone1[:, :], onek1, sem=ldc)
            sq.start(sonec[:, :], onecl, sem=ldc)
            sq.start(srkm[:, :], rankm, sem=ldc)
            sq.start(stkb[:, :], tokbase, sem=ldc)
            sq.start(sidn[:, :], ident_bf, sem=ldc)
            sq.start(sonesb[:, :], ones_bf, sem=ldc)
            pass
        ldc_done = ldc.n

        @block.gpsimd
        def _(g):
            gq.start(swsd2[:, :], wsd, sem=ldc)

        @block.vector
        def _(v):
            ve.wait(ldc, ldc_done)
            ve.tensor_copy(swg[:, :], swgf[:, :])
            ve.memset(szer[:, :], 0.0)
            ve.memset(soff[:, :], 0.0)
            ve.memset(sonef[:, :], 1.0)
        v_cst = ve.n

        K = {}
        for g in range(NG):
            go = g * GWD

            @block.gpsimd
            def _(gg, g=g, go=go):
                if g >= 1:
                    gq.wait(ve, K[("xn", g - 1)])
                for k in range(16):
                    gq.start(sx[:, k * GWD:(k + 1) * GWD],
                             x1[k * 128:(k + 1) * 128, go:go + GWD], sem=ldx)
            K[("ldx", g)] = ldx.n

            for k in range(16):
                pp = k % 2

                @block.vector
                def _(v, g=g, k=k, pp=pp):
                    if k == 0:
                        ve.wait(ldx, K[("ldx", g)])
                    if k >= 2:
                        ve.wait(pe, K[("st2", g, k - 2)])
                    ve.tensor_mul(sx2[:, pp * GWD:(pp + 1) * GWD],
                                  sx[:, k * GWD:(k + 1) * GWD],
                                  sx[:, k * GWD:(k + 1) * GWD])
                    K[("x2", g, k)] = ve.n

                @block.tensor
                def _(t, g=g, k=k, pp=pp):
                    if g > 0 and k == 0:
                        pe.wait(se, K[("rs", g - 1)])
                    pe.wait(ve, K[("x2", g, k)])
                    if k == 0:
                        pe.wait(ve, v_cst)
                    pe.matmul(pS1[:, :], sonef[:], sx[:, k * GWD:(k + 1) * GWD],
                              start=(k == 0), stop=(k == 15))
                    i = pe.matmul(pS2[:, :], sonesb[:], sx2[:, pp * GWD:(pp + 1) * GWD],
                                  start=(k == 0), stop=(k == 15))
                    K[("st2", g, k)] = pe.done(i)
                    if k == 15:
                        K[("stats", g)] = pe.n

            @block.vector
            def _(v, g=g):
                ve.wait(pe, K[("stats", g)])
                ve.tensor_scalar_mul(sm[:, :], pS1[:, :], 1.0 / D)
                ve.tensor_mul(svar[:, :], sm[:, :], sm[:, :])
                ve.scalar_tensor_tensor(svar[:, :], pS2[:, :], 1.0 / D, svar[:, :],
                                        ALU.mult, ALU.subtract)
                ve.tensor_scalar_add(svar[:, :], svar[:, :], LN_EPS)
            K[("var", g)] = ve.n

            @block.scalar
            def _(s, g=g):
                se.wait(ve, K[("var", g)])
                se.activation(srs[:, :], svar[:, :], AF.Ln)
                se.activation(srs[:, :], srs[:, :], AF.Exp, scale=-0.5)
            K[("rs", g)] = se.n

            @block.vector
            def _(v, g=g, go=go):
                ve.wait(se, K[("rs", g)])
                for k in range(16):
                    sl = sxn[:, k * TC + go:k * TC + go + GWD]
                    ve.tensor_sub(sx[:, k * GWD:(k + 1) * GWD], sx[:, k * GWD:(k + 1) * GWD],
                                  sm[:, :])
                    ve.tensor_mul(sl, sx[:, k * GWD:(k + 1) * GWD], srs[:, :])
            K[("xn", g)] = ve.n

        @block.sync
        def _(s):
            sq.wait(ve, K[("xn", NG - 1)])
            for k in range(16):
                sq.start(xnf_o[k * 128:(k + 1) * 128, :], sxn[:, k * TC:(k + 1) * TC], sem=stf)
        K["xnf_done"] = stf.n

        for tt in range(NT):
            @block.tensor
            def _(t, tt=tt):
                pe.wait(ve, K[("xn", NG - 1)])
                if tt >= 1:
                    pe.wait(se, K[("sig", tt - 1)])
                for k in range(16):
                    i = pe.matmul(psc[:, 0:EMOE], sxn[:, k * TC + tt * 128:k * TC + (tt + 1) * 128],
                                  swg[:, k * EMOE:(k + 1) * EMOE],
                                  start=(k == 0), stop=(k == 15))
                K[("sc", tt)] = pe.done(i)

            @block.scalar
            def _(s, tt=tt):
                se.wait(pe, K[("sc", tt)])
                se.activation(ssc[:, tt * EMOE:(tt + 1) * EMOE], psc[:, 0:EMOE], AF.Sigmoid)
            K[("sig", tt)] = se.n

        for tt in range(NT):
            @block.vector
            def _(v, tt=tt):
                ve.wait(se, K[("sig", tt)])
                ve.wait(ldc, ldc_done)
                sc = ssc[:, tt * EMOE:(tt + 1) * EMOE]
                ve.max(out=smax[:, :], in_=sc)
                ve.memset(smax[:, 6:8], -1e30)
                ve.match_replace(out=srem[:, :], in_to_replace=smax[:, :], in_values=sc,
                                 imm_value=-1e30)
                ve.tensor_tensor(smask[:, tt * EMOE:(tt + 1) * EMOE], sc, srem[:, :],
                                 ALU.not_equal)
            K[("msk", tt)] = ve.n

            @block.tensor
            def _(t, tt=tt):
                pe.wait(ve, K[("msk", tt)])
                if tt > 0:
                    pe.wait(ve, K[("pos", tt - 1)])
                pe.matmul(ppos[:, 0:EMOE], stril[:, :], smask[:, tt * EMOE:(tt + 1) * EMOE],
                          start=True, stop=False)
                i = pe.matmul(ppos[:, 0:EMOE], sone1[:, :], soff[:, :],
                              start=False, stop=True)
                K[("posm", tt)] = pe.done(i)
                i = pe.matmul(ppos[0:1, 256:256 + EMOE], sonec[:, :],
                              smask[:, tt * EMOE:(tt + 1) * EMOE], start=True, stop=True)
                K[("cnt", tt)] = pe.done(i)

            @block.vector
            def _(v, tt=tt):
                ve.wait(pe, K[("cnt", tt)])
                ve.tensor_copy(spos[:, tt * EMOE:(tt + 1) * EMOE], ppos[:, 0:EMOE])
                ve.tensor_add(soff[:, :], soff[:, :], ppos[0:1, 256:256 + EMOE])
            K[("pos", tt)] = ve.n

        @block.sync
        def _(s):
            sq.wait(ve, K[("pos", NT - 1)])
            sq.start(cnt_o, soff[:, :], sem=stf)
            sq.start(cc_cnt, soff[:, :], sem=sct)

        @block.gpsimd
        def _(g):
            g.wait_ge(sct.h, sct.n)
            gq.collective("AllGather", ALU.bypass, [list(range(NCORES))],
                          [cc_cnt.opt()], [ag_cnt.opt()], sem=agc)

        @block.sync
        def _(s):
            s.wait_ge(agc.h, agc.n)
            sq.start(srem[0:8, :], ag_cnt, sem=agl)
        K["agl"] = agl.n

        @block.tensor
        def _(t):
            pe.wait(agl, K["agl"])
            i = pe.matmul(ppos[0:1, 0:EMOE], srkm[:, :], srem[0:8, :], start=True, stop=True)
            pe.done(i)
            K["goffmm"] = pe.n

        @block.vector
        def _(v):
            ve.wait(pe, K["goffmm"])
            ve.tensor_copy(soff[:, :], ppos[0:1, 0:EMOE])
        K["goffrow"] = ve.n

        @block.tensor
        def _(t):
            pe.wait(ve, K["goffrow"])
            i = pe.matmul(ppos[:, 256:256 + EMOE], sone1[:, :], soff[:, :], start=True, stop=True)
            K["goffbc"] = pe.done(i)

        @block.vector
        def _(v):
            ve.wait(pe, K["goffbc"])
            ve.tensor_copy(sgoff[:, :], ppos[:, 256:256 + EMOE])
        K["goff"] = ve.n

        for tt in range(NT):
            @block.vector
            def _(v, tt=tt):
                if tt > 0:
                    ve.wait(sct, K[("sctt", tt - 1, TOPK - 1)])
                    ve.wait(stsw, stsw.base + 32 * tt)
                msk = smask[:, tt * EMOE:(tt + 1) * EMOE]
                sc = ssc[:, tt * EMOE:(tt + 1) * EMOE]
                ve.tensor_tensor_scan(scum[:, :], msk, szer[:, :], 0.0, ALU.add, ALU.add)
                ve.tensor_add(sd1[:, :], spos[:, tt * EMOE:(tt + 1) * EMOE], sgoff[:, :])
                for k in range(TOPK):
                    ve.tensor_scalar(soh[:, :], scum[:, :], float(k + 1), None, ALU.is_equal)
                    ve.tensor_mul(soh[:, :], soh[:, :], msk)
                    ve.tensor_mul(srem[:, :], soh[:, :], siota[:, :])
                    ve.tensor_reduce(scol[:, 0:1], srem[:, :], mybir.AxisListType.X, ALU.add)
                    ve.tensor_mul(srem[:, :], soh[:, :], sd1[:, :])
                    ve.tensor_reduce(scol[:, 1:2], srem[:, :], mybir.AxisListType.X, ALU.add)
                    ve.tensor_mul(srem[:, :], soh[:, :], sc)
                    ve.tensor_reduce(swv[:, k:k + 1], srem[:, :], mybir.AxisListType.X, ALU.add)
                    ve.tensor_scalar(svald[:, k:k + 1], scol[:, 1:2], 1024.0, None, ALU.is_lt)
                    ve.tensor_scalar_min(scol[:, 1:2], scol[:, 1:2], 1024.0)
                    ve.scalar_tensor_tensor(sslot[:, k:k + 1], scol[:, 0:1], 1025.0,
                                            scol[:, 1:2], ALU.mult, ALU.add)
                ve.tensor_reduce(scol[:, 2:3], swv[:, :], mybir.AxisListType.X, ALU.add)
                ve.reciprocal(scol[:, 3:4], scol[:, 2:3])
                ve.tensor_scalar_mul(swv[:, :], swv[:, :], scol[:, 3:4])
                ve.tensor_mul(swv[:, :], swv[:, :], svald[:, :])
                ve.tensor_scalar_add(stok[:, :], stkb[:, :], float(tt * 128))
            K[("ext", tt)] = ve.n

            @block.sync
            def _(s, tt=tt):
                sq.wait(ve, K[("ext", tt)])
                sq.start(slot_o[tt * 128:(tt + 1) * 128, :], sslot[:, :], sem=stsw)
                sq.start(wv_o[tt * 128:(tt + 1) * 128, :], swv[:, :], sem=stsw)
            K[("exts", tt)] = stsw.n

            for k in range(TOPK):
                @block.vector
                def _(v, tt=tt, k=k):
                    if (tt, k) != (0, 0):
                        gq_prev = K.get(("sctt", tt, k - 1) if k else ("sctt", tt - 1, TOPK - 1))
                        ve.wait(sct, gq_prev)
                    ve.tensor_copy(sslo32[:, :], sslot[:, k:k + 1])
                K[("i32", tt, k)] = ve.n

                @block.gpsimd
                def _(g, tt=tt, k=k):
                    gq.wait(ve, K[("i32", tt, k)])
                    gq.indirect(out=tbl_o, out_offset=IndirectOffsetOnAxis(sslo32[:, :], axis=0),
                                in_=stok[:, :], in_offset=None, sem=sct)
                K[("sctt", tt, k)] = sct.n

        for grp in range(NG):
            go = grp * GWD
            for mt in range(8):
                pp = mt % 2
                gi = grp * 8 + mt

                @block.gpsimd
                def _(g, grp=grp, mt=mt, pp=pp, gi=gi):
                    if gi >= 2:
                        gq.wait(pe, K[("su", gi - 2)])
                    gq.start(swsu[:, pp * 16 * 128:(pp + 1) * 16 * 128],
                             wsu[:, mt * 128:(mt + 1) * 128].rearrange("(k p) m -> p k m", p=128),
                             sem=ldsu[pp])
                K[("ldsu", gi)] = ldsu[pp].base + 16 * (gi // 2 + 1)

                @block.tensor
                def _(t, grp=grp, go=go, mt=mt, pp=pp, gi=gi):
                    pe.wait(ldsu[pp], K[("ldsu", gi)])
                    if gi >= 2:
                        pe.wait(ve, K[("hh", gi - 2)] if (mt % 2 == 0) else K[("hh", gi - 2)])
                    for k in range(16):
                        i = pe.matmul(pmm[:, pp * 512:pp * 512 + GWD],
                                      swsu[:, pp * 16 * 128 + k * 128:pp * 16 * 128 + (k + 1) * 128],
                                      sxn[:, k * TC + go:k * TC + go + GWD],
                                      start=(k == 0), stop=(k == 15))
                    K[("su", gi)] = pe.done(i)

                @block.scalar
                def _(s, grp=grp, mt=mt, pp=pp, gi=gi):
                    se.wait(pe, K[("su", gi)])
                    if mt < 4:
                        se.copy(ssu[:, mt * GWD:(mt + 1) * GWD], pmm[:, pp * 512:pp * 512 + GWD])
                    else:
                        se.activation(ssu[:, (mt % 4 + 4) * GWD:(mt % 4 + 5) * GWD],
                                      pmm[:, pp * 512:pp * 512 + GWD], AF.Silu)
                K[("hh", gi)] = se.n

            @block.vector
            def _(v, grp=grp):
                ve.wait(se, K[("hh", grp * 8 + 7)])
                for j in range(4):
                    ve.tensor_mul(ssu[:, j * GWD:(j + 1) * GWD],
                                  ssu[:, j * GWD:(j + 1) * GWD],
                                  ssu[:, (j + 4) * GWD:(j + 5) * GWD])
            K[("hmul", grp)] = ve.n

            for ot in range(16):
                @block.tensor
                def _(t, grp=grp, ot=ot):
                    pe.wait(ve, K[("hmul", grp)])
                    if ot >= 1:
                        pe.wait(se, K[("shev", grp, ot - 1)])
                    for k in range(4):
                        i = pe.matmul(pmm[:, 0:GWD],
                                      swsd2[:, (k * 16 + ot) * 128:(k * 16 + ot + 1) * 128],
                                      ssu[:, k * GWD:(k + 1) * GWD],
                                      start=(k == 0), stop=(k == 3))
                    K[("shmm", grp, ot)] = pe.done(i)

                @block.scalar
                def _(s, grp=grp, ot=ot):
                    se.wait(pe, K[("shmm", grp, ot)])
                    se.copy(sshf[:, ot * GWD:(ot + 1) * GWD], pmm[:, 0:GWD])
                K[("shev", grp, ot)] = se.n

            for sub in range(4):
                tt = grp * 4 + sub
                for dt_ in range(16):
                    idx = (tt * 16 + dt_)
                    par = idx % 2

                    @block.tensor
                    def _(t, grp=grp, go=go, sub=sub, tt=tt, dt_=dt_, par=par, idx=idx):
                        if dt_ == 0 and sub == 0:
                            pe.wait(se, K[("shev", grp, 15)])
                        prev = K.get(("trev", idx - 1))
                        if prev:
                            pe.wait(ve, prev)
                        i = pe.transpose(ptr[:, 0:128], sxn[:, dt_ * TC + tt * 128:dt_ * TC + (tt + 1) * 128], sidn[:, :])
                        pe.done(i)
                        i = pe.transpose(ptr[:, 128:256], sshf[:, dt_ * GWD + sub * 128:dt_ * GWD + (sub + 1) * 128], sidn[:, :])
                        K[("tr", idx)] = pe.done(i)

                    @block.vector
                    def _(v, tt=tt, dt_=dt_, par=par, idx=idx):
                        ve.wait(pe, K[("tr", idx)])
                        if idx >= 2:
                            ve.wait(stt_[par], stt_[par].base + 32 * (idx // 2))
                        ve.tensor_copy(sstg[:, par * 256:par * 256 + 128], ptr[:, 0:128])
                        ve.tensor_copy(sstg[:, par * 256 + 128:par * 256 + 256], ptr[:, 128:256])
                    K[("trev", idx)] = ve.n

                    @block.sync
                    def _(s, tt=tt, dt_=dt_, par=par, idx=idx):
                        sq.wait(ve, K[("trev", idx)])
                        sq.start(xnt_o[tt * 128:(tt + 1) * 128, dt_ * 128:(dt_ + 1) * 128],
                                 sstg[:, par * 256:par * 256 + 128], sem=stt_[par])
                        sq.start(sht_o[tt * 128:(tt + 1) * 128, dt_ * 128:(dt_ + 1) * 128],
                                 sstg[:, par * 256 + 128:par * 256 + 256], sem=stt_[par])

        @block.sync
        def _(s):
            for sem in (stf, stt_[0], stt_[1], ldc, agl):
                s.wait_ge(sem.h, sem.n)

        @block.gpsimd
        def _(g):
            for sem in (sct, agc, ldx, ldsu[0], ldsu[1]):
                g.wait_ge(sem.h, sem.n)



def host_moe_route(x1_all, w_gate, w_shared_up, w_shared_down):
    wsd_t = np.zeros((128, 64 * 128), np.float32)
    for k in range(4):
        for mt in range(16):
            wsd_t[:, (k * 16 + mt) * 128:(k * 16 + mt + 1) * 128] = \
                w_shared_down[k * 128:(k + 1) * 128, mt * 128:(mt + 1) * 128]
    iot = np.broadcast_to(np.arange(EMOE, dtype=np.float32), (128, EMOE)).copy()
    trils = (np.arange(128)[None, :] < np.arange(128)[:, None]).astype(np.float32)
    onek1 = np.ones((1, 128), np.float32)
    onecl = np.ones((128, 1), np.float32)
    ident_bf = np.eye(128, dtype=np.float32).astype(BF)
    ones_bf = np.ones((128, 128), dtype=BF)
    in_maps = []
    for c in range(NCORES):
        rk = np.zeros((8, 1), np.float32)
        rk[:c] = 1.0
        tb = (np.arange(128, dtype=np.float32) + c * TC).reshape(128, 1)
        in_maps.append({
            "x1": _f32(x1_all[c]),
            "wg": _f32(w_gate),
            "wsu": _f32(w_shared_up),
            "wsd": wsd_t,
            "iot": iot, "trils": trils, "onek1": onek1, "onecl": onecl,
            "rankm": rk, "tokbase": tb,
            "ident_bf": ident_bf, "ones_bf": ones_bf,
        })
    return in_maps



def phase_moe_expert(P):
    nc = P.nc
    xg = P.dio("xg", [NTOK, D], BF16, "in")
    tbls = P.dio("tbls", [NCORES, EMOE * ROWS_E], F32, "in")
    sel64 = P.dio("sel64", [64, 8], F32, "in")
    cntT = P.dio("cntT", [8, 8], F32, "in")
    wup = P.dio("wup", [8 * D, 512], F32, "in")
    wdn = P.dio("wdn", [8 * 256, D], F32, "in")
    islot = P.dio("islot", [8, ROWS_E], F32, "in")
    idn = P.dio("idn", [128, 128], F32, "in")

    y_o = P.dio("Yout", [8 * ROWS_E, D], BF16, "out")
    xe_d = P.dio("xe_d", [8 * CAPM, D], BF16, "int")
    ye_d = P.dio("ye_d", [D, CAPM], BF16, "int")

    with contextlib.ExitStack() as ctx:
        stbl = ctx.enter_context(nc.sbuf_tensor(P.nm("stbl"), [8, ROWS_E], F32))
        stbl64 = ctx.enter_context(nc.sbuf_tensor(P.nm("stbl64"), [64, ROWS_E], F32))
        ssel = ctx.enter_context(nc.sbuf_tensor(P.nm("sselE"), [64, 8], F32))
        smrg = ctx.enter_context(nc.sbuf_tensor(P.nm("smrg"), [8, ROWS_E], F32))
        scnt = ctx.enter_context(nc.sbuf_tensor(P.nm("scnt"), [8, 8], F32))
        sgo = ctx.enter_context(nc.sbuf_tensor(P.nm("sgo"), [8, 8], F32))
        send = ctx.enter_context(nc.sbuf_tensor(P.nm("send"), [8, 8], F32))
        szer8 = ctx.enter_context(nc.sbuf_tensor(P.nm("szer8"), [8, 8], F32))
        sio = ctx.enter_context(nc.sbuf_tensor(P.nm("sio"), [8, ROWS_E], F32))
        si1 = ctx.enter_context(nc.sbuf_tensor(P.nm("si1"), [8, ROWS_E], F32))
        si2 = ctx.enter_context(nc.sbuf_tensor(P.nm("si2"), [8, ROWS_E], F32))
        sidn = ctx.enter_context(nc.sbuf_tensor(P.nm("sidn"), [128, 128], F32))
        stc = ctx.enter_context(nc.sbuf_tensor(P.nm("stc"), [128, 64], F32))
        sti = ctx.enter_context(nc.sbuf_tensor(P.nm("sti"), [128, 64], I32))
        sxe = ctx.enter_context(nc.sbuf_tensor(P.nm("sxe"), [128, 2 * 16 * 1024], BF16))
        swu = ctx.enter_context(nc.sbuf_tensor(P.nm("swu"), [128, 2 * 64 * 128], BF16))
        swd = ctx.enter_context(nc.sbuf_tensor(P.nm("swd"), [128, 2 * 32 * 128], BF16))
        sh = ctx.enter_context(nc.sbuf_tensor(P.nm("sh"), [128, 4 * 512], BF16))
        sg_ = ctx.enter_context(nc.sbuf_tensor(P.nm("sg_"), [128, 1024], BF16))
        syst = ctx.enter_context(nc.sbuf_tensor(P.nm("syst"), [128, 2 * 512], BF16))
        sytm = ctx.enter_context(nc.sbuf_tensor(P.nm("sytm"), [128, 2 * D], BF16))
        sgst = ctx.enter_context(nc.sbuf_tensor(P.nm("sgst"), [128, 2 * D], BF16))
        szero = ctx.enter_context(nc.sbuf_tensor(P.nm("szero"), [1, D], BF16))

        pmm = ctx.enter_context(nc.psum_tensor(P.nm("pmm"), [128, 4 * 512], F32))
        ptr = ctx.enter_context(nc.psum_tensor(P.nm("ptr"), [128, 512], F32))

        E = P.engines()
        block = P.block
        ve, se, pe, gq, sq = E.ve, E.se, E.pe, E.gq, E.sq
        ldc = P.sem("ldc")
        ldt = P.sem("ldt")
        gat = P.sem("gat")
        ldxe = [P.sem("ldxe0"), P.sem("ldxe1")]
        ldwu = [P.sem("ldwu0"), P.sem("ldwu1")]
        ldwd = [P.sem("ldwd0"), P.sem("ldwd1")]
        stye = P.sem("stye")
        ldyt = P.sem("ldyt")
        sty = [P.sem("sty0"), P.sem("sty1")]
        sttr = P.sem("sttr")

        @block.sync
        def _(s):
            sq.start(scnt[:, :], cntT, sem=ldc)
            sq.start(sio[:, :], islot, sem=ldc)
            sq.start(sidn[:, :], idn, sem=ldc)
            sq.start(ssel[:, :], sel64, sem=ldc)
        ldc_done = ldc.n

        @block.vector
        def _(v):
            ve.wait(ldc, ldc_done)
            ve.memset(szer8[:, :], 0.0)
            ve.memset(szero[:, :], 0.0)
            ve.tensor_tensor_scan(sgo[:, :], scnt[:, :], szer8[:, :], 0.0, ALU.add, ALU.add)
            ve.tensor_copy(send[:, 1:8], sgo[:, 0:7])
            ve.memset(send[:, 0:1], 0.0)
            ve.tensor_add(sgo[:, :], send[:, :], scnt[:, :])
            ve.memset(smrg[:, :], 0.0)
        v_go = ve.n

        K = {}
        for r in range(NCORES):
            @block.sync
            def _(s, r=r):
                if r == 0:
                    sq.wait(ve, v_go)
                else:
                    sq.wait(ve, K[("mrg", r - 1)])
                sq.start(stbl64[:, :], tbls[r:r + 1, :].rearrange("o (e s) -> (o e) s", e=EMOE),
                         sem=ldt)
            ldt_r = ldt.n

            for nchk, o in ((512, 0), (512, 512), (1, 1024)):
                @block.tensor
                def _(t, r=r, nchk=nchk, o=o, ldt_r=ldt_r):
                    if o == 0:
                        pe.wait(ldt, ldt_r)
                        pe.wait(ldc, ldc_done)
                        if r > 0:
                            pe.wait(ve, K[("mrg", r - 1)])
                    i = pe.matmul(pmm[0:8, 0:nchk] if o == 0 else
                                  (pmm[0:8, 512:512 + nchk] if o == 512 else pmm[0:8, 1024:1025]),
                                  ssel[:, :], stbl64[:, o:o + nchk], start=True, stop=True)
                    K[("sel", r, o)] = pe.done(i)

            @block.vector
            def _(v, r=r):
                ve.wait(pe, K[("sel", r, 1024)])
                ve.tensor_copy(stbl[:, 0:512], pmm[0:8, 0:512])
                ve.tensor_copy(stbl[:, 512:1024], pmm[0:8, 512:1024])
                ve.tensor_copy(stbl[:, 1024:1025], pmm[0:8, 1024:1025])
                ve.tensor_scalar(si1[:, :], sio[:, :], send[:, r:r + 1], None, ALU.is_ge)
                ve.tensor_scalar(si2[:, :], sio[:, :], sgo[:, r:r + 1], None, ALU.is_lt)
                ve.tensor_mul(si1[:, :], si1[:, :], si2[:, :])
                ve.tensor_mul(si1[:, :], si1[:, :], stbl[:, :])
                ve.tensor_add(smrg[:, :], smrg[:, :], si1[:, :])
            K[("mrg", r)] = ve.n

        for st in range(8):
            @block.tensor
            def _(t, st=st):
                if st == 0:
                    pe.wait(ve, K[("mrg", NCORES - 1)])
                else:
                    pe.wait(ve, K[("trc", st - 1)])
                i = pe.transpose(ptr[:, 0:8], smrg[:, st * 128:(st + 1) * 128], sidn[0:8, 0:8])
                K[("tr", st)] = pe.done(i)

            @block.vector
            def _(v, st=st):
                ve.wait(pe, K[("tr", st)])
                ve.tensor_copy(stc[:, st * 8:(st + 1) * 8], ptr[:, 0:8])
            K[("trc", st)] = ve.n

        @block.vector
        def _(v):
            ve.tensor_copy(sti[:, :], stc[:, :])
        v_idx = ve.n

        gstg = [P.sem("gstg0"), P.sem("gstg1")]
        gi = 0
        for el in range(8):
            for st in range(8):
                par = gi % 2

                @block.gpsimd
                def _(g, el=el, st=st, par=par, gi=gi):
                    if gi == 0:
                        gq.wait(ve, v_idx)
                    if gi >= 2:
                        g.wait_ge(gstg[par].h, gstg[par].base + 16 * (gi // 2))
                    gq.indirect(out=sgst[:, par * D:(par + 1) * D],
                                out_offset=None, in_=xg,
                                in_offset=IndirectOffsetOnAxis(sti[:, st * 8 + el:st * 8 + el + 1], axis=0),
                                sem=gat)
                K[("gath", gi)] = gat.n

                @block.sync
                def _(s, el=el, st=st, par=par, gi=gi):
                    s.wait_ge(gat.h, K[("gath", gi)])
                    sq.start(xe_d[el * CAPM + st * 128:el * CAPM + (st + 1) * 128, :],
                             sgst[:, par * D:(par + 1) * D], sem=gstg[par])
                gi += 1
        gat_all = (gstg[0].n, gstg[1].n)

        for el in range(8):
            pp = el % 2

            @block.gpsimd
            def _(g, el=el, pp=pp):
                if el >= 2:
                    gq.wait(pe, K[("upmm", el - 2)])
                gq.start(swu[:, pp * 8192:(pp + 1) * 8192],
                         wup[el * D:(el + 1) * D, :]
                         .rearrange("(k p) (j m) -> p j k m", p=128, m=128), sem=ldwu[pp])
                if el >= 2:
                    gq.wait(pe, K[("dnmm", el - 2)])
                gq.start(swd[:, pp * 4096:(pp + 1) * 4096],
                         wdn[el * 256:(el + 1) * 256, :]
                         .rearrange("(k p) (t m) -> p t k m", p=128, m=128), sem=ldwd[pp])
            K[("ldw", el)] = (ldwu[pp].n, ldwd[pp].n)

            @block.sync
            def _(s, el=el, pp=pp):
                if el == 0:
                    s.wait_ge(gstg[0].h, gat_all[0])
                    s.wait_ge(gstg[1].h, gat_all[1])
                if el >= 2:
                    sq.wait(pe, K[("upmm", el - 2)])
                for dt_ in range(16):
                    sq.transpose(sxe[:, pp * 16384 + dt_ * 1024:pp * 16384 + (dt_ + 1) * 1024],
                                 xe_d[el * CAPM:(el + 1) * CAPM, dt_ * 128:(dt_ + 1) * 128],
                                 sem=ldxe[pp])
            K[("ldxe", el)] = ldxe[pp].n

            for n2 in range(2):
                @block.tensor
                def _(t, el=el, pp=pp, n2=n2):
                    if n2 == 0:
                        pe.wait(ldxe[pp], K[("ldxe", el)])
                        pe.wait(ldwu[pp], K[("ldw", el)][0])
                        if el >= 1:
                            pe.wait(ve, K[("h", el - 1, 1)])
                            pe.wait(se, K[("ydev", el - 1, 15)])
                    else:
                        pe.wait(ve, K[("h", el, 0)])
                    for jt in range(4):
                        for k in range(16):
                            i = pe.matmul(pmm[:, jt * 512:(jt + 1) * 512],
                                          swu[:, pp * 8192 + (jt * 16 + k) * 128:
                                              pp * 8192 + (jt * 16 + k + 1) * 128],
                                          sxe[:, pp * 16384 + k * 1024 + n2 * 512:
                                              pp * 16384 + k * 1024 + (n2 + 1) * 512],
                                          start=(k == 0), stop=(k == 15))
                    K[("upmm", el, n2)] = pe.done(i)
                    if n2 == 1:
                        K[("upmm", el)] = pe.n

                @block.scalar
                def _(s, el=el, pp=pp, n2=n2):
                    se.wait(pe, K[("upmm", el, n2)])
                    se.activation(sg_[:, 0:512], pmm[:, 0:512], AF.Silu)
                    se.activation(sg_[:, 512:1024], pmm[:, 512:1024], AF.Silu)
                K[("silu", el, n2)] = se.n

                @block.vector
                def _(v, el=el, pp=pp, n2=n2):
                    ve.wait(se, K[("silu", el, n2)])
                    ve.tensor_mul(sh[:, n2 * 512:(n2 + 1) * 512], sg_[:, 0:512],
                                  pmm[:, 1024:1536])
                    ve.tensor_mul(sh[:, 1024 + n2 * 512:1024 + (n2 + 1) * 512],
                                  sg_[:, 512:1024], pmm[:, 1536:2048])
                K[("h", el, n2)] = ve.n

            for mt in range(16):
                spp = mt % 2

                @block.tensor
                def _(t, el=el, pp=pp, mt=mt, spp=spp):
                    if mt == 0:
                        pe.wait(ve, K[("h", el, 1)])
                        pe.wait(ldwd[pp], K[("ldw", el)][1])
                    if mt >= 2:
                        pe.wait(se, K[("ydev", el, mt - 2)])
                    for n2 in range(2):
                        for kh in range(2):
                            i = pe.matmul(pmm[:, (spp * 2 + n2) * 512:(spp * 2 + n2 + 1) * 512],
                                          swd[:, pp * 4096 + (mt * 2 + kh) * 128:
                                              pp * 4096 + (mt * 2 + kh + 1) * 128],
                                          sh[:, kh * 1024 + n2 * 512:kh * 1024 + (n2 + 1) * 512],
                                          start=(kh == 0), stop=(kh == 1))
                    K[("dnmm", el, mt)] = pe.done(i)

                @block.scalar
                def _(s, el=el, mt=mt, spp=spp):
                    se.wait(pe, K[("dnmm", el, mt)])
                    if (el, mt) != (0, 0):
                        pel, pmt = (el, mt - 1) if mt else (el - 1, 15)
                        se.wait(stye, K[("yst", pel, pmt)])
                    se.copy(syst[:, 0:512], pmm[:, spp * 1024:spp * 1024 + 512])
                    se.copy(syst[:, 512:1024], pmm[:, spp * 1024 + 512:spp * 1024 + 1024])
                K[("ydev", el, mt)] = se.n

                @block.sync
                def _(s, el=el, mt=mt):
                    sq.wait(se, K[("ydev", el, mt)])
                    if el >= 1 and mt == 0:
                        s.wait_ge(ldyt.h, K[("ytr", el - 1, 7)])
                    sq.start(ye_d[mt * 128:(mt + 1) * 128, 0:512], syst[:, 0:512], sem=stye)
                    sq.start(ye_d[mt * 128:(mt + 1) * 128, 512:1024], syst[:, 512:1024], sem=stye)
                K[("yst", el, mt)] = stye.n
            K[("dnmm", el)] = pe.n

            for st in range(8):
                tpp = st % 2

                @block.sync
                def _(s, el=el, st=st, tpp=tpp):
                    if st == 0:
                        s.wait_ge(stye.h, K[("yst", el, 15)])
                    gi = el * 8 + st
                    if gi >= 2:
                        s.wait_ge(sty[tpp].h, sty[tpp].base + 16 * (gi // 2))
                    sq.transpose(sytm[:, tpp * D:(tpp + 1) * D],
                                 ye_d[:, st * 128:(st + 1) * 128], sem=ldyt)
                K[("ytr", el, st)] = ldyt.n

                @block.sync
                def _(s, el=el, st=st, tpp=tpp):
                    s.wait_ge(ldyt.h, K[("ytr", el, st)])
                    sq.start(y_o[el * ROWS_E + st * 128:el * ROWS_E + (st + 1) * 128, :],
                             sytm[:, tpp * D:(tpp + 1) * D], sem=sty[tpp])

            @block.sync
            def _(s, el=el):
                sq.wait(ve, v_go)
                sq.start(y_o[el * ROWS_E + CAPM:el * ROWS_E + CAPM + 1, :], szero[:, :],
                         sem=sttr)

        @block.sync
        def _(s):
            for sem in (sty[0], sty[1], sttr, stye, ldt, ldc, ldyt, ldxe[0], ldxe[1]):
                s.wait_ge(sem.h, sem.n)

        @block.gpsimd
        def _(g):
            for sem in (gat, ldwu[0], ldwu[1], ldwd[0], ldwd[1], gstg[0], gstg[1]):
                g.wait_ge(sem.h, sem.n)



def host_moe_expert(xnt_all, tbl_all, cnt_all, w_up, w_down):
    xg = np.concatenate([np.asarray(x) for x in xnt_all], 0)
    tbls = np.stack([np.asarray(t).reshape(-1) for t in tbl_all]).astype(np.float32)
    islot = np.broadcast_to(np.arange(ROWS_E, dtype=np.float32), (8, ROWS_E)).copy()
    idn = np.eye(128, dtype=np.float32)
    cnt_mat = np.stack([np.asarray(c).reshape(-1) for c in cnt_all])
    in_maps = []
    for c in range(NCORES):
        cntT = np.ascontiguousarray(cnt_mat[:, 8 * c:8 * c + 8].T)
        sel = np.zeros((64, 8), np.float32)
        for j in range(8):
            sel[8 * c + j, j] = 1.0
        in_maps.append({
            "xg": xg,
            "tbls": tbls,
            "sel64": sel,
            "cntT": _f32(cntT),
            "wup": _f32(w_up[8 * c:8 * c + 8].reshape(8 * D, 512)),
            "wdn": _f32(w_down[8 * c:8 * c + 8].reshape(8 * 256, D)),
            "islot": islot,
            "idn": idn,
        })
    return in_maps



def phase_moe_combine(P):
    nc = P.nc
    yall = P.dio("yall", [EMOE * ROWS_E, D], BF16, "in")
    slot = P.dio("slot", [TC, TOPK], F32, "in")
    wv = P.dio("wv", [TC, TOPK], F32, "in")
    sht = P.dio("sht", [TC, D], BF16, "in")
    x1t = P.dio("x1t", [TC, D], F32, "in")
    x2_o = P.dio("x2t", [TC, D], F32, "out")

    NT = 16
    with contextlib.ExitStack() as ctx:
        sslot = ctx.enter_context(nc.sbuf_tensor(P.nm("sslot"), [128, TOPK], F32))
        sslo32 = ctx.enter_context(nc.sbuf_tensor(P.nm("sslo32"), [128, TOPK], I32))
        swv = ctx.enter_context(nc.sbuf_tensor(P.nm("swv"), [128, TOPK], F32))
        sga = ctx.enter_context(nc.sbuf_tensor(P.nm("sga"), [128, 2 * D], BF16))
        ssh = ctx.enter_context(nc.sbuf_tensor(P.nm("ssh"), [128, D], BF16))
        sx1 = ctx.enter_context(nc.sbuf_tensor(P.nm("sx1"), [128, D], F32))
        sacc = ctx.enter_context(nc.sbuf_tensor(P.nm("sacc"), [128, 2 * D], F32))

        E = P.engines()
        block = P.block
        ve, gq, sq = E.ve, E.gq, E.sq
        lds = P.sem("lds")
        gat = [P.sem("gat0"), P.sem("gat1")]
        stx = [P.sem("stx0"), P.sem("stx1")]

        K = {}
        for tt in range(NT):
            par = tt % 2

            @block.sync
            def _(s, tt=tt, par=par):
                if tt >= 1:
                    sq.wait(ve, K[("acc", tt - 1)])
                sq.start(sslot[:, :], slot[tt * 128:(tt + 1) * 128, :], sem=lds)
                sq.start(swv[:, :], wv[tt * 128:(tt + 1) * 128, :], sem=lds)
                sq.start(ssh[:, :], sht[tt * 128:(tt + 1) * 128, :], sem=lds)
                sq.start(sx1[:, :], x1t[tt * 128:(tt + 1) * 128, :], sem=lds)
            K[("lds", tt)] = lds.n

            @block.vector
            def _(v, tt=tt, par=par):
                ve.wait(lds, K[("lds", tt)])
                if tt >= 1:
                    ve.wait(gat[(tt - 1) % 2], K[("ga", tt - 1, TOPK - 1)])
                if tt >= 2:
                    ve.wait(stx[par], stx[par].base + 16 * (tt // 2))
                ve.tensor_copy(sslo32[:, :], sslot[:, :])
                ve.tensor_add(sacc[:, par * D:(par + 1) * D], sx1[:, :], ssh[:, :])
            K[("i32", tt)] = ve.n

            for k in range(TOPK):
                @block.gpsimd
                def _(g, tt=tt, k=k, par=par):
                    if k == 0:
                        gq.wait(ve, K[("i32", tt)])
                    else:
                        gq.wait(ve, K[("k", tt, k - 1)])
                    gq.indirect(out=sga[:, par * D:(par + 1) * D], out_offset=None,
                                in_=yall,
                                in_offset=IndirectOffsetOnAxis(sslo32[:, k:k + 1], axis=0),
                                sem=gat[par])
                K[("ga", tt, k)] = gat[par].n

                @block.vector
                def _(v, tt=tt, k=k, par=par):
                    ve.wait(gat[par], K[("ga", tt, k)])
                    ve.scalar_tensor_tensor(sacc[:, par * D:(par + 1) * D],
                                            sga[:, par * D:(par + 1) * D],
                                            swv[:, k:k + 1],
                                            sacc[:, par * D:(par + 1) * D],
                                            ALU.mult, ALU.add)
                K[("k", tt, k)] = ve.n
            K[("acc", tt)] = ve.n

            @block.sync
            def _(s, tt=tt, par=par):
                sq.wait(ve, K[("acc", tt)])
                sq.start(x2_o[tt * 128:(tt + 1) * 128, :], sacc[:, par * D:(par + 1) * D],
                         sem=stx[par])

        @block.sync
        def _(s):
            for sem in (stx[0], stx[1], lds):
                s.wait_ge(sem.h, sem.n)

        @block.gpsimd
        def _(g):
            for sem in (gat[0], gat[1]):
                g.wait_ge(sem.h, sem.n)



def host_moe_combine(y_all_cores, slot_all, wv_all, sht_all, x1_all):
    yall = np.concatenate([np.asarray(y) for y in y_all_cores], 0)
    in_maps = []
    for c in range(NCORES):
        in_maps.append({
            "yall": yall,
            "slot": _f32(slot_all[c]),
            "wv": _f32(wv_all[c]),
            "sht": sht_all[c] if np.asarray(sht_all[c]).dtype == BF else _bf(sht_all[c]),
            "x1t": _f32(x1_all[c]),
        })
    return in_maps
N_HEAD = 16
N_KV = 4
HD = 128
ROPE_EPS = 1e-6


def phase_qkv(P):
    nc = P.nc
    x2 = P.dio("x2", [D, TC], F32, "in")
    wq = P.dio("wq", [D, D], F32, "in")
    wk = P.dio("wk", [D, 512], F32, "in")
    wv = P.dio("wv", [D, 512], F32, "in")
    cosr = P.dio("cosr", [128, TC], F32, "in")
    sinr = P.dio("sinr", [128, TC], F32, "in")
    ones_bf = P.dio("ones_bf", [128, 128], BF16, "in")

    qh_o = P.dio("qh", [D, TC], BF16, "out")
    kh_o = P.dio("kh", [512, TC], BF16, "out")
    v_o = P.dio("vv", [512, TC], BF16, "out")
    xn_o = P.dio("xn2", [D, TC], BF16, "out")

    NG = 4
    GWD = 512
    with contextlib.ExitStack() as ctx:
        sx = ctx.enter_context(nc.sbuf_tensor(P.nm("sx"), [128, 16 * GWD], F32))
        sxn = ctx.enter_context(nc.sbuf_tensor(P.nm("sxn"), [128, 16 * TC], BF16))
        sx2b = ctx.enter_context(nc.sbuf_tensor(P.nm("sx2b"), [128, 2 * GWD], BF16))
        sm = ctx.enter_context(nc.sbuf_tensor(P.nm("sm"), [128, GWD], F32))
        srs = ctx.enter_context(nc.sbuf_tensor(P.nm("srs"), [128, GWD], F32))
        svar = ctx.enter_context(nc.sbuf_tensor(P.nm("svar"), [128, GWD], F32))
        sonesb = ctx.enter_context(nc.sbuf_tensor(P.nm("sonesb"), [128, 128], BF16))
        sonef = ctx.enter_context(nc.sbuf_tensor(P.nm("sonef"), [128, 128], F32))
        sw = ctx.enter_context(nc.sbuf_tensor(P.nm("sw"), [128, 2 * 16 * 128], BF16))
        shead = ctx.enter_context(nc.sbuf_tensor(P.nm("shead"), [128, 2 * TC], BF16))
        sqo = ctx.enter_context(nc.sbuf_tensor(P.nm("sqo"), [128, TC], BF16))
        scos = ctx.enter_context(nc.sbuf_tensor(P.nm("scos"), [128, TC], F32))
        ssin = ctx.enter_context(nc.sbuf_tensor(P.nm("ssin"), [128, TC], F32))
        st1 = ctx.enter_context(nc.sbuf_tensor(P.nm("st1"), [128, TC], F32))
        st2 = ctx.enter_context(nc.sbuf_tensor(P.nm("st2"), [128, TC], F32))
        sq2 = ctx.enter_context(nc.sbuf_tensor(P.nm("sq2"), [128, TC], BF16))
        srms = ctx.enter_context(nc.sbuf_tensor(P.nm("srms"), [128, TC], F32))

        pS1 = ctx.enter_context(nc.psum_tensor(P.nm("pS1"), [128, 512], F32))
        pS2 = ctx.enter_context(nc.psum_tensor(P.nm("pS2"), [128, 512], F32))
        pmm = ctx.enter_context(nc.psum_tensor(P.nm("pmm"), [128, 2 * 512], F32))
        pn = ctx.enter_context(nc.psum_tensor(P.nm("pn"), [128, 2 * 512], F32))

        E = P.engines()
        block = P.block
        ve, se, pe, gq, sq = E.ve, E.se, E.pe, E.gq, E.sq
        ldc = P.sem("ldc")
        ldx = P.sem("ldx")
        ldw = [P.sem("ldw0"), P.sem("ldw1")]
        sto = P.sem("sto")

        @block.sync
        def _(s):
            sq.start(sonesb[:, :], ones_bf, sem=ldc)
            sq.start(scos[:, :], cosr, sem=ldc)
            sq.start(ssin[:, :], sinr, sem=ldc)
        ldc_done = ldc.n

        K = {}
        for g in range(NG):
            go = g * GWD

            @block.gpsimd
            def _(gg, g=g, go=go):
                if g >= 1:
                    gq.wait(ve, K[("xn", g - 1)])
                for k in range(16):
                    gq.start(sx[:, k * GWD:(k + 1) * GWD],
                             x2[k * 128:(k + 1) * 128, go:go + GWD], sem=ldx)
            K[("ldx", g)] = ldx.n

            for k in range(16):
                pp = k % 2

                @block.vector
                def _(v, g=g, k=k, pp=pp):
                    if k == 0:
                        ve.wait(ldx, K[("ldx", g)])
                        if g == 0:
                            ve.wait(ldc, ldc_done)
                            ve.memset(sonef[:, :], 1.0)
                    if k >= 2:
                        ve.wait(pe, K[("st2", g, k - 2)])
                    ve.tensor_mul(sx2b[:, pp * GWD:(pp + 1) * GWD],
                                  sx[:, k * GWD:(k + 1) * GWD],
                                  sx[:, k * GWD:(k + 1) * GWD])
                    K[("x2", g, k)] = ve.n

                @block.tensor
                def _(t, g=g, k=k, pp=pp):
                    if g > 0 and k == 0:
                        pe.wait(se, K[("rs", g - 1)])
                    pe.wait(ve, K[("x2", g, k)])
                    pe.matmul(pS1[:, :], sonef[:], sx[:, k * GWD:(k + 1) * GWD],
                              start=(k == 0), stop=(k == 15))
                    i = pe.matmul(pS2[:, :], sonesb[:], sx2b[:, pp * GWD:(pp + 1) * GWD],
                                  start=(k == 0), stop=(k == 15))
                    K[("st2", g, k)] = pe.done(i)

            @block.vector
            def _(v, g=g):
                ve.wait(pe, K[("st2", g, 15)])
                ve.tensor_scalar_mul(sm[:, :], pS1[:, :], 1.0 / D)
                ve.tensor_mul(svar[:, :], sm[:, :], sm[:, :])
                ve.scalar_tensor_tensor(svar[:, :], pS2[:, :], 1.0 / D, svar[:, :],
                                        ALU.mult, ALU.subtract)
                ve.tensor_scalar_add(svar[:, :], svar[:, :], LN_EPS)
            K[("var", g)] = ve.n

            @block.scalar
            def _(s, g=g):
                se.wait(ve, K[("var", g)])
                se.activation(srs[:, 0:GWD], svar[:, :], AF.Ln)
                se.activation(srs[:, 0:GWD], srs[:, 0:GWD], AF.Exp, scale=-0.5)
            K[("rs", g)] = se.n

            @block.vector
            def _(v, g=g, go=go):
                ve.wait(se, K[("rs", g)])
                for k in range(16):
                    ve.tensor_sub(sx[:, k * GWD:(k + 1) * GWD], sx[:, k * GWD:(k + 1) * GWD],
                                  sm[:, :])
                    ve.tensor_mul(sxn[:, k * TC + go:k * TC + go + GWD],
                                  sx[:, k * GWD:(k + 1) * GWD], srs[:, 0:GWD])
            K[("xn", g)] = ve.n

        @block.sync
        def _(s):
            sq.wait(ve, K[("xn", NG - 1)])
            for k in range(16):
                sq.start(xn_o[k * 128:(k + 1) * 128, :], sxn[:, k * TC:(k + 1) * TC], sem=sto)

        for ot in range(24):
            pp = ot % 2
            if ot < 16:
                wsrc, off = wq, ot * 128
                outdst = qh_o[ot * 128:(ot + 1) * 128, :]
            elif ot < 20:
                wsrc, off = wk, (ot - 16) * 128
                outdst = kh_o[(ot - 16) * 128:(ot - 15) * 128, :]
            else:
                wsrc, off = wv, (ot - 20) * 128
                outdst = v_o[(ot - 20) * 128:(ot - 19) * 128, :]

            @block.gpsimd
            def _(g, ot=ot, pp=pp, wsrc=wsrc, off=off):
                if ot >= 2:
                    gq.wait(pe, K[("mm", ot - 2, NG - 1)])
                gq.start(sw[:, pp * 2048:(pp + 1) * 2048],
                         wsrc[:, off:off + 128].rearrange("(k p) e -> p k e", p=128),
                         sem=ldw[pp])
            K[("ldw", ot)] = ldw[pp].base + 16 * (ot // 2 + 1)

            for g in range(NG):
                idx = ot * NG + g
                bank = idx % 2

                @block.tensor
                def _(t, ot=ot, pp=pp, g=g, idx=idx, bank=bank):
                    if g == 0:
                        pe.wait(ldw[pp], K[("ldw", ot)])
                        if ot == 0:
                            pe.wait(ve, K[("xn", NG - 1)])
                    if idx >= 2:
                        pidx = idx - 2
                        pe.wait(se, K[("ev", pidx // NG, pidx % NG)])
                    for k in range(16):
                        i = pe.matmul(pmm[:, bank * 512:(bank + 1) * 512],
                                      sw[:, pp * 2048 + k * 128:pp * 2048 + (k + 1) * 128],
                                      sxn[:, k * TC + g * GWD:k * TC + (g + 1) * GWD],
                                      start=(k == 0), stop=(k == 15))
                    K[("mm", ot, g)] = pe.done(i)

                @block.scalar
                def _(s, ot=ot, pp=pp, g=g, idx=idx, bank=bank):
                    if g == 0:
                        if ot >= 2 and ot - 2 < 20:
                            se.wait(ve, K[("rmul", ot - 2, NG - 1)])
                        elif ot >= 2:
                            se.wait(sto, sto.n)
                    se.wait(pe, K[("mm", ot, g)])
                    se.copy(shead[:, pp * TC + g * GWD:pp * TC + (g + 1) * GWD],
                            pmm[:, bank * 512:(bank + 1) * 512])
                    K[("ev", ot, g)] = se.n

            if ot >= 20:
                @block.sync
                def _(s, ot=ot, pp=pp, outdst=outdst):
                    sq.wait(se, K[("ev", ot, NG - 1)])
                    sq.start(outdst, shead[:, pp * TC:(pp + 1) * TC], sem=sto)
            else:
                @block.vector
                def _(v, ot=ot, pp=pp):
                    ve.wait(se, K[("ev", ot, NG - 1)])
                    if ot >= 1:
                        ve.wait(sto, sto.n)
                    x1 = shead[:, pp * TC:(pp + 1) * TC]
                    ve.tensor_copy(st2[0:64, 0:TC], x1[64:128, :])
                    ve.tensor_copy(st2[64:128, 0:TC], x1[0:64, :])
                    ve.tensor_tensor(st1[0:64, 0:TC], x1[0:64, :], scos[0:64, :], ALU.mult)
                    ve.tensor_tensor(st1[64:128, 0:TC], x1[64:128, :], scos[64:128, :], ALU.mult)
                    ve.tensor_tensor(st2[0:64, 0:TC], st2[0:64, 0:TC], ssin[0:64, :], ALU.mult)
                    ve.tensor_tensor(st2[64:128, 0:TC], st2[64:128, 0:TC], ssin[64:128, :], ALU.mult)
                    ve.tensor_add(st1[0:64, 0:TC], st1[0:64, 0:TC], st2[0:64, 0:TC])
                    ve.tensor_sub(st1[64:128, 0:TC], st1[64:128, 0:TC], st2[64:128, 0:TC])
                    ve.tensor_mul(st2[:, 0:TC], st1[:, 0:TC], st1[:, 0:TC])
                    ve.tensor_copy(sq2[:, 0:TC], st2[:, 0:TC])
                K[("rope", ot)] = ve.n

                for g in range(NG):
                    @block.tensor
                    def _(t, ot=ot, g=g):
                        pe.wait(ve, K[("rope", ot)])
                        prev = K.get(("rv", ot, g - 2) if g >= 2 else ("rv", ot - 1, g + 2))
                        if prev:
                            pe.wait(ve, prev)
                        i = pe.matmul(pn[:, (g % 2) * 512:(g % 2) * 512 + 512], sonesb[:, :],
                                      sq2[:, g * 512:(g + 1) * 512], start=True, stop=True)
                        K[("rn", ot, g)] = pe.done(i)

                    @block.vector
                    def _(v, ot=ot, g=g):
                        ve.wait(pe, K[("rn", ot, g)])
                        ve.tensor_scalar(srms[:, g * 512:(g + 1) * 512],
                                         pn[:, (g % 2) * 512:(g % 2) * 512 + 512],
                                         1.0 / HD, ROPE_EPS, ALU.mult, ALU.add)
                    K[("rv", ot, g)] = ve.n

                    @block.scalar
                    def _(s, ot=ot, g=g):
                        se.wait(ve, K[("rv", ot, g)])
                        se.activation(srms[:, g * 512:(g + 1) * 512],
                                      srms[:, g * 512:(g + 1) * 512], AF.Ln)
                        se.activation(srms[:, g * 512:(g + 1) * 512],
                                      srms[:, g * 512:(g + 1) * 512], AF.Exp, scale=-0.5)
                    K[("srm", ot, g)] = se.n

                    @block.vector
                    def _(v, ot=ot, g=g):
                        ve.wait(se, K[("srm", ot, g)])
                        ve.tensor_mul(sqo[:, g * GWD:(g + 1) * GWD],
                                      st1[:, g * GWD:(g + 1) * GWD],
                                      srms[:, g * 512:(g + 1) * 512])
                    K[("rmul", ot, g)] = ve.n

                @block.sync
                def _(s, ot=ot, outdst=outdst):
                    for g in range(NG):
                        sq.wait(ve, K[("rmul", ot, g)])
                    sq.start(outdst, sqo[:, :], sem=sto)

        @block.sync
        def _(s):
            for sem in (sto, ldc):
                s.wait_ge(sem.h, sem.n)

        @block.gpsimd
        def _(g):
            for sem in (ldx, ldw[0], ldw[1]):
                g.wait_ge(sem.h, sem.n)



def host_qkv(x2_all, wq, wk, wv):
    ones_bf = np.ones((128, 128), dtype=BF)
    half = HD // 2
    inv_freq = 1.0 / (100000.0 ** (np.arange(half, dtype=np.float32) / half))
    in_maps = []
    for c in range(NCORES):
        s0 = c * TC
        t0 = s0 % T
        pos = np.arange(t0, t0 + TC, dtype=np.float32)
        fr = pos[None, :] * inv_freq[:, None]
        in_maps.append({
            "x2": _f32(np.asarray(x2_all[c % len(x2_all)]).T),
            "wq": _f32(wq), "wk": _f32(wk), "wv": _f32(wv),
            "cosr": _f32(np.concatenate([np.cos(fr), np.cos(fr)], 0)),
            "sinr": _f32(np.concatenate([np.sin(fr), np.sin(fr)], 0)),
            "ones_bf": ones_bf,
        })
    return in_maps



def phase_attn(P):
    nc = P.nc
    NU = 4
    qh = P.dio("qh", [NU * 128, T], BF16, "in")
    kh = P.dio("kh", [NU * 128, T], BF16, "in")
    vp = P.dio("vp", [NU * T, 128], BF16, "in")
    dmask = P.dio("dmask", [128, 4 * 512], F32, "in")
    ones_bf = P.dio("ones_bf", [128, 128], BF16, "in")
    y_out = P.dio("yat", [NU * 128, T], BF16, "out")

    NQT = T // 512
    SCALE = 1.0 / np.sqrt(128.0)

    with contextlib.ExitStack() as ctx:
        sq_ = ctx.enter_context(nc.sbuf_tensor(P.nm("sq_"), [128, 2 * T], BF16))
        sk = ctx.enter_context(nc.sbuf_tensor(P.nm("sk"), [128, 2 * T], BF16))
        sv = ctx.enter_context(nc.sbuf_tensor(P.nm("sv"), [128, 2 * T], BF16))
        sp = ctx.enter_context(nc.sbuf_tensor(P.nm("sp"), [128, 2 * 512], BF16))
        sms = ctx.enter_context(nc.sbuf_tensor(P.nm("sms"), [128, 2 * 512], F32))
        smk = ctx.enter_context(nc.sbuf_tensor(P.nm("smk"), [128, 4 * 512], F32))
        sonesb = ctx.enter_context(nc.sbuf_tensor(P.nm("sonesb"), [128, 128], BF16))
        sli = ctx.enter_context(nc.sbuf_tensor(P.nm("sli"), [128, 512], F32))
        sy = ctx.enter_context(nc.sbuf_tensor(P.nm("sy"), [128, 2 * 512], BF16))

        pS = ctx.enter_context(nc.psum_tensor(P.nm("pS"), [128, 2 * 512], F32))
        pL = ctx.enter_context(nc.psum_tensor(P.nm("pL"), [128, 512], F32))
        pY = ctx.enter_context(nc.psum_tensor(P.nm("pY"), [128, 512], F32))

        E = P.engines()
        block = P.block
        ve, se, pe, gq, sq = E.ve, E.se, E.pe, E.gq, E.sq
        ldc = P.sem("ldc")
        ldq = [P.sem("ldq0"), P.sem("ldq1")]
        ldk = [P.sem("ldk0"), P.sem("ldk1")]
        ldv = [P.sem("ldv0"), P.sem("ldv1")]
        sty = [P.sem("sty0"), P.sem("sty1")]

        @block.sync
        def _(s):
            sq.start(sonesb[:, :], ones_bf, sem=ldc)
            sq.start(smk[:, :], dmask, sem=ldc)
        ldc_done = ldc.n

        K = {}
        for u in range(4):
            upp = u % 2

            @block.gpsimd
            def _(g, u=u, upp=upp):
                if u >= 2:
                    gq.wait(pe, K[("ulast", u - 2)])
                gq.start(sq_[:, upp * T:(upp + 1) * T], qh[u * 128:(u + 1) * 128, :],
                         sem=ldq[upp])
                gq.start(sk[:, upp * T:(upp + 1) * T], kh[u * 128:(u + 1) * 128, :],
                         sem=ldk[upp])
                gq.start(sv[:, upp * T:(upp + 1) * T],
                         vp[u * T:(u + 1) * T, :].rearrange("(n p) d -> p n d", p=128),
                         sem=ldv[upp])
            K[("ld", u)] = (ldq[upp].base + 16 * (u // 2 + 1))

            for qt in range(NQT):
                nkt = 4 * qt + 4
                for kt in range(nkt):
                    it = (u, qt, kt)
                    par = kt % 2
                    diag = (kt >= 4 * qt)

                    @block.tensor
                    def _(t, u=u, upp=upp, qt=qt, kt=kt, par=par):
                        if (u, qt, kt) == (0, 0, 0):
                            pe.wait(ldc, ldc_done)
                        if qt == 0 and kt == 0:
                            rel = K[("ld", u)] - ldq[upp].base
                            pe.wait(ldq[upp], ldq[upp].base + rel)
                            pe.wait(ldk[upp], ldk[upp].base + rel)
                            pe.wait(ldv[upp], ldv[upp].base + rel)
                        prev = K.get(("exp", u, qt, kt - 2)) or \
                            (K.get(("exp", u, qt - 1, 4 * qt - 1)) if kt - 2 < 0 and qt > 0 else None) or \
                            (K.get(("exp", u - 1, NQT - 1, 4 * NQT - 1)) if kt - 2 < 0 and qt == 0 and u > 0 else None)
                        if kt >= 2:
                            pe.wait(se, K[("exp", u, qt, kt - 2)])
                        elif (qt > 0 or u > 0):
                            pk = (u, qt - 1, 4 * (qt - 1) + 3) if qt > 0 else (u - 1, NQT - 1, 4 * (NQT - 1) + 3)
                            if kt == 0:
                                pe.wait(se, K[("exp", pk[0], pk[1], pk[2] - 1)])
                            else:
                                pe.wait(se, K[("exp", pk[0], pk[1], pk[2])])
                        i = pe.matmul(pS[:, par * 512:(par + 1) * 512],
                                      sk[:, upp * T + kt * 128:upp * T + (kt + 1) * 128],
                                      sq_[:, upp * T + qt * 512:upp * T + (qt + 1) * 512],
                                      start=True, stop=True)
                        K[("smm", u, qt, kt)] = pe.done(i)

                    if diag:
                        @block.vector
                        def _(v, u=u, qt=qt, kt=kt, par=par):
                            ve.wait(pe, K[("smm", u, qt, kt)])
                            mi = kt - 4 * qt
                            ve.tensor_tensor(sms[:, par * 512:(par + 1) * 512],
                                             pS[:, par * 512:(par + 1) * 512],
                                             smk[:, mi * 512:(mi + 1) * 512], ALU.min)
                        K[("min", u, qt, kt)] = ve.n

                    @block.scalar
                    def _(s, u=u, qt=qt, kt=kt, par=par, diag=diag):
                        if diag:
                            se.wait(ve, K[("min", u, qt, kt)])
                            src = sms[:, par * 512:(par + 1) * 512]
                        else:
                            se.wait(pe, K[("smm", u, qt, kt)])
                            src = pS[:, par * 512:(par + 1) * 512]
                        if kt >= 2:
                            se.wait(pe, K[("av", u, qt, kt - 2)])
                        se.activation(sp[:, par * 512:(par + 1) * 512], src, AF.Exp,
                                      scale=SCALE)
                    K[("exp", u, qt, kt)] = se.n

                    @block.tensor
                    def _(t, u=u, upp=upp, qt=qt, kt=kt, par=par):
                        pe.wait(se, K[("exp", u, qt, kt)])
                        if kt == 0 and (qt > 0 or u > 0):
                            pu, pqt = (u, qt - 1) if qt > 0 else (u - 1, NQT - 1)
                            pe.wait(ve, K[("ep", pu, pqt)])
                        i = pe.matmul(pL[:, :], sonesb[:, :], sp[:, par * 512:(par + 1) * 512],
                                      start=(kt == 0), stop=(kt == 4 * qt + 3))
                        i = pe.matmul(pY[:, :], sv[:, upp * T + kt * 128:upp * T + (kt + 1) * 128],
                                      sp[:, par * 512:(par + 1) * 512],
                                      start=(kt == 0), stop=(kt == 4 * qt + 3))
                        K[("av", u, qt, kt)] = pe.done(i)

                @block.vector
                def _(v, u=u, qt=qt):
                    ve.wait(pe, K[("av", u, qt, 4 * qt + 3)])
                    gi = u * NQT + qt
                    if gi >= 2:
                        ve.wait(sty[gi % 2], sty[gi % 2].base + 16 * (gi // 2))
                    ve.reciprocal(sli[:, :], pL[:, :])
                    ve.tensor_tensor(sy[:, (gi % 2) * 512:(gi % 2 + 1) * 512],
                                     pY[:, :], sli[:, :], ALU.mult)
                K[("ep", u, qt)] = ve.n

                @block.sync
                def _(s, u=u, qt=qt):
                    sq.wait(ve, K[("ep", u, qt)])
                    gi = u * NQT + qt
                    sq.start(y_out[u * 128:(u + 1) * 128, qt * 512:(qt + 1) * 512],
                             sy[:, (gi % 2) * 512:(gi % 2 + 1) * 512], sem=sty[gi % 2])
            K[("ulast", u)] = pe.n

        @block.sync
        def _(s):
            for sem in (sty[0], sty[1], ldc):
                s.wait_ge(sem.h, sem.n)

        @block.gpsimd
        def _(g):
            for sem in (ldq[0], ldq[1], ldk[0], ldk[1], ldv[0], ldv[1]):
                g.wait_ge(sem.h, sem.n)



def host_attn(qh_all, kh_all, v_all):
    qh_full = np.zeros((2, N_HEAD * 128, T), dtype=BF)
    kh_full = np.zeros((2, N_KV * 128, T), dtype=BF)
    v_full = np.zeros((2, N_KV * 128, T), dtype=BF)
    for c in range(NCORES):
        s0 = c * TC
        b0, t0 = s0 // T, s0 % T
        qh_full[b0, :, t0:t0 + TC] = np.asarray(qh_all[c])
        kh_full[b0, :, t0:t0 + TC] = np.asarray(kh_all[c])
        v_full[b0, :, t0:t0 + TC] = np.asarray(v_all[c])
    MASKV = -10000.0 * np.sqrt(128.0)
    dmask = np.zeros((128, 4 * 512), np.float32)
    for mi in range(4):
        q_idx = np.arange(512)[None, :]
        k_idx = (np.arange(128) + mi * 128)[:, None]
        dmask[:, mi * 512:(mi + 1) * 512] = np.where(q_idx >= k_idx, 3e38, MASKV)
    ones_bf = np.ones((128, 128), dtype=BF)
    in_maps = []
    for c in range(NCORES):
        qs, ks, vs, = [], [], []
        for un in range(4):
            uid = c * 4 + un
            b, h = uid // 16, uid % 16
            kvh = h // 4
            qs.append(qh_full[b, h * 128:(h + 1) * 128, :])
            ks.append(kh_full[b, kvh * 128:(kvh + 1) * 128, :])
            vs.append(np.ascontiguousarray(v_full[b, kvh * 128:(kvh + 1) * 128, :].T))
        in_maps.append({
            "qh": np.concatenate(qs, 0), "kh": np.concatenate(ks, 0),
            "vp": np.concatenate(vs, 0),
            "dmask": dmask, "ones_bf": ones_bf,
        })
    return in_maps



def phase_attnout(P):
    nc = P.nc
    ya = P.dio("ya", [D, TC], BF16, "in")
    wo = P.dio("wo", [D, D], F32, "in")
    x2 = P.dio("x2", [D, TC], F32, "in")
    x3_o = P.dio("x3", [D, TC], F32, "out")

    NG = 4
    GWD = 512
    with contextlib.ExitStack() as ctx:
        sy = ctx.enter_context(nc.sbuf_tensor(P.nm("sy"), [128, 16 * TC], BF16))
        sw = ctx.enter_context(nc.sbuf_tensor(P.nm("sw"), [128, 2 * 2048], BF16))
        sr = ctx.enter_context(nc.sbuf_tensor(P.nm("sr"), [128, 2 * GWD], F32))
        so = ctx.enter_context(nc.sbuf_tensor(P.nm("so"), [128, 2 * GWD], F32))
        pmm = ctx.enter_context(nc.psum_tensor(P.nm("pmm"), [128, 2 * 512], F32))

        E = P.engines()
        block = P.block
        ve, pe, gq, sq = E.ve, E.pe, E.gq, E.sq
        ldy = P.sem("ldy")
        ldw = [P.sem("ldw0"), P.sem("ldw1")]
        ldr = [P.sem("ldr0"), P.sem("ldr1")]
        stx = [P.sem("stx0"), P.sem("stx1")]

        @block.sync
        def _(s):
            for k in range(16):
                sq.start(sy[:, k * TC:(k + 1) * TC], ya[k * 128:(k + 1) * 128, :], sem=ldy)
        ldy_done = ldy.n

        K = {}
        for gi in range(64):
            ot, grp = gi // 4, gi % 4
            pp = ot % 2
            spp = gi % 2
            go = grp * GWD

            if grp == 0:
                @block.gpsimd
                def _(g, ot=ot, pp=pp):
                    if ot >= 2:
                        gq.wait(pe, K[("mm", (ot - 1) * 4 + 3)])
                    gq.start(sw[:, pp * 2048:(pp + 1) * 2048],
                             wo[:, ot * 128:(ot + 1) * 128].rearrange("(k p) e -> p k e", p=128),
                             sem=ldw[pp])

            @block.sync
            def _(s, gi=gi, ot=ot, go=go, spp=spp):
                if gi >= 2:
                    sq.wait(ve, K[("o", gi - 2)])
                sq.start(sr[:, spp * GWD:(spp + 1) * GWD],
                         x2[ot * 128:(ot + 1) * 128, go:go + GWD], sem=ldr[spp])
            K[("ldr", gi)] = ldr[spp].n

            @block.tensor
            def _(t, gi=gi, ot=ot, grp=grp, pp=pp, spp=spp, go=go):
                if grp == 0:
                    pe.wait(ldw[pp], ldw[pp].base + 16 * (ot // 2 + 1))
                    if ot == 0:
                        pe.wait(ldy, ldy_done)
                if gi >= 2:
                    pe.wait(ve, K[("o", gi - 2)])
                for k in range(16):
                    i = pe.matmul(pmm[:, spp * 512:(spp + 1) * 512],
                                  sw[:, pp * 2048 + k * 128:pp * 2048 + (k + 1) * 128],
                                  sy[:, k * TC + go:k * TC + go + GWD],
                                  start=(k == 0), stop=(k == 15))
                K[("mm", gi)] = pe.done(i)

            @block.vector
            def _(v, gi=gi, spp=spp):
                ve.wait(pe, K[("mm", gi)])
                ve.wait(ldr[spp], K[("ldr", gi)])
                if gi >= 2:
                    ve.wait(stx[spp], stx[spp].base + 16 * (gi // 2))
                ve.tensor_add(so[:, spp * GWD:(spp + 1) * GWD],
                              pmm[:, spp * 512:(spp + 1) * 512],
                              sr[:, spp * GWD:(spp + 1) * GWD])
            K[("o", gi)] = ve.n

            @block.sync
            def _(s, gi=gi, ot=ot, go=go, spp=spp):
                sq.wait(ve, K[("o", gi)])
                sq.start(x3_o[ot * 128:(ot + 1) * 128, go:go + GWD],
                         so[:, spp * GWD:(spp + 1) * GWD], sem=stx[spp])

        @block.sync
        def _(s):
            for sem in (stx[0], stx[1], ldy, ldr[0], ldr[1]):
                s.wait_ge(sem.h, sem.n)

        @block.gpsimd
        def _(g):
            for sem in (ldw[0], ldw[1]):
                g.wait_ge(sem.h, sem.n)



def host_attnout(yat_all, wo, x2_all):
    y_full = np.zeros((2, N_HEAD * 128, T), dtype=BF)
    for c in range(NCORES):
        for un in range(4):
            uid = c * 4 + un
            b, h = uid // 16, uid % 16
            y_full[b, h * 128:(h + 1) * 128, :] = np.asarray(yat_all[c])[un * 128:(un + 1) * 128, :]
    in_maps = []
    for c in range(NCORES):
        s0 = c * TC
        b0, t0 = s0 // T, s0 % T
        in_maps.append({
            "ya": np.ascontiguousarray(y_full[b0, :, t0:t0 + TC]),
            "wo": _f32(wo),
            "x2": _f32(x2_all[c]),
        })
    return in_maps


INTER = 5632
NIT = 44


def phase_ffn(P):
    nc = P.nc
    x3 = P.dio("x3", [D, TC], F32, "in")
    gwT = P.dio("gwT", [D, INTER], F32, "in")
    uwT = P.dio("uwT", [D, INTER], F32, "in")
    dwT = P.dio("dwT", [INTER, D], F32, "in")
    gs = P.dio("gs", [128, NIT], F32, "in")
    us = P.dio("us", [128, NIT], F32, "in")
    ds = P.dio("ds", [128, 16], F32, "in")
    ones_bf = P.dio("ones_bf", [128, 128], BF16, "in")
    x4_o = P.dio("x4", [D, TC], F32, "out")
    h_d = P.dio("h_d", [INTER, TC], BF16, "int")

    NG = 4
    GWD = 512
    with contextlib.ExitStack() as ctx:
        sx = ctx.enter_context(nc.sbuf_tensor(P.nm("sx"), [128, 16 * GWD], F32))
        sxn = ctx.enter_context(nc.sbuf_tensor(P.nm("sxn"), [128, 16 * TC], BF16))
        sx2b = ctx.enter_context(nc.sbuf_tensor(P.nm("sx2b"), [128, 2 * GWD], BF16))
        sm = ctx.enter_context(nc.sbuf_tensor(P.nm("sm"), [128, GWD], F32))
        srs = ctx.enter_context(nc.sbuf_tensor(P.nm("srs"), [128, GWD], F32))
        svar = ctx.enter_context(nc.sbuf_tensor(P.nm("svar"), [128, GWD], F32))
        sonesb = ctx.enter_context(nc.sbuf_tensor(P.nm("sonesb"), [128, 128], BF16))
        sonef = ctx.enter_context(nc.sbuf_tensor(P.nm("sonef"), [128, 128], F32))
        sgw = ctx.enter_context(nc.sbuf_tensor(P.nm("sgw"), [128, 2 * 2048], BF16))
        suw = ctx.enter_context(nc.sbuf_tensor(P.nm("suw"), [128, 2 * 2048], BF16))
        sdw = ctx.enter_context(nc.sbuf_tensor(P.nm("sdw"), [128, 2 * NIT * 128], BF16))
        sgs = ctx.enter_context(nc.sbuf_tensor(P.nm("sgs"), [128, NIT], F32))
        sus = ctx.enter_context(nc.sbuf_tensor(P.nm("sus"), [128, NIT], F32))
        sds = ctx.enter_context(nc.sbuf_tensor(P.nm("sds"), [128, 16], F32))
        sg = ctx.enter_context(nc.sbuf_tensor(P.nm("sg"), [128, TC], BF16))
        sh = ctx.enter_context(nc.sbuf_tensor(P.nm("sh"), [128, TC], BF16))
        shl = ctx.enter_context(nc.sbuf_tensor(P.nm("shl"), [128, NIT * GWD], BF16))
        sr = ctx.enter_context(nc.sbuf_tensor(P.nm("sr"), [128, 2 * GWD], F32))
        so = ctx.enter_context(nc.sbuf_tensor(P.nm("so"), [128, 2 * GWD], F32))

        pS1 = ctx.enter_context(nc.psum_tensor(P.nm("pS1"), [128, 512], F32))
        pS2 = ctx.enter_context(nc.psum_tensor(P.nm("pS2"), [128, 512], F32))
        pg = ctx.enter_context(nc.psum_tensor(P.nm("pg"), [128, 2 * 512], F32))
        pu = ctx.enter_context(nc.psum_tensor(P.nm("pu"), [128, 2 * 512], F32))
        pd = ctx.enter_context(nc.psum_tensor(P.nm("pd"), [128, 2 * 512], F32))

        E = P.engines()
        block = P.block
        ve, se, pe, gq, sq = E.ve, E.se, E.pe, E.gq, E.sq
        ldc = P.sem("ldc")
        ldx = P.sem("ldx")
        ldg = [P.sem("ldg0"), P.sem("ldg1")]
        ldu = [P.sem("ldu0"), P.sem("ldu1")]
        ldd = [P.sem("ldd0"), P.sem("ldd1")]
        ldh = [P.sem("ldh0"), P.sem("ldh1")]
        ldr = [P.sem("ldr0"), P.sem("ldr1")]
        sth = [P.sem("sth0"), P.sem("sth1")]
        stx = [P.sem("stx0"), P.sem("stx1")]

        @block.sync
        def _(s):
            sq.start(sonesb[:, :], ones_bf, sem=ldc)
            sq.start(sgs[:, :], gs, sem=ldc)
            sq.start(sus[:, :], us, sem=ldc)
            sq.start(sds[:, :], ds, sem=ldc)
        ldc_done = ldc.n

        K = {}
        for g in range(NG):
            go = g * GWD

            @block.gpsimd
            def _(gg, g=g, go=go):
                if g >= 1:
                    gq.wait(ve, K[("xn", g - 1)])
                for k in range(16):
                    gq.start(sx[:, k * GWD:(k + 1) * GWD],
                             x3[k * 128:(k + 1) * 128, go:go + GWD], sem=ldx)
            K[("ldx", g)] = ldx.n

            for k in range(16):
                pp = k % 2

                @block.vector
                def _(v, g=g, k=k, pp=pp):
                    if k == 0:
                        ve.wait(ldx, K[("ldx", g)])
                        if g == 0:
                            ve.wait(ldc, ldc_done)
                            ve.memset(sonef[:, :], 1.0)
                    if k >= 2:
                        ve.wait(pe, K[("st2", g, k - 2)])
                    ve.tensor_mul(sx2b[:, pp * GWD:(pp + 1) * GWD],
                                  sx[:, k * GWD:(k + 1) * GWD],
                                  sx[:, k * GWD:(k + 1) * GWD])
                    K[("x2", g, k)] = ve.n

                @block.tensor
                def _(t, g=g, k=k, pp=pp):
                    if g > 0 and k == 0:
                        pe.wait(se, K[("rs", g - 1)])
                    pe.wait(ve, K[("x2", g, k)])
                    pe.matmul(pS1[:, :], sonef[:], sx[:, k * GWD:(k + 1) * GWD],
                              start=(k == 0), stop=(k == 15))
                    i = pe.matmul(pS2[:, :], sonesb[:], sx2b[:, pp * GWD:(pp + 1) * GWD],
                                  start=(k == 0), stop=(k == 15))
                    K[("st2", g, k)] = pe.done(i)

            @block.vector
            def _(v, g=g):
                ve.wait(pe, K[("st2", g, 15)])
                ve.tensor_scalar_mul(sm[:, :], pS1[:, :], 1.0 / D)
                ve.tensor_mul(svar[:, :], sm[:, :], sm[:, :])
                ve.scalar_tensor_tensor(svar[:, :], pS2[:, :], 1.0 / D, svar[:, :],
                                        ALU.mult, ALU.subtract)
                ve.tensor_scalar_add(svar[:, :], svar[:, :], LN_EPS)
            K[("var", g)] = ve.n

            @block.scalar
            def _(s, g=g):
                se.wait(ve, K[("var", g)])
                se.activation(srs[:, 0:GWD], svar[:, :], AF.Ln)
                se.activation(srs[:, 0:GWD], srs[:, 0:GWD], AF.Exp, scale=-0.5)
            K[("rs", g)] = se.n

            @block.vector
            def _(v, g=g, go=go):
                ve.wait(se, K[("rs", g)])
                for k in range(16):
                    ve.tensor_sub(sx[:, k * GWD:(k + 1) * GWD], sx[:, k * GWD:(k + 1) * GWD],
                                  sm[:, :])
                    ve.tensor_mul(sxn[:, k * TC + go:k * TC + go + GWD],
                                  sx[:, k * GWD:(k + 1) * GWD], srs[:, 0:GWD])
            K[("xn", g)] = ve.n

        for it in range(NIT):
            pp = it % 2

            @block.gpsimd
            def _(g, it=it, pp=pp):
                if it >= 2:
                    gq.wait(pe, K[("gum", it - 2)])
                gq.start(sgw[:, pp * 2048:(pp + 1) * 2048],
                         gwT[:, it * 128:(it + 1) * 128].rearrange("(k p) e -> p k e", p=128),
                         sem=ldg[pp])
                gq.start(suw[:, pp * 2048:(pp + 1) * 2048],
                         uwT[:, it * 128:(it + 1) * 128].rearrange("(k p) e -> p k e", p=128),
                         sem=ldu[pp])

            for grp in range(NG):
                @block.tensor
                def _(t, it=it, pp=pp, grp=grp):
                    if grp == 0:
                        pe.wait(ldg[pp], ldg[pp].base + 16 * (it // 2 + 1))
                        pe.wait(ldu[pp], ldu[pp].base + 16 * (it // 2 + 1))
                        if it == 0:
                            pe.wait(ve, K[("xn", NG - 1)])
                    prev = K.get(("h", it, grp - 2) if grp >= 2 else ((it - 1, grp + 2)
                                 and ("h", it - 1, grp + 2)))
                    if prev:
                        pe.wait(ve, prev)
                    for k in range(16):
                        pe.matmul(pg[:, (grp % 2) * 512:(grp % 2 + 1) * 512],
                                  sgw[:, pp * 2048 + k * 128:pp * 2048 + (k + 1) * 128],
                                  sxn[:, k * TC + grp * GWD:k * TC + (grp + 1) * GWD],
                                  start=(k == 0), stop=(k == 15))
                        i = pe.matmul(pu[:, (grp % 2) * 512:(grp % 2 + 1) * 512],
                                      suw[:, pp * 2048 + k * 128:pp * 2048 + (k + 1) * 128],
                                      sxn[:, k * TC + grp * GWD:k * TC + (grp + 1) * GWD],
                                      start=(k == 0), stop=(k == 15))
                    K[("gum", it, grp)] = pe.done(i)
                    if grp == NG - 1:
                        K[("gum", it)] = pe.n

                @block.scalar
                def _(s, it=it, grp=grp):
                    se.wait(pe, K[("gum", it, grp)])
                    se.activation(sg[:, grp * GWD:(grp + 1) * GWD],
                                  pg[:, (grp % 2) * 512:(grp % 2 + 1) * 512], AF.Silu,
                                  scale=sgs[:, it:it + 1])
                    K[("sg", it, grp)] = se.n

                @block.vector
                def _(v, it=it, grp=grp):
                    ve.wait(se, K[("sg", it, grp)])
                    if grp == 0 and it >= 1:
                        ve.wait(sth[0], K[("sth", it - 1)])
                    ve.scalar_tensor_tensor(sh[:, grp * GWD:(grp + 1) * GWD],
                                            pu[:, (grp % 2) * 512:(grp % 2 + 1) * 512],
                                            sus[:, it:it + 1],
                                            sg[:, grp * GWD:(grp + 1) * GWD],
                                            ALU.mult, ALU.mult)
                K[("h", it, grp)] = ve.n

            @block.sync
            def _(s, it=it):
                sq.wait(ve, K[("h", it, NG - 1)])
                sq.start(h_d[it * 128:(it + 1) * 128, :], sh[:, 0:TC], sem=sth[0])
            K[("sth", it)] = sth[0].n

        for grp in range(NG):
            go = grp * GWD

            @block.sync
            def _(s, grp=grp, go=go):
                if grp == 0:
                    s.wait_ge(sth[0].h, sth[0].n)
                else:
                    sq.wait(pe, K[("dmm", grp - 1, 15)])
                for it in range(NIT):
                    sq.start(shl[:, it * GWD:(it + 1) * GWD],
                             h_d[it * 128:(it + 1) * 128, go:go + GWD], sem=ldh[grp % 2])
            K[("ldh", grp)] = ldh[grp % 2].n

            for ot in range(16):
                gi = grp * 16 + ot
                pp = gi % 2

                @block.gpsimd
                def _(g, grp=grp, ot=ot, gi=gi, pp=pp):
                    if gi >= 2:
                        gq.wait(pe, K[("dmm", (gi - 2) // 16, (gi - 2) % 16)])
                    gq.start(sdw[:, pp * NIT * 128:(pp + 1) * NIT * 128],
                             dwT[:, ot * 128:(ot + 1) * 128]
                             .rearrange("(k p) e -> p k e", p=128),
                             sem=ldd[pp])

                @block.sync
                def _(s, grp=grp, go=go, ot=ot, gi=gi, pp=pp):
                    if gi >= 2:
                        sq.wait(ve, K[("o", gi - 2)])
                    sq.start(sr[:, (gi % 2) * GWD:(gi % 2 + 1) * GWD],
                             x3[ot * 128:(ot + 1) * 128, go:go + GWD], sem=ldr[gi % 2])
                K[("ldr", gi)] = ldr[gi % 2].n

                @block.tensor
                def _(t, grp=grp, ot=ot, gi=gi, pp=pp):
                    pe.wait(ldd[pp], ldd[pp].base + 16 * (gi // 2 + 1))
                    if ot == 0:
                        pe.wait(ldh[grp % 2], K[("ldh", grp)])
                    if gi >= 2:
                        pe.wait(ve, K[("o", gi - 2)])
                    for k in range(NIT):
                        i = pe.matmul(pd[:, (gi % 2) * 512:(gi % 2 + 1) * 512],
                                      sdw[:, pp * NIT * 128 + k * 128:pp * NIT * 128 + (k + 1) * 128],
                                      shl[:, k * GWD:(k + 1) * GWD],
                                      start=(k == 0), stop=(k == NIT - 1))
                    K[("dmm", grp, ot)] = pe.done(i)

                @block.vector
                def _(v, grp=grp, ot=ot, gi=gi):
                    ve.wait(pe, K[("dmm", grp, ot)])
                    ve.wait(ldr[gi % 2], K[("ldr", gi)])
                    if gi >= 2:
                        ve.wait(stx[gi % 2], stx[gi % 2].base + 16 * (gi // 2))
                    ve.scalar_tensor_tensor(so[:, (gi % 2) * GWD:(gi % 2 + 1) * GWD],
                                            pd[:, (gi % 2) * 512:(gi % 2 + 1) * 512],
                                            sds[:, ot:ot + 1],
                                            sr[:, (gi % 2) * GWD:(gi % 2 + 1) * GWD],
                                            ALU.mult, ALU.add)
                K[("o", gi)] = ve.n

                @block.sync
                def _(s, grp=grp, go=go, ot=ot, gi=gi):
                    sq.wait(ve, K[("o", gi)])
                    sq.start(x4_o[ot * 128:(ot + 1) * 128, go:go + GWD],
                             so[:, (gi % 2) * GWD:(gi % 2 + 1) * GWD], sem=stx[gi % 2])

        @block.sync
        def _(s):
            for sem in (stx[0], stx[1], sth[0], ldc, ldr[0], ldr[1], ldh[0], ldh[1]):
                s.wait_ge(sem.h, sem.n)

        @block.gpsimd
        def _(g):
            for sem in (ldx, ldg[0], ldg[1], ldu[0], ldu[1], ldd[0], ldd[1]):
                g.wait_ge(sem.h, sem.n)



def host_ffn(x3_all, gate_w, up_w, down_w, gate_s, up_s, down_s):
    gwT = _f32(gate_w.T)
    uwT = _f32(up_w.T)
    dwT = _f32(down_w.T)
    gs_t = _f32(gate_s.reshape(NIT, 128).T)
    us_t = _f32(up_s.reshape(NIT, 128).T)
    ds_t = _f32(down_s.reshape(16, 128).T)
    ones_bf = np.ones((128, 128), dtype=BF)
    in_maps = []
    for c in range(NCORES):
        in_maps.append({
            "x3": _f32(x3_all[c]),
            "gwT": gwT, "uwT": uwT, "dwT": dwT,
            "gs": gs_t, "us": us_t, "ds": ds_t,
            "ones_bf": ones_bf,
        })
    return in_maps


def _standalone(phase_fn, prefix):
    nc = bass.Bass("TRN2", target_bir_lowering=False)
    ctx = contextlib.ExitStack()
    with ctx:
        P = Phase(nc, ctx, prefix)
        phase_fn(P)
    return nc


def build_mamba_front():
    return _standalone(phase_mamba_front, "m1_")


def build_ssd():
    return _standalone(phase_ssd, "m2_")


def build_mgate():
    return _standalone(phase_mgate, "m3_")


def build_moe_route():
    return _standalone(phase_moe_route, "r4_")


def build_moe_expert():
    return _standalone(phase_moe_expert, "r5_")


def build_moe_combine():
    return _standalone(phase_moe_combine, "r6_")


def build_qkv():
    return _standalone(phase_qkv, "q7_")


def build_attn():
    return _standalone(phase_attn, "a8_")


def build_attnout():
    return _standalone(phase_attnout, "o9_")


def build_ffn():
    return _standalone(phase_ffn, "f10_")



_NC_CACHE = {}


def _get(name, builder):
    if name not in _NC_CACHE:
        _NC_CACHE[name] = builder()
    return _NC_CACHE[name]


def kernel(**inputs):
    return kernel_fused(**inputs)


def kernel_unfused(**inputs):
    I = {k: np.asarray(v) for k, v in inputs.items()}
    u_flat = _f32(I["L0_attn_u"].reshape(NTOK, D))
    R = list(range(NCORES))

    r1 = run_bass_kernel_spmd(_get("l1", build_mamba_front),
                              host_mamba_front(u_flat, I["L0_attn_in_proj_w"],
                                               I["L0_attn_conv_w"], I["L0_attn_conv_b"],
                                               I["L0_attn_dt_bias"], I["L0_attn_A_log"]),
                              core_ids=R).results
    r2 = run_bass_kernel_spmd(_get("l2", build_ssd),
                              host_ssd([r1[c]["xc"] for c in R], [r1[c]["Bf"] for c in R],
                                       [r1[c]["Cf"] for c in R], [r1[c]["dtv"] for c in R],
                                       [r1[c]["cumA"] for c in R], I["L0_attn_D"]),
                              core_ids=R).results
    r3 = run_bass_kernel_spmd(_get("l3", build_mgate),
                              host_mgate([r2[c]["y_pm"] for c in R], [r1[c]["zs"] for c in R],
                                         u_flat, I["L0_attn_out_proj_w"], I["L0_attn_norm_w"]),
                              core_ids=R).results
    r4a = run_bass_kernel_spmd(_get("l4a", build_moe_route),
                               host_moe_route([r3[c]["x1_fm"] for c in R], I["L0_ffn_w_gate"],
                                              I["L0_ffn_w_shared_up"], I["L0_ffn_w_shared_down"]),
                               core_ids=R).results
    r4b = run_bass_kernel_spmd(_get("l4b", build_moe_expert),
                               host_moe_expert([r4a[c]["xnt"] for c in R],
                                               [r4a[c]["tbl"] for c in R],
                                               [r4a[c]["cnt"] for c in R],
                                               I["L0_ffn_w_up"], I["L0_ffn_w_down"]),
                               core_ids=R).results
    r4c = run_bass_kernel_spmd(_get("l4c", build_moe_combine),
                               host_moe_combine([r4b[c]["Yout"] for c in R],
                                                [r4a[c]["slot"] for c in R],
                                                [r4a[c]["wv"] for c in R],
                                                [r4a[c]["sht"] for c in R],
                                                [np.asarray(r3[c]["x1_fm"]).T for c in R]),
                               core_ids=R).results
    r5 = run_bass_kernel_spmd(_get("l5", build_qkv),
                              host_qkv([r4c[c]["x2t"] for c in R], I["L1_attn_wq"],
                                       I["L1_attn_wk"], I["L1_attn_wv"]),
                              core_ids=R).results
    r6 = run_bass_kernel_spmd(_get("l6", build_attn),
                              host_attn([r5[c]["qh"] for c in R], [r5[c]["kh"] for c in R],
                                        [r5[c]["vv"] for c in R]),
                              core_ids=R).results
    r7 = run_bass_kernel_spmd(_get("l7", build_attnout),
                              host_attnout([r6[c]["yat"] for c in R], I["L1_attn_wo"],
                                           [np.asarray(r4c[c]["x2t"]).T for c in R]),
                              core_ids=R).results
    r8 = run_bass_kernel_spmd(_get("l8", build_ffn),
                              host_ffn([r7[c]["x3"] for c in R], I["L1_ffn_gate_w"],
                                       I["L1_ffn_up_w"], I["L1_ffn_down_w"],
                                       I["L1_ffn_gate_s"], I["L1_ffn_up_s"],
                                       I["L1_ffn_down_s"]),
                              core_ids=R).results
    out = np.concatenate([np.asarray(r8[c]["x4"], np.float32).T for c in R], 0)
    return out.reshape(B, T, D)



def _emit_f32_transpose(P, src_ap, dst_ap, idn_ap, tag):
    nc = P.nc
    E = P.engines()
    block = P.block
    ve, pe, sq = E.ve, E.pe, E.sq
    R, C = src_ap.shape
    with contextlib.ExitStack() as ctx:
        sb = ctx.enter_context(nc.sbuf_tensor(P.nm(tag + "_sb"), [128, 2 * C], F32))
        st = ctx.enter_context(nc.sbuf_tensor(P.nm(tag + "_st"), [128, 2 * R], F32))
        si = ctx.enter_context(nc.sbuf_tensor(P.nm(tag + "_si"), [128, 128], F32))
        pt = ctx.enter_context(nc.psum_tensor(P.nm(tag + "_pt"), [128, 1024], F32))
        lds = [P.sem("tldA0"), P.sem("tldA1")]
        sts = [P.sem("tstA0"), P.sem("tstA1")]
        ldi = P.sem("tldI")
        K = {}

        @block.sync
        def _(s):
            sq.start(si[:, :], idn_ap, sem=ldi)
        ldi_n = ldi.n

        for rt in range(R // 128):
            pp = rt % 2

            @block.sync
            def _(s, rt=rt, pp=pp):
                if rt >= 2:
                    sq.wait(pe, K[("tr", rt - 2, C // 128 - 1)])
                sq.start(sb[:, pp * C:(pp + 1) * C], src_ap[rt * 128:(rt + 1) * 128, :],
                         sem=lds[pp])
            K[("ld", rt)] = lds[pp].n

            for ct in range(C // 128):
                @block.tensor
                def _(t, rt=rt, ct=ct, pp=pp):
                    if ct == 0:
                        pe.wait(ldi, ldi_n)
                        pe.wait(lds[pp], K[("ld", rt)])
                    prev = K.get(("ev", rt, ct - 1) if ct else (rt - 1, "evl"))
                    if ct == 0:
                        prev = K.get((rt - 1, "evl"))
                    else:
                        prev = K.get(("ev", rt, ct - 1))
                    if ct >= 2:
                        pe.wait(ve, K[("ev", rt, ct - 2)])
                    elif rt >= 1 and ct < 2:
                        pe.wait(ve, K[("ev", rt - 1, C // 128 - 2 + ct)])
                    i = pe.transpose(pt[:, 0:128] if ct % 2 == 0 else pt[:, 512:640],
                                     sb[:, pp * C + ct * 128:pp * C + (ct + 1) * 128],
                                     si[:, :])
                    K[("tr", rt, ct)] = pe.done(i)

                @block.vector
                def _(v, rt=rt, ct=ct, pp=pp):
                    ve.wait(pe, K[("tr", rt, ct)])
                    if ct == 0 and rt >= 2:
                        ve.wait(sts[pp], K[("st", rt - 2)])
                    ve.tensor_copy(st[:, pp * R + ct * 128:pp * R + (ct + 1) * 128],
                                   pt[:, 0:128] if ct % 2 == 0 else pt[:, 512:640])
                K[("ev", rt, ct)] = ve.n
            K[(rt, "evl")] = ve.n

            @block.sync
            def _(s, rt=rt, pp=pp):
                sq.wait(ve, K[(rt, "evl")])
                sq.start(dst_ap[:, rt * 128:(rt + 1) * 128].rearrange("(ct p) q -> p ct q", p=128),
                         st[:, pp * R:(pp + 1) * R].rearrange("p (ct q) -> p ct q", q=128),
                         sem=sts[pp])
            K[("st", rt)] = sts[pp].n

        @block.sync
        def _(s):
            for sem in (sts[0], sts[1], lds[0], lds[1], ldi):
                s.wait_ge(sem.h, sem.n)


def phase_moe_adapt(P):
    nc = P.nc
    xnt = P.dio("xnt", [TC, D], BF16, "int")
    tbl = P.dio("tbl", [EMOE * ROWS_E, 1], F32, "int")
    ag_cnt = P.dio("ag_cnt", [NCORES, EMOE], F32, "int")
    sel64 = P.dio("sel64", [64, 8], F32, "in")
    idnf = P.dio("idnf", [128, 128], F32, "in")
    ag_xnt = P.dio("ag_xnt", [NTOK, D], BF16, "int")
    ag_tbl = P.dio("ag_tbl", [NCORES, EMOE * ROWS_E], F32, "int")
    cntT = P.dio("cntT", [8, 8], F32, "int")
    E = P.engines()
    block = P.block
    ve, pe, gq, sq = E.ve, E.pe, E.gq, E.sq
    with contextlib.ExitStack() as ctx:
        sag = ctx.enter_context(nc.sbuf_tensor(P.nm("sag"), [8, 64], F32))
        sagT = ctx.enter_context(nc.sbuf_tensor(P.nm("sagT"), [64, 8], F32))
        ssel = ctx.enter_context(nc.sbuf_tensor(P.nm("ssel"), [64, 8], F32))
        scT = ctx.enter_context(nc.sbuf_tensor(P.nm("scT"), [8, 8], F32))
        sidn = ctx.enter_context(nc.sbuf_tensor(P.nm("sidn8"), [128, 128], F32))
        pt = ctx.enter_context(nc.psum_tensor(P.nm("ptA"), [128, 512], F32))
        agc = P.sem("agc")
        lds = P.sem("ldsA")
        sts = P.sem("stsA")

        @block.gpsimd
        def _(g):
            gq.collective("AllGather", ALU.bypass, [list(range(NCORES))],
                          [xnt.opt()], [ag_xnt.opt()], sem=agc)
            gq.collective("AllGather", ALU.bypass, [list(range(NCORES))],
                          [tbl.rearrange("r o -> o r").opt()], [ag_tbl.opt()], sem=agc)
        agc_n = agc.n

        @block.sync
        def _(s):
            sq.start(sag[:, :], ag_cnt, sem=lds)
            sq.start(ssel[:, :], sel64, sem=lds)
            sq.start(sidn[:, :], idnf, sem=lds)
        lds_n = lds.n

        @block.tensor
        def _(t):
            pe.wait(lds, lds_n)
            i = pe.transpose(pt[0:64, 0:8], sag[:, :], sidn[0:8, 0:8])
            pe.done(i)
        pe_t = pe.n

        @block.vector
        def _(v):
            ve.wait(pe, pe_t)
            ve.tensor_copy(sagT[:, :], pt[0:64, 0:8])
        ve_t = ve.n

        @block.tensor
        def _(t):
            pe.wait(ve, ve_t)
            i = pe.matmul(pt[0:8, 128:136], ssel[:, :], sagT[:, :], start=True, stop=True)
            pe.done(i)
        pe_m = pe.n

        @block.vector
        def _(v):
            ve.wait(pe, pe_m)
            ve.tensor_copy(scT[:, :], pt[0:8, 128:136])
        ve_m = ve.n

        @block.sync
        def _(s):
            sq.wait(ve, ve_m)
            sq.start(cntT, scT[:, :], sem=sts)
            s.wait_ge(sts.h, sts.n)
            s.wait_ge(agc.h, agc_n)


def phase_attn_adapt(P):
    nc = P.nc
    qh = P.dio("qh", [D, TC], BF16, "int")
    kh = P.dio("kh", [512, TC], BF16, "int")
    vv = P.dio("vv", [512, TC], BF16, "int")
    qgi = P.dio("qgi", [128, 16], I32, "in")
    kgi = P.dio("kgi", [128, 16], I32, "in")
    vgi = P.dio("vgi", [128, 16], I32, "in")
    ident_bf = P.dio("ident_bf", [128, 128], BF16, "in")
    ag_q = P.dio("ag_q", [NCORES * D, TC], BF16, "int")
    ag_k = P.dio("ag_k", [NCORES * 512, TC], BF16, "int")
    ag_v = P.dio("ag_v", [NCORES * 512, TC], BF16, "int")
    qh6 = P.dio("qh6", [512, T], BF16, "int")
    kh6 = P.dio("kh6", [512, T], BF16, "int")
    vp6 = P.dio("vp6", [4 * T, 128], BF16, "int")
    E = P.engines()
    block = P.block
    ve, pe, gq, sq = E.ve, E.pe, E.gq, E.sq
    with contextlib.ExitStack() as ctx:
        sqi = ctx.enter_context(nc.sbuf_tensor(P.nm("sqi"), [128, 16], I32))
        ski = ctx.enter_context(nc.sbuf_tensor(P.nm("ski"), [128, 16], I32))
        svi = ctx.enter_context(nc.sbuf_tensor(P.nm("svi"), [128, 16], I32))
        sga = ctx.enter_context(nc.sbuf_tensor(P.nm("sgaA"), [128, 2 * TC], BF16))
        svt = ctx.enter_context(nc.sbuf_tensor(P.nm("svtA"), [128, 2 * TC], BF16))
        sidb = ctx.enter_context(nc.sbuf_tensor(P.nm("sidbA"), [128, 128], BF16))
        ptb = ctx.enter_context(nc.psum_tensor(P.nm("ptbA"), [128, 2048], BF16))
        agc = P.sem("agc")
        lds = P.sem("ldsB")
        gts = [P.sem("gtsB0"), P.sem("gtsB1")]
        sts = [P.sem("stsB0"), P.sem("stsB1")]
        K = {}

        @block.gpsimd
        def _(g):
            gq.collective("AllGather", ALU.bypass, [list(range(NCORES))],
                          [qh.opt()], [ag_q.opt()], sem=agc)
            gq.collective("AllGather", ALU.bypass, [list(range(NCORES))],
                          [kh.opt()], [ag_k.opt()], sem=agc)
            gq.collective("AllGather", ALU.bypass, [list(range(NCORES))],
                          [vv.opt()], [ag_v.opt()], sem=agc)
        agc_n = agc.n

        @block.sync
        def _(s):
            sq.start(sqi[:, :], qgi, sem=lds)
            sq.start(ski[:, :], kgi, sem=lds)
            sq.start(svi[:, :], vgi, sem=lds)
            sq.start(sidb[:, :], ident_bf, sem=lds)
        lds_n = lds.n

        gi = 0
        for name, idxb, agsrc, dst in (("q", 0, 0, 0), ("k", 1, 1, 1)):
            srcap = [ag_q, ag_k][idxb]
            dstap = [qh6, kh6][idxb]
            ib = [sqi, ski][idxb]
            for j in range(16):
                par = gi % 2

                @block.gpsimd
                def _(g, gi=gi, j=j, par=par, srcap=srcap, ib=ib):
                    if gi == 0:
                        g.wait_ge(agc.h, agc_n)
                        g.wait_ge(lds.h, lds_n)
                    if gi >= 2:
                        g.wait_ge(sts[par].h, sts[par].base + 16 * (gi // 2))
                    gq.indirect(out=sga[:, par * TC:(par + 1) * TC], out_offset=None,
                                in_=srcap,
                                in_offset=IndirectOffsetOnAxis(ib[:, j:j + 1], axis=0),
                                sem=gts[par])
                K[("g", gi)] = gts[par].n

                @block.sync
                def _(s, gi=gi, j=j, par=par, dstap=dstap):
                    s.wait_ge(gts[par].h, K[("g", gi)])
                    un, rq = j // 4, j % 4
                    sq.start(dstap[un * 128:(un + 1) * 128, rq * TC:(rq + 1) * TC],
                             sga[:, par * TC:(par + 1) * TC], sem=sts[par])
                gi += 1

        for j in range(16):
            par = j % 2

            @block.gpsimd
            def _(g, j=j, par=par):
                if j == 0:
                    g.wait_ge(agc.h, agc_n)
                    g.wait_ge(lds.h, lds_n)
                g.wait_ge(sts[par].h, sts[par].base + 16 * (16 + j // 2))
                gq.indirect(out=sga[:, par * TC:(par + 1) * TC], out_offset=None,
                            in_=ag_v,
                            in_offset=IndirectOffsetOnAxis(svi[:, j:j + 1], axis=0),
                            sem=gts[par])
            K[("vg", j)] = gts[par].n

            for st_ in range(16):
                @block.tensor
                def _(t, j=j, par=par, st_=st_):
                    if j == 0 and st_ == 0:
                        pe.wait(lds, lds_n)
                    if st_ == 0:
                        pe.wait(gts[par], K[("vg", j)])
                    prev = K.get(("vev", j, st_ - 2) if st_ >= 2 else
                                 ("vev", j - 1, 14 + st_))
                    if prev:
                        pe.wait(ve, prev)
                    i = pe.transpose(ptb[:, (st_ % 2) * 1024:(st_ % 2) * 1024 + 128],
                                     sga[:, par * TC + st_ * 128:par * TC + (st_ + 1) * 128],
                                     sidb[:, :])
                    K[("vtr", j, st_)] = pe.done(i)

                @block.vector
                def _(v, j=j, par=par, st_=st_):
                    ve.wait(pe, K[("vtr", j, st_)])
                    if st_ == 0 and j >= 2:
                        ve.wait(sts[par], K[("vst", j - 2)])
                    ve.tensor_copy(svt[:, par * TC + st_ * 128:par * TC + (st_ + 1) * 128],
                                   ptb[:, (st_ % 2) * 1024:(st_ % 2) * 1024 + 128])
                K[("vev", j, st_)] = ve.n

            @block.sync
            def _(s, j=j, par=par):
                sq.wait(ve, K[("vev", j, 15)])
                un, rq = j // 4, j % 4
                sq.start(vp6[un * T + rq * TC:un * T + (rq + 1) * TC, :]
                         .rearrange("(st p) d -> p st d", p=128),
                         svt[:, par * TC:(par + 1) * TC].rearrange("p (st d) -> p st d", d=128),
                         sem=sts[par])
            K[("vst", j)] = sts[par].n

        @block.sync
        def _(s):
            for sem in (sts[0], sts[1]):
                s.wait_ge(sem.h, sem.n)


def phase_y_adapt(P):
    nc = P.nc
    yat = P.dio("yat", [512, T], BF16, "int")
    ygi = P.dio("ygi", [128, 16], I32, "in")
    ag_y = P.dio("ag_y", [NCORES * 512, T], BF16, "int")
    ya = P.dio("ya", [D, TC], BF16, "int")
    E = P.engines()
    block = P.block
    ve, gq, sq = E.ve, E.gq, E.sq
    with contextlib.ExitStack() as ctx:
        syi = ctx.enter_context(nc.sbuf_tensor(P.nm("syi"), [128, 16], I32))
        sga = ctx.enter_context(nc.sbuf_tensor(P.nm("sgaY"), [128, 2 * TC], BF16))
        agc = P.sem("agc")
        lds = P.sem("ldsC")
        gts = [P.sem("gtsC0"), P.sem("gtsC1")]
        sts = [P.sem("stsC0"), P.sem("stsC1")]
        K = {}

        @block.gpsimd
        def _(g):
            gq.collective("AllGather", ALU.bypass, [list(range(NCORES))],
                          [yat.opt()], [ag_y.opt()], sem=agc)
        agc_n = agc.n

        @block.sync
        def _(s):
            sq.start(syi[:, :], ygi, sem=lds)
        lds_n = lds.n

        agyv = ag_y.rearrange("r (s t) -> (r s) t", s=4)
        for j in range(16):
            par = j % 2

            @block.gpsimd
            def _(g, j=j, par=par):
                if j == 0:
                    g.wait_ge(agc.h, agc_n)
                    g.wait_ge(lds.h, lds_n)
                if j >= 2:
                    g.wait_ge(sts[par].h, sts[par].base + 16 * (j // 2))
                gq.indirect(out=sga[:, par * TC:(par + 1) * TC], out_offset=None,
                            in_=agyv,
                            in_offset=IndirectOffsetOnAxis(syi[:, j:j + 1], axis=0),
                            sem=gts[par])
            K[("g", j)] = gts[par].n

            @block.sync
            def _(s, j=j, par=par):
                s.wait_ge(gts[par].h, K[("g", j)])
                sq.start(ya[j * 128:(j + 1) * 128, :], sga[:, par * TC:(par + 1) * TC],
                         sem=sts[par])


def phase_yout_ag(P):
    nc = P.nc
    yout = P.dio("Yout", [8 * ROWS_E, D], BF16, "int")
    ag_Y = P.dio("ag_Y", [EMOE * ROWS_E, D], BF16, "int")
    E = P.engines()
    block = P.block
    gq = E.gq
    agc = P.sem("agc")

    @block.gpsimd
    def _(g):
        gq.collective("AllGather", ALU.bypass, [list(range(NCORES))],
                      [yout.opt()], [ag_Y.opt()], sem=agc)
    n = agc.n

    @block.gpsimd
    def _(g):
        g.wait_ge(agc.h, n)



def build_fused(upto=99):
    nc = bass.Bass("TRN2", target_bir_lowering=False)
    TE = TC + 3
    with contextlib.ExitStack() as octx:
        P = Phase(nc, octx, "")

        def mi(name, shape, dt):
            return nc.dram_tensor(name, list(shape), dt, kind="ExternalInput").ap()

        def ms(name, shape, dt, kind="Internal"):
            return nc.dram_tensor(name, list(shape), dt, kind=kind).ap()

        xt = mi("xt", [D, TE], F32)
        wT = mi("wT", [D, D_IN_PROJ], F32)
        cw = mi("cw", [128, 136], F32)
        cb = mi("cb", [128, 34], F32)
        dtb = mi("dtb", [64, 1], F32)
        alog = mi("alog", [64, 1], F32)
        ones_bf = mi("ones_bf", [128, 128], BF16)
        idnf = mi("idnf", [128, 128], F32)
        drow = mi("drow", [1, 64], F32)
        trilm = mi("trilm", [128, 128], F32)
        onecol = mi("onecol", [1, 128], F32)
        bmask = mi("bmask", [8, 8], F32)
        pmask = mi("pmask", [1, 8], F32)
        woT = mi("woT", [D_INNER, D], F32)
        normw = mi("normw", [128, 32], F32)
        wg = mi("wg", [D, EMOE], F32)
        wsu = mi("wsu", [D, 1024], F32)
        wsd = mi("wsd", [128, 64 * 128], F32)
        iot = mi("iot", [128, EMOE], F32)
        trils = mi("trils", [128, 128], F32)
        onek1 = mi("onek1", [1, 128], F32)
        onecl = mi("onecl", [128, 1], F32)
        rankm = mi("rankm", [8, 1], F32)
        tokbase = mi("tokbase", [128, 1], F32)
        ident_bf = mi("ident_bf", [128, 128], BF16)
        sel64 = mi("sel64", [64, 8], F32)
        wup = mi("wup", [8 * D, 512], F32)
        wdn = mi("wdn", [8 * 256, D], F32)
        islot = mi("islot", [8, ROWS_E], F32)
        idn = mi("idn", [128, 128], F32)
        wq = mi("wq", [D, D], F32)
        wk = mi("wk", [D, 512], F32)
        wv_w = mi("wv", [D, 512], F32)
        cosr = mi("cosr", [128, TC], F32)
        sinr = mi("sinr", [128, TC], F32)
        dmask = mi("dmask", [128, 4 * 512], F32)
        qgi = mi("qgi", [128, 16], I32)
        kgi = mi("kgi", [128, 16], I32)
        vgi = mi("vgi", [128, 16], I32)
        ygi = mi("ygi", [128, 16], I32)
        wo = mi("wo", [D, D], F32)
        gwT = mi("gwT", [D, INTER], F32)
        uwT = mi("uwT", [D, INTER], F32)
        dwT = mi("dwT", [INTER, D], F32)
        gs = mi("gs", [128, NIT], F32)
        us = mi("us", [128, NIT], F32)
        ds = mi("ds", [128, 16], F32)

        zs = ms("zs_s", [D_INNER, TC], BF16)
        xc = ms("xc_s", [D_INNER, TC], BF16)
        Bf = ms("Bf_s", [D_STATE, TC], BF16)
        Cf = ms("Cf_s", [D_STATE, TC], BF16)
        dtv = ms("dtv_s", [64, TC], F32)
        cumA = ms("cumA_s", [64, TC], F32)
        xnj = ms("xn_j", [D, TE], BF16)
        y_pm = ms("ypm_s", [TC, D_INNER], BF16)
        x1 = ms("x1_s", [D, TC], F32)
        xnfj = ms("xnf_j", [D, TC], BF16)
        xnt = ms("xnt_s", [TC, D], BF16)
        sht = ms("sht_s", [TC, D], BF16)
        slot = ms("slot_s", [TC, TOPK], F32)
        wv_s = ms("wv_s", [TC, TOPK], F32)
        cntj = ms("cnt_j", [1, EMOE], F32)
        tbl = ms("tbl_s", [EMOE * ROWS_E, 1], F32)
        ag_cnt = ms("agcnt_s", [NCORES, EMOE], F32)
        ag_xnt = ms("agxnt_s", [NTOK, D], BF16)
        ag_tbl = ms("agtbl_s", [NCORES, EMOE * ROWS_E], F32)
        cntT = ms("cntT_s", [8, 8], F32)
        Yout = ms("Yout_s", [8 * ROWS_E, D], BF16)
        ag_Y = ms("agY_s", [EMOE * ROWS_E, D], BF16)
        x1t = ms("x1t_s", [TC, D], F32)
        x2t = ms("x2t_s", [TC, D], F32)
        x2f = ms("x2f_s", [D, TC], F32)
        qh = ms("qh_s", [D, TC], BF16)
        kh = ms("kh_s", [512, TC], BF16)
        vv = ms("vv_s", [512, TC], BF16)
        xn2j = ms("xn2_j", [D, TC], BF16)
        ag_q = ms("agq_s", [NCORES * D, TC], BF16)
        ag_k = ms("agk_s", [NCORES * 512, TC], BF16)
        ag_v = ms("agv_s", [NCORES * 512, TC], BF16)
        qh6 = ms("qh6_s", [512, T], BF16)
        kh6 = ms("kh6_s", [512, T], BF16)
        vp6 = ms("vp6_s", [4 * T, 128], BF16)
        yat = ms("yat_s", [512, T], BF16)
        ag_y = ms("agy_s", [NCORES * 512, T], BF16)
        ya = ms("ya_s", [D, TC], BF16)
        x3 = ms("x3_s", [D, TC], F32)
        x4 = ms("x4", [D, TC], F32, kind="ExternalOutput")

        stage = [0]

        def run(prefix, fn, mapping):
            stage[0] += 1
            if stage[0] > upto:
                return
            P.prefix = prefix
            P.dram = dict(mapping)
            fn(P)
            P.barrier()

        run("m1_", phase_mamba_front,
            dict(xt=xt, wT=wT, cw=cw, cb=cb, dtb=dtb, alog=alog, ones_bf=ones_bf,
                 zs=zs, xc=xc, Bf=Bf, Cf=Cf, dtv=dtv, cumA=cumA, xn=xnj))
        run("m2_", phase_ssd,
            dict(xc=xc, Bf=Bf, Cf=Cf, dtv=dtv, cumA=cumA, idnf=idnf, drow=drow,
                 trilm=trilm, onecol=onecol, bmask=bmask, pmask=pmask, y_pm=y_pm))
        run("m3_", phase_mgate,
            dict(y_pm=y_pm, zs=zs, u_fm=xt[:, 3:3 + TC], woT=woT, normw=normw,
                 ones_bf=ones_bf, x1_fm=x1))
        run("r4_", phase_moe_route,
            dict(x1=x1, wg=wg, wsu=wsu, wsd=wsd, iot=iot, trils=trils, onek1=onek1,
                 onecl=onecl, rankm=rankm, tokbase=tokbase, ident_bf=ident_bf,
                 ones_bf=ones_bf, xnf=xnfj, xnt=xnt, sht=sht, slot=slot, wv=wv_s,
                 cnt=cntj, tbl=tbl, ag_cnt=ag_cnt))
        run("a4_", phase_moe_adapt,
            dict(xnt=xnt, tbl=tbl, ag_cnt=ag_cnt, sel64=sel64, idnf=idnf,
                 ag_xnt=ag_xnt, ag_tbl=ag_tbl, cntT=cntT))
        run("r5_", phase_moe_expert,
            dict(xg=ag_xnt, tbls=ag_tbl, cntT=cntT, wup=wup, wdn=wdn, islot=islot,
                 idn=idn, sel64=sel64, Yout=Yout))
        run("a5_", phase_yout_ag, dict(Yout=Yout, ag_Y=ag_Y))
        stage[0] += 1
        if stage[0] <= upto:
            P.prefix = "a5b_"
            P.dram = {}
            _emit_f32_transpose(P, x1, x1t, idnf, "x1t")
            P.barrier()
        run("r6_", phase_moe_combine,
            dict(yall=ag_Y, slot=slot, wv=wv_s, sht=sht, x1t=x1t, x2t=x2t))
        stage[0] += 1
        if stage[0] <= upto:
            P.prefix = "a6_"
            P.dram = {}
            _emit_f32_transpose(P, x2t, x2f, idnf, "x2f")
            P.barrier()
        run("q7_", phase_qkv,
            dict(x2=x2f, wq=wq, wk=wk, wv=wv_w, cosr=cosr, sinr=sinr, ones_bf=ones_bf,
                 qh=qh, kh=kh, vv=vv, xn2=xn2j))
        run("a7_", phase_attn_adapt,
            dict(qh=qh, kh=kh, vv=vv, qgi=qgi, kgi=kgi, vgi=vgi, ident_bf=ident_bf,
                 ag_q=ag_q, ag_k=ag_k, ag_v=ag_v, qh6=qh6, kh6=kh6, vp6=vp6))
        run("a8_", phase_attn,
            dict(qh=qh6, kh=kh6, vp=vp6, dmask=dmask, ones_bf=ones_bf, yat=yat))
        run("a9_", phase_y_adapt, dict(yat=yat, ygi=ygi, ag_y=ag_y, ya=ya))
        run("o9_", phase_attnout, dict(ya=ya, wo=wo, x2=x2f, x3=x3))
        run("f10_", phase_ffn,
            dict(x3=x3, gwT=gwT, uwT=uwT, dwT=dwT, gs=gs, us=us, ds=ds,
                 ones_bf=ones_bf, x4=x4))
    return nc


def host_fused(I):
    u_flat = _f32(np.asarray(I["L0_attn_u"]).reshape(NTOK, D))
    m1 = host_mamba_front(u_flat, I["L0_attn_in_proj_w"], I["L0_attn_conv_w"],
                          I["L0_attn_conv_b"], I["L0_attn_dt_bias"], I["L0_attn_A_log"])
    tril_ge = (np.arange(128)[None, :] >= np.arange(128)[:, None]).astype(np.float32)
    tril_lt = (np.arange(128)[None, :] < np.arange(128)[:, None]).astype(np.float32)
    onec = np.ones((1, 128), np.float32)
    onecl_ = np.ones((128, 1), np.float32)
    idnf = np.eye(128, dtype=np.float32)
    ident_b = idnf.astype(BF)
    woT = _f32(np.asarray(I["L0_attn_out_proj_w"]).T)
    normw = _f32(np.asarray(I["L0_attn_norm_w"]).reshape(32, 128).T)
    wsd_t = np.zeros((128, 64 * 128), np.float32)
    wsdm = np.asarray(I["L0_ffn_w_shared_down"], np.float32)
    for k in range(4):
        for mt in range(16):
            wsd_t[:, (k * 16 + mt) * 128:(k * 16 + mt + 1) * 128] = \
                wsdm[k * 128:(k + 1) * 128, mt * 128:(mt + 1) * 128]
    iotm = np.broadcast_to(np.arange(EMOE, dtype=np.float32), (128, EMOE)).copy()
    islot_ = np.broadcast_to(np.arange(ROWS_E, dtype=np.float32), (8, ROWS_E)).copy()
    half = HD // 2
    inv_freq = 1.0 / (100000.0 ** (np.arange(half, dtype=np.float32) / half))
    MASKV = -10000.0 * np.sqrt(128.0)
    dmask = np.zeros((128, 4 * 512), np.float32)
    for mi_ in range(4):
        q_idx = np.arange(512)[None, :]
        k_idx = (np.arange(128) + mi_ * 128)[:, None]
        dmask[:, mi_ * 512:(mi_ + 1) * 512] = np.where(q_idx >= k_idx, 3e38, MASKV)
    gs_t = _f32(np.asarray(I["L1_ffn_gate_s"]).reshape(NIT, 128).T)
    us_t = _f32(np.asarray(I["L1_ffn_up_s"]).reshape(NIT, 128).T)
    ds_t = _f32(np.asarray(I["L1_ffn_down_s"]).reshape(16, 128).T)
    gwT = _f32(np.asarray(I["L1_ffn_gate_w"]).T)
    uwT = _f32(np.asarray(I["L1_ffn_up_w"]).T)
    dwT = _f32(np.asarray(I["L1_ffn_down_w"]).T)
    w_up = np.asarray(I["L0_ffn_w_up"], np.float32)
    w_dn = np.asarray(I["L0_ffn_w_down"], np.float32)
    pvec = np.arange(128, dtype=np.int64)

    in_maps = []
    for c in range(NCORES):
        p = c % 4
        b0c = (c // 4) * 4
        bm = np.zeros((8, 8), np.float32)
        pm = np.zeros((1, 8), np.float32)
        for s in range(8):
            if b0c <= s < b0c + p:
                pm[0, s] = 1.0
                for k in range(s + 1, b0c + p):
                    bm[k, s] = 1.0
        rk = np.zeros((8, 1), np.float32)
        rk[:c] = 1.0
        tb = (np.arange(128, dtype=np.float32) + c * TC).reshape(128, 1)
        sel = np.zeros((64, 8), np.float32)
        for j in range(8):
            sel[8 * c + j, j] = 1.0
        s0 = c * TC
        t0 = s0 % T
        pos = np.arange(t0, t0 + TC, dtype=np.float32)
        fr = pos[None, :] * inv_freq[:, None]
        qgi = np.zeros((128, 16), np.int64)
        kgi = np.zeros((128, 16), np.int64)
        vgi = np.zeros((128, 16), np.int64)
        ygi = np.zeros((128, 16), np.int64)
        for j in range(16):
            un, rq = j // 4, j % 4
            uid = 4 * c + un
            b, h = uid // 16, uid % 16
            kvh = h // 4
            qgi[:, j] = (4 * b + rq) * D + h * 128 + pvec
            kgi[:, j] = (4 * b + rq) * 512 + kvh * 128 + pvec
            vgi[:, j] = (4 * b + rq) * 512 + kvh * 128 + pvec
        b0 = s0 // T
        for hj in range(16):
            uid = b0 * 16 + hj
            ygi[:, hj] = (uid * 128 + pvec) * 4 + (c % 4)
        in_maps.append({
            **m1[c],
            "idnf": idnf, "drow": _f32(np.asarray(I["L0_attn_D"]).reshape(1, 64)),
            "trilm": tril_ge, "onecol": onec, "bmask": bm, "pmask": pm,
            "woT": woT, "normw": normw,
            "wg": _f32(I["L0_ffn_w_gate"]), "wsu": _f32(I["L0_ffn_w_shared_up"]),
            "wsd": wsd_t, "iot": iotm, "trils": tril_lt, "onek1": onec,
            "onecl": onecl_, "rankm": rk, "tokbase": tb, "ident_bf": ident_b,
            "sel64": sel,
            "wup": _f32(w_up[8 * c:8 * c + 8].reshape(8 * D, 512)),
            "wdn": _f32(w_dn[8 * c:8 * c + 8].reshape(8 * 256, D)),
            "islot": islot_, "idn": idnf,
            "wq": _f32(I["L1_attn_wq"]), "wk": _f32(I["L1_attn_wk"]),
            "wv": _f32(I["L1_attn_wv"]),
            "cosr": _f32(np.concatenate([np.cos(fr), np.cos(fr)], 0)),
            "sinr": _f32(np.concatenate([np.sin(fr), np.sin(fr)], 0)),
            "dmask": dmask,
            "qgi": qgi.astype(np.int32), "kgi": kgi.astype(np.int32),
            "vgi": vgi.astype(np.int32), "ygi": ygi.astype(np.int32),
            "wo": _f32(I["L1_attn_wo"]),
            "gwT": gwT, "uwT": uwT, "dwT": dwT, "gs": gs_t, "us": us_t, "ds": ds_t,
        })
    return in_maps


def kernel_fused(**inputs):
    I = {k: np.asarray(v) for k, v in inputs.items()}
    nc = _get("fused", build_fused)
    res = run_bass_kernel_spmd(nc, host_fused(I), core_ids=list(range(NCORES)))
    out = np.concatenate([np.asarray(res.results[c]["x4"], np.float32).T
                          for c in range(NCORES)], 0)
    return out.reshape(B, T, D)


# version 03 (final)
# speedup vs baseline: 1.2152x; 1.2152x over previous
import contextlib
import os
os.environ.setdefault("NEURON_SCRATCHPAD_PAGE_SIZE", "640")
import numpy as np
import ml_dtypes
import concourse.bass as bass
import concourse.mybir as mybir
from concourse.bass_utils import run_bass_kernel_spmd
from concourse.bass import IndirectOffsetOnAxis

F32 = mybir.dt.float32
BF16 = mybir.dt.bfloat16
I32 = mybir.dt.int32
AF = mybir.ActivationFunctionType
ALU = mybir.AluOpType
BF = ml_dtypes.bfloat16

NCORES = 8
B = 2
T = 8192
D = 2048
NTOK = B * T
TC = NTOK // NCORES
D_INNER = 4096
D_STATE = 128
NHEADS = 64
HEADDIM = 64
CONV_DIM = D_INNER + 2 * D_STATE
D_IN_PROJ = 2 * D_INNER + 2 * D_STATE + NHEADS
CHUNK = 128
LN_EPS = 1e-5

_PASSTHRU = {"dma_start", "dma_start_transpose", "indirect_dma_start",
             "collective_compute", "wait_ge", "nop", "engine_nop"}


class Eng:

    def __init__(self, ctx, nc, eng, name):
        self._e = eng
        self.h = ctx.enter_context(nc.semaphore(name))
        self.n = 0

    def wait(self, other, n=None):
        self._e.wait_ge(other.h, other.n if n is None else n)

    def raw(self):
        return self._e

    def __getattr__(self, name):
        f = getattr(self._e, name)
        if name in _PASSTHRU or name.startswith("_"):
            return f

        def wrap(*a, **k):
            if self.n:
                self._e.wait_ge(self.h, self.n)
            inst = f(*a, **k)
            self.n += 1
            inst.then_inc(self.h, 1)
            return inst
        return wrap


class Sem:

    def __init__(self, ctx, nc, name):
        self.h = ctx.enter_context(nc.semaphore(name))
        self.n = 0
        self.base = 0


class SemView:

    def __init__(self, core):
        self._c = core
        self.base = core.n

    @property
    def h(self):
        return self._c.h

    @property
    def n(self):
        return self._c.n

    @n.setter
    def n(self, v):
        self._c.n = v


class Dma:

    def __init__(self, ctx, nc, eng, name):
        self._e = eng
        self._ctx = ctx
        self._nc = nc
        self.q = Sem(ctx, nc, name)

    def wait(self, other, n=None):
        self._e.wait_ge(other.h, other.n if n is None else n)

    def start(self, out, in_, sem=None, **kw):
        sem = sem or self.q
        inst = self._e.dma_start(out=out, in_=in_, **kw)
        sem.n += 16
        inst.then_inc(sem.h, 16)
        return inst

    def indirect(self, out, out_offset, in_, in_offset, sem=None, **kw):
        sem = sem or self.q
        inst = self._e.indirect_dma_start(out=out, out_offset=out_offset,
                                          in_=in_, in_offset=in_offset, **kw)
        sem.n += 16
        inst.then_inc(sem.h, 16)
        return inst

    def transpose(self, out, in_, sem=None):
        sem = sem or self.q
        inst = self._e.dma_start(out=out, in_=in_, transpose=True)
        sem.n += 16
        inst.then_inc(sem.h, 16)
        return inst

    def collective(self, kind, op, replica_groups, ins, outs, sem=None):
        sem = sem or self.q
        inst = self._e.collective_compute(kind, op, replica_groups=replica_groups,
                                          ins=ins, outs=outs)
        sem.n += 1
        inst.then_inc(sem.h, 1)
        return inst


class Pe:

    def __init__(self, ctx, nc, name="pe"):
        self._e = nc.tensor
        self.h = ctx.enter_context(nc.semaphore(name))
        self.n = 0

    def wait(self, other, n=None):
        self._e.wait_ge(other.h, other.n if n is None else n)

    def matmul(self, *a, **k):
        return self._e.matmul(*a, **k)

    def transpose(self, *a, **k):
        return self._e.transpose(*a, **k)

    def done(self, inst):
        self.n += 1
        inst.then_inc(self.h, 1)
        return self.n


def _bf(x):
    return np.ascontiguousarray(np.asarray(x).astype(BF))


def _f32(x):
    return np.ascontiguousarray(np.asarray(x).astype(np.float32))





class Phase:
    def __init__(self, nc, ctx, prefix=""):
        self.nc = nc
        self.ctx = ctx
        self.prefix = prefix
        self.block = None
        self.E = None
        self.sems = {}
        self.dram = {}
        self.nbar = 0

    def nm(self, name):
        return self.prefix + name

    def sem(self, name):
        if name not in self.sems:
            self.sems[name] = Sem(self.ctx, self.nc, "S_" + name)
        return SemView(self.sems[name])

    def dio(self, name, shape, dt, kind):
        if name in self.dram:
            return self.dram[name]
        kindmap = {"in": "ExternalInput", "out": "ExternalOutput", "int": "Internal"}
        nm = name if kind != "int" else self.prefix + name
        ap = self.nc.dram_tensor(nm, list(shape), dt, kind=kindmap[kind]).ap()
        self.dram[name] = ap
        return ap

    def engines(self):
        if self.E is not None:
            return self.E
        self.block = self.ctx.enter_context(self.nc.Block())
        block = self.block

        class NS:
            pass
        E = NS()

        @block.vector
        def _(v):
            E.ve = Eng(self.ctx, self.nc, v, "E_ve")

        @block.scalar
        def _(s):
            E.se = Eng(self.ctx, self.nc, s, "E_se")

        @block.tensor
        def _(t):
            E.pe = Pe(self.ctx, self.nc, "E_pe")

        @block.gpsimd
        def _(g):
            E.gq = Dma(self.ctx, self.nc, g, "E_gq")
            E.ge = Eng(self.ctx, self.nc, g, "E_ge")

        @block.sync
        def _(s):
            E.sq = Dma(self.ctx, self.nc, s, "E_sq")
        self.E = E
        return E

    def barrier(self):
        E = self.E
        block = self.block
        self.nbar += 1
        engs = [E.ve, E.se, E.pe, E.ge]
        dsems = [s for s in self.sems.values()] + [E.gq.q, E.sq.q]
        targets = [(e.h, e.n) for e in engs] + [(s.h, s.n) for s in dsems]

        def emit(eng):
            for h, n in targets:
                if n:
                    eng.wait_ge(h, n)

        @block.vector
        def _(v):
            emit(v)

        @block.scalar
        def _(s):
            emit(s)

        @block.tensor
        def _(t):
            emit(t)

        @block.gpsimd
        def _(g):
            emit(g)

        @block.sync
        def _(s):
            emit(s)



def phase_mamba_front(P):
    nc = P.nc
    TE = TC + 3
    xt = P.dio("xt", [D, TE], F32, "in")
    wT = P.dio("wT", [D, D_IN_PROJ], F32, "in")
    cw = P.dio("cw", [128, 34 * 4], F32, "in")
    cb = P.dio("cb", [128, 34], F32, "in")
    dtb = P.dio("dtb", [NHEADS, 1], F32, "in")
    alog = P.dio("alog", [NHEADS, 1], F32, "in")
    ones_bf = P.dio("ones_bf", [128, 128], BF16, "in")

    zs_o = P.dio("zs", [D_INNER, TC], BF16, "out")
    xc_o = P.dio("xc", [D_INNER, TC], BF16, "out")
    bf_o = P.dio("Bf", [D_STATE, TC], BF16, "out")
    cf_o = P.dio("Cf", [D_STATE, TC], BF16, "out")
    dtv_o = P.dio("dtv", [NHEADS, TC], F32, "out")
    cuma_o = P.dio("cumA", [NHEADS, TC], F32, "out")
    xn_o = P.dio("xn", [D, TE], BF16, "out")

    NG = 5
    GOFF = [0, 512, 1024, 1536, 2048]
    GW = [512, 512, 512, 512, 3]
    NE = 67

    with contextlib.ExitStack() as ctx:
        sx = ctx.enter_context(nc.sbuf_tensor(P.nm("sx"), [128, 16 * TE], BF16))
        sx2 = ctx.enter_context(nc.sbuf_tensor(P.nm("sx2"), [128, 16 * 512], BF16))
        sm = ctx.enter_context(nc.sbuf_tensor(P.nm("sm"), [128, TE], F32))
        srs = ctx.enter_context(nc.sbuf_tensor(P.nm("srs"), [128, TE], F32))
        svar = ctx.enter_context(nc.sbuf_tensor(P.nm("svar"), [128, 512], F32))
        sw = ctx.enter_context(nc.sbuf_tensor(P.nm("sw"), [128, 2 * 2048], BF16))
        sasm = ctx.enter_context(nc.sbuf_tensor(P.nm("sasm"), [128, 2 * TE], F32))
        stmp = ctx.enter_context(nc.sbuf_tensor(P.nm("stmp"), [128, 2 * TC], F32))
        sconv = ctx.enter_context(nc.sbuf_tensor(P.nm("sconv"), [128, 2 * TC], BF16))
        sones = ctx.enter_context(nc.sbuf_tensor(P.nm("sones"), [128, 128], BF16))
        scw = ctx.enter_context(nc.sbuf_tensor(P.nm("scw"), [128, 34 * 4], F32))
        scb = ctx.enter_context(nc.sbuf_tensor(P.nm("scb"), [128, 34], F32))
        sdtb = ctx.enter_context(nc.sbuf_tensor(P.nm("sdtb"), [64, 1], F32))
        salog = ctx.enter_context(nc.sbuf_tensor(P.nm("salog"), [64, 1], F32))
        sdta = ctx.enter_context(nc.sbuf_tensor(P.nm("sdta"), [64, TE], F32))
        sdtb2 = ctx.enter_context(nc.sbuf_tensor(P.nm("sdtb2"), [64, TE], F32))
        sdtv = ctx.enter_context(nc.sbuf_tensor(P.nm("sdtv"), [64, TE], F32))
        sneg = ctx.enter_context(nc.sbuf_tensor(P.nm("sneg"), [64, 1], F32))
        scum = ctx.enter_context(nc.sbuf_tensor(P.nm("scum"), [64, TC], F32))
        szero = ctx.enter_context(nc.sbuf_tensor(P.nm("szero"), [64, 128], F32))

        pS1 = ctx.enter_context(nc.psum_tensor(P.nm("pS1"), [128, 512], F32))
        pS2 = ctx.enter_context(nc.psum_tensor(P.nm("pS2"), [128, 512], F32))
        pmm = ctx.enter_context(nc.psum_tensor(P.nm("pmm"), [128, 2 * 512], F32))

        E = P.engines()
        block = P.block
        ve, se, pe, gq, sq = E.ve, E.se, E.pe, E.gq, E.sq
        ldx = P.sem("ldx")
        ldwb = [P.sem("ldwb0"), P.sem("ldwb1")]
        ldc = P.sem("ldc")
        stxn = P.sem("stxn")
        stcb = [P.sem("stcb0"), P.sem("stcb1")]
        stfin = P.sem("stfin")

        @block.gpsimd
        def _(g):
            for k in range(16):
                gq.start(sx[:, k * TE:(k + 1) * TE], xt[k * 128:(k + 1) * 128, :], sem=ldx)
            gq.start(sw[:, 0:2048], wT[:, 0:128].rearrange("(k p) e -> p k e", p=128),
                     sem=ldwb[0])

        @block.sync
        def _(s):
            sq.start(scw[:, :], cw, sem=ldc)
            sq.start(scb[:, :], cb, sem=ldc)
            sq.start(sdtb[:], dtb, sem=ldc)
            sq.start(salog[:], alog, sem=ldc)
            sq.start(sones[:], ones_bf, sem=ldc)
        ldc_done = ldc.n

        v_xn_done = None
        for g in range(NG):
            o, w = GOFF[g], GW[g]

            @block.vector
            def _(v, g=g, o=o, w=w):
                if g == 0:
                    ve.wait(ldx)
                for k in range(16):
                    ve.tensor_mul(sx2[:, k * 512:k * 512 + w],
                                  sx[:, k * TE + o:k * TE + o + w],
                                  sx[:, k * TE + o:k * TE + o + w])
            vx2 = ve.n

            @block.tensor
            def _(t, g=g, o=o, w=w, vx2=vx2):
                pe.wait(ldc, ldc_done)
                pe.wait(ve, vx2)
                for k in range(16):
                    pe.matmul(pS1[:, 0:w], sones[:], sx[:, k * TE + o:k * TE + o + w],
                              start=(k == 0), stop=(k == 15))
                for k in range(16):
                    i = pe.matmul(pS2[:, 0:w], sones[:], sx2[:, k * 512:k * 512 + w],
                                  start=(k == 0), stop=(k == 15))
                pe.done(i)
            pest = pe.n

            @block.vector
            def _(v, g=g, o=o, w=w, pest=pest):
                ve.wait(pe, pest)
                ve.tensor_scalar_mul(sm[:, o:o + w], pS1[:, 0:w], 1.0 / D)
                ve.tensor_mul(svar[:, 0:w], sm[:, o:o + w], sm[:, o:o + w])
                ve.scalar_tensor_tensor(svar[:, 0:w], pS2[:, 0:w], 1.0 / D,
                                        svar[:, 0:w], ALU.mult, ALU.subtract)
                ve.tensor_scalar_add(svar[:, 0:w], svar[:, 0:w], LN_EPS)
            vnorm = ve.n

            @block.scalar
            def _(s, g=g, o=o, w=w, vnorm=vnorm):
                se.wait(ve, vnorm)
                se.activation(srs[:, o:o + w], svar[:, 0:w], AF.Ln)
                se.activation(srs[:, o:o + w], srs[:, o:o + w], AF.Exp, scale=-0.5)
            snorm = se.n

            @block.vector
            def _(v, g=g, o=o, w=w, snorm=snorm):
                ve.wait(se, snorm)
                for k in range(16):
                    sl = slice(k * TE + o, k * TE + o + w)
                    ve.tensor_sub(sx[:, sl], sx[:, sl], sm[:, o:o + w])
                    ve.tensor_mul(sx[:, sl], sx[:, sl], srs[:, o:o + w])
            v_xn_done = ve.n

        @block.sync
        def _(s):
            sq.wait(ve, v_xn_done)
            for k in range(16):
                sq.start(xn_o[k * 128:(k + 1) * 128, :], sx[:, k * TE:(k + 1) * TE], sem=stxn)

        pemm = {}
        sev = {}
        sev_tile = {}
        vcv = {}
        scv = {}

        for e in range(NE):
            rows = 64 if e == 66 else 128
            pp = e % 2
            ppn = (e + 1) % 2

            @block.gpsimd
            def _(g, e=e, ppn=ppn):
                if e + 1 < NE:
                    nrows = 64 if e + 1 == 66 else 128
                    if e >= 1:
                        gq.wait(pe, pemm[(e - 1, NG - 1)])
                    gq.start(sw[:, ppn * 2048:ppn * 2048 + 16 * nrows],
                             wT[:, (e + 1) * 128:(e + 1) * 128 + nrows]
                             .rearrange("(k p) e -> p k e", p=128), sem=ldwb[ppn])

            for g in range(NG):
                o, w = GOFF[g], GW[g]
                idx = e * NG + g
                bank = idx % 2

                @block.tensor
                def _(t, e=e, rows=rows, pp=pp, g=g, o=o, w=w, idx=idx, bank=bank):
                    if g == 0:
                        pe.wait(ldwb[e % 2], ldwb[e % 2].base + 16 * (e // 2 + 1))
                        if e == 0:
                            pe.wait(ve, v_xn_done)
                    if idx >= 2:
                        pidx = idx - 2
                        pe.wait(se, sev[(pidx // NG, pidx % NG)])
                    for k in range(16):
                        i = pe.matmul(pmm[0:rows, bank * 512:bank * 512 + w],
                                      sw[:, pp * 2048 + k * rows: pp * 2048 + (k + 1) * rows],
                                      sx[:, k * TE + o:k * TE + o + w],
                                      start=(k == 0), stop=(k == 15))
                    pemm[(e, g)] = pe.done(i)

                @block.scalar
                def _(s, e=e, rows=rows, pp=pp, g=g, o=o, w=w, idx=idx, bank=bank):
                    if g == 0:
                        if e < 32 and e >= 2:
                            se.wait(stcb[e % 2], stcb[e % 2].base + 16 * (e // 2))
                        if 34 <= e < 66:
                            se.wait(ve, vcv[e - 2])
                    se.wait(pe, pemm[(e, g)])
                    if e < 32:
                        lo = max(o, 3) - 3
                        hi = o + w - 3
                        se.activation(sconv[:, pp * TC + lo:pp * TC + hi],
                                      pmm[:, bank * 512 + (lo + 3 - o):bank * 512 + (hi + 3 - o)],
                                      AF.Silu)
                    elif e == 66:
                        se.copy(sdta[0:64, o:o + w], pmm[0:64, bank * 512:bank * 512 + w])
                    else:
                        se.copy(sasm[:, pp * TE + o:pp * TE + o + w],
                                pmm[:, bank * 512:bank * 512 + w])
                    sev[(e, g)] = se.n
            sev_tile[e] = se.n

            if e < 32:
                @block.sync
                def _(s, e=e, pp=pp):
                    sq.wait(se, sev_tile[e])
                    sq.start(zs_o[e * 128:(e + 1) * 128, :], sconv[:, pp * TC:(pp + 1) * TC],
                             sem=stcb[pp])
            elif e < 66:
                ct = e - 32

                @block.vector
                def _(v, e=e, pp=pp, ct=ct):
                    ve.wait(se, sev_tile[e])
                    if e >= 34:
                        ve.wait(se, scv[e - 2])
                    a = sasm[:, pp * TE:(pp + 1) * TE]
                    tb = stmp[:, pp * TC:(pp + 1) * TC]
                    ve.tensor_scalar(tb, a[:, 0:TC], scw[:, 4 * ct:4 * ct + 1],
                                     scb[:, ct:ct + 1], ALU.mult, ALU.add)
                    for k in range(1, 4):
                        ve.scalar_tensor_tensor(tb, a[:, k:TC + k], scw[:, 4 * ct + k:4 * ct + k + 1],
                                                tb, ALU.mult, ALU.add)
                    vcv[e] = ve.n

                @block.scalar
                def _(s, e=e, pp=pp):
                    se.wait(ve, vcv[e])
                    if e >= 2:
                        se.wait(stcb[e % 2], stcb[e % 2].base + 16 * (e // 2))
                    se.activation(sconv[:, pp * TC:(pp + 1) * TC],
                                  stmp[:, pp * TC:(pp + 1) * TC], AF.Silu)
                    scv[e] = se.n

                @block.sync
                def _(s, e=e, pp=pp):
                    sq.wait(se, scv[e])
                    if e < 64:
                        dst = xc_o[(e - 32) * 128:(e - 31) * 128, :]
                    elif e == 64:
                        dst = bf_o
                    else:
                        dst = cf_o
                    sq.start(dst, sconv[:, pp * TC:(pp + 1) * TC], sem=stcb[pp])

        @block.scalar
        def _(s):
            se.wait(ldc, ldc_done)
            se.activation(sdtb2[0:64, :], sdta[0:64, :], AF.Abs, bias=sdtb[:])
            se.activation(sdtb2[0:64, :], sdtb2[0:64, :], AF.Exp, scale=-1.0)
            se.activation(sdtb2[0:64, :], sdtb2[0:64, :], AF.Ln, bias=1.0)
            se.activation(sdta[0:64, :], sdta[0:64, :], AF.Relu, bias=sdtb[:])
            se.activation(sneg[:], salog[:], AF.Exp)
        s_dt = se.n

        @block.vector
        def _(v):
            ve.wait(se, s_dt)
            ve.tensor_add(sdtv[0:64, :], sdta[0:64, :], sdtb2[0:64, :])
            ve.tensor_scalar_mul(sneg[:], sneg[:], -1.0)
            ve.memset(szero[:], 0.0)
            ve.tensor_scalar_mul(sdtb2[0:64, 0:TC], sdtv[0:64, 3:3 + TC], sneg[:])
            for c in range(TC // CHUNK):
                ve.tensor_tensor_scan(scum[:, c * CHUNK:(c + 1) * CHUNK],
                                      sdtb2[0:64, c * CHUNK:(c + 1) * CHUNK],
                                      szero[:], 0.0, ALU.add, ALU.add)
        v_dt = ve.n

        @block.sync
        def _(s):
            sq.wait(ve, v_dt)
            sq.start(dtv_o, sdtv[0:64, 3:3 + TC], sem=stfin)
            sq.start(cuma_o, scum[0:64, :], sem=stfin)
            for sem in (stfin, stxn, stcb[0], stcb[1], ldc):
                s.wait_ge(sem.h, sem.n)

        @block.gpsimd
        def _(g):
            for sem in (ldx, ldwb[0], ldwb[1]):
                g.wait_ge(sem.h, sem.n)



def host_mamba_front(u_flat, in_proj_w, conv_w, conv_b, dt_bias, A_log):
    wT = _f32(in_proj_w.T)
    ones_bf = np.ones((128, 128), dtype=BF)
    cw_t = _f32(conv_w.reshape(34, 128, 4).transpose(1, 0, 2).reshape(128, 136))
    cb_t = _f32(conv_b.reshape(34, 128).T)
    in_maps = []
    for c in range(NCORES):
        s0 = c * TC
        bstart = (s0 // T) * T
        nh = min(3, s0 - bstart)
        ext = np.zeros((TC + 3, D), np.float32)
        if nh:
            ext[3 - nh:3] = u_flat[s0 - nh:s0]
        ext[3:] = u_flat[s0:s0 + TC]
        in_maps.append({
            "xt": _f32(ext.T),
            "wT": wT,
            "cw": cw_t,
            "cb": cb_t,
            "dtb": _f32(dt_bias.reshape(-1, 1)),
            "alog": _f32(A_log.reshape(-1, 1)),
            "ones_bf": ones_bf,
        })
    return in_maps




def phase_ssd(P):
    nc = P.nc
    NCH = TC // CHUNK
    HP = D_INNER
    xc_i = P.dio("xc", [D_INNER, TC], BF16, "in")
    b_fm = P.dio("Bf", [D_STATE, TC], BF16, "in")
    c_fm = P.dio("Cf", [D_STATE, TC], BF16, "in")
    dtv_i = P.dio("dtv", [NHEADS, TC], F32, "in")
    cuma_i = P.dio("cumA", [NHEADS, TC], F32, "in")
    idnf = P.dio("idnf", [128, 128], F32, "in")
    drow = P.dio("drow", [1, 64], F32, "in")
    trilm = P.dio("trilm", [128, 128], F32, "in")
    onecol = P.dio("onecol", [1, 128], F32, "in")
    bmask = P.dio("bmask", [8, 8], F32, "in")
    pmask = P.dio("pmask", [1, 8], F32, "in")

    y_o = P.dio("y_pm", [TC, HP], BF16, "out")

    cc_S = P.dio("cc_S", [D_STATE, HP], F32, "int")
    cc_sl = P.dio("cc_sl", [1, 64], F32, "int")
    ag_S = P.dio("ag_S", [NCORES * D_STATE, HP], F32, "int")
    ag_sl = P.dio("ag_sl", [NCORES, 64], F32, "int")

    with contextlib.ExitStack() as ctx:
        sB_pm = ctx.enter_context(nc.sbuf_tensor(P.nm("sB_pm"), [128, NCH * 128], BF16))
        sB_fm = ctx.enter_context(nc.sbuf_tensor(P.nm("sB_fm"), [128, TC], BF16))
        sC_fm = ctx.enter_context(nc.sbuf_tensor(P.nm("sC_fm"), [128, TC], BF16))
        sdtv = ctx.enter_context(nc.sbuf_tensor(P.nm("sdtv"), [128, NCH * 64], F32))
        scuma = ctx.enter_context(nc.sbuf_tensor(P.nm("scuma"), [128, NCH * 64], F32))
        sculc = ctx.enter_context(nc.sbuf_tensor(P.nm("sculc"), [1, NCH * 64], F32))
        sculh = ctx.enter_context(nc.sbuf_tensor(P.nm("sculh"), [1, 64 * NCH], F32))
        sdtvf = ctx.enter_context(nc.sbuf_tensor(P.nm("sdtvf"), [64, TC], F32))
        scumaf = ctx.enter_context(nc.sbuf_tensor(P.nm("scumaf"), [64, TC], F32))
        sidnf = ctx.enter_context(nc.sbuf_tensor(P.nm("sidnf"), [128, 128], F32))
        srowp = ctx.enter_context(nc.sbuf_tensor(P.nm("srowp"), [1, 64 * 128], F32))
        sdrow = ctx.enter_context(nc.sbuf_tensor(P.nm("sdrow"), [1, 64], F32))
        stril = ctx.enter_context(nc.sbuf_tensor(P.nm("stril"), [128, 128], F32))
        sonec = ctx.enter_context(nc.sbuf_tensor(P.nm("sonec"), [1, 128], F32))
        sbm = ctx.enter_context(nc.sbuf_tensor(P.nm("sbm"), [8, 8], F32))
        spm = ctx.enter_context(nc.sbuf_tensor(P.nm("spm"), [1, 8], F32))
        sx = ctx.enter_context(nc.sbuf_tensor(P.nm("sx"), [128, 2 * HP], BF16))
        sxdt = ctx.enter_context(nc.sbuf_tensor(P.nm("sxdt"), [128, HP], BF16))
        sxdt2 = ctx.enter_context(nc.sbuf_tensor(P.nm("sxdt2"), [128, HP], BF16))
        sS = ctx.enter_context(nc.sbuf_tensor(P.nm("sS"), [128, HP], F32))
        sSb = ctx.enter_context(nc.sbuf_tensor(P.nm("sSb"), [128, HP], BF16))
        sdec = ctx.enter_context(nc.sbuf_tensor(P.nm("sdec"), [128, 64], F32))
        sds = ctx.enter_context(nc.sbuf_tensor(P.nm("sds"), [128, 64], F32))
        sdd = ctx.enter_context(nc.sbuf_tensor(P.nm("sdd"), [128, 64], F32))
        sDbc = ctx.enter_context(nc.sbuf_tensor(P.nm("sDbc"), [128, 64], F32))
        seA = ctx.enter_context(nc.sbuf_tensor(P.nm("seA"), [128, 64], F32))
        sGtm = ctx.enter_context(nc.sbuf_tensor(P.nm("sGtm"), [128, 128], BF16))
        sLt = ctx.enter_context(nc.sbuf_tensor(P.nm("sLt"), [128, 1024], F32))
        sMt = ctx.enter_context(nc.sbuf_tensor(P.nm("sMt"), [128, 1024], BF16))
        syo = ctx.enter_context(nc.sbuf_tensor(P.nm("syo"), [128, 512], F32))
        sy = ctx.enter_context(nc.sbuf_tensor(P.nm("sy"), [128, 3 * 512], BF16))
        ssum = ctx.enter_context(nc.sbuf_tensor(P.nm("ssum"), [1, 64], F32))
        srow = ctx.enter_context(nc.sbuf_tensor(P.nm("srow"), [1, 64], F32))
        sSin = ctx.enter_context(nc.sbuf_tensor(P.nm("sSin"), [128, HP], F32))
        scoef = ctx.enter_context(nc.sbuf_tensor(P.nm("scoef"), [128, 64], F32))
        sfac = ctx.enter_context(nc.sbuf_tensor(P.nm("sfac"), [128, 64], F32))
        sagsl = ctx.enter_context(nc.sbuf_tensor(P.nm("sagsl"), [8, 64], F32))
        scexp = ctx.enter_context(nc.sbuf_tensor(P.nm("scexp"), [128, 64], F32))
        sagS = sx[:, 0:2 * HP].bitcast(F32)
        sSinb = sxdt[:, 0:HP]
        stmp = ctx.enter_context(nc.sbuf_tensor(P.nm("stmp"), [128, 2048], F32))

        pst = ctx.enter_context(nc.psum_tensor(P.nm("pst"), [128, 1024], F32))
        pLt = ctx.enter_context(nc.psum_tensor(P.nm("pLt"), [128, 1024], F32))
        pYd = ctx.enter_context(nc.psum_tensor(P.nm("pYd"), [128, 512], F32))
        pYo = ctx.enter_context(nc.psum_tensor(P.nm("pYo"), [128, 512], F32))
        pbcC = ctx.enter_context(nc.psum_tensor(P.nm("pbcC"), [128, 512], F32))
        pGt = ctx.enter_context(nc.psum_tensor(P.nm("pGt"), [128, 512], F32))

        E = P.engines()
        block = P.block
        ve, se, pe, gq, ge, sq = E.ve, E.se, E.pe, E.gq, E.ge, E.sq
        K = {}
        ldc = P.sem("ldc2")
        ldxb = [P.sem("ldxb0"), P.sem("ldxb1")]
        styp = [P.sem("styp0"), P.sem("styp1")]
        stS = P.sem("stS")
        agc = P.sem("agc")
        agl = P.sem("agl")
        ags = P.sem("ags")
        sty2 = [P.sem("sty2a"), P.sem("sty2b")]
        ldrow = P.sem("ldrow")

        @block.sync
        def _(s):
            for k in range(NCH):
                sq.transpose(sB_pm[:, k * 128:(k + 1) * 128],
                             b_fm[:, k * 128:(k + 1) * 128], sem=ldc)
            sq.start(sB_fm[:, :], b_fm, sem=ldc)
            sq.start(sC_fm[:, :], c_fm, sem=ldc)
            sq.start(sdtvf[:, :], dtv_i, sem=ldc)
            sq.start(scumaf[:, :], cuma_i, sem=ldc)
            sq.start(sidnf[:, :], idnf, sem=ldc)
            for k in range(NCH):
                sq.start(sculc[0:1, k * 64:(k + 1) * 64].unsqueeze(2),
                         cuma_i[:, k * 128 + 127:k * 128 + 128].rearrange("h q -> () h q"),
                         sem=ldc, allow_slow_non_contiguous=True)
                sq.start(sculh[0:1, :].rearrange("o (h c) -> o h c", c=NCH)[:, :, k:k + 1],
                         cuma_i[:, k * 128 + 127:k * 128 + 128].rearrange("h q -> () h q"),
                         sem=ldc, allow_slow_non_contiguous=True)
            sq.start(sdrow[:, :], drow, sem=ldc)
            sq.start(stril[:, :], trilm, sem=ldc)
            sq.start(sonec[:, :], onecol, sem=ldc)
            sq.start(sbm[:, :], bmask, sem=ldc)
            sq.start(spm[:, :], pmask, sem=ldc)
        ld_all = ldc.n

        for k in range(NCH):
            @block.tensor
            def _(t, k=k):
                if k == 0:
                    pe.wait(ldc, ld_all)
                else:
                    pe.wait(ve, K[("cvt", k - 1)])
                i = pe.transpose(pGt[:, 256:320], sdtvf[:, k * 128:(k + 1) * 128],
                                 sidnf[0:64, 0:64])
                pe.done(i)
                i = pe.transpose(pGt[:, 320:384], scumaf[:, k * 128:(k + 1) * 128],
                                 sidnf[0:64, 0:64])
                K[("tcma", k)] = pe.done(i)

            @block.vector
            def _(v, k=k):
                ve.wait(pe, K[("tcma", k)])
                ve.tensor_copy(sdtv[:, k * 64:(k + 1) * 64], pGt[:, 256:320])
                ve.tensor_copy(scuma[:, k * 64:(k + 1) * 64], pGt[:, 320:384])
            K[("cvt", k)] = ve.n
        cvt_done = ve.n

        @block.sync
        def _(s):
            sq.transpose(sx[:, 0:HP], xc_i[:, 0:128], sem=ldxb[0])

        @block.tensor
        def _(t):
            pe.wait(ldc, ld_all)
            pe.wait(ve, cvt_done)
            i = pe.matmul(pGt[:, 0:64], sonec[:, :], sdrow[:, :], start=True, stop=True)
            pe.done(i)
        pe_dbc = pe.n

        @block.vector
        def _(v):
            ve.wait(pe, pe_dbc)
            ve.tensor_copy(sDbc[:, :], pGt[:, 0:64])
            ve.memset(sS[:, :], 0.0)
        v_init = ve.n

        for c in range(NCH):
            pp = c % 2
            ppn = (c + 1) % 2
            cg = slice(c * 64, (c + 1) * 64)

            @block.sync
            def _(s, c=c, ppn=ppn):
                if c + 1 < NCH:
                    if c >= 1:
                        sq.wait(ve, K[(c - 1, "xdt")])
                        sq.wait(ge, K[(c - 1, 7, "gd")])
                    sq.transpose(sx[:, ppn * HP:(ppn + 1) * HP],
                                 xc_i[:, (c + 1) * 128:(c + 2) * 128], sem=ldxb[ppn])

            @block.gpsimd
            def _(g, c=c, ppn=ppn):
                if True:
                    pass
                if c > 0:
                    gq.wait(pe, K[(c - 1, 7, "lt")])
                gq.start(srowp[0:1, :].rearrange("o (h q) -> o h q", q=128),
                         cuma_i[:, (c) * 128:(c + 1) * 128].rearrange("h q -> () h q"),
                         sem=ldrow)

            @block.tensor
            def _(t, c=c):
                if c == 0:
                    pe.wait(ve, v_init)
                else:
                    pe.wait(se, K[(c - 1, "dec")])
                    pe.wait(ve, K[(c - 1, "ds")])
                i = pe.matmul(pbcC[:, 0:64], sonec[:, :],
                              sculc[:, c * 64:(c + 1) * 64], start=True, stop=True)
                K[(c, "bc")] = pe.done(i)

            @block.vector
            def _(v, c=c, cg=cg):
                ve.wait(pe, K[(c, "bc")])
                if c == 0:
                    ve.wait(ve, cvt_done) if False else None
                ve.tensor_sub(sds[:, :], pbcC[:, 0:64], scuma[:, cg])
            K[(c, "ds")] = ve.n

            @block.scalar
            def _(s, c=c, cg=cg):
                se.wait(ve, K[(c, "ds")])
                se.activation(sdec[:, :], pbcC[:, 0:64], AF.Exp)
                se.activation(sds[:, :], sds[:, :], AF.Exp)
                se.activation(seA[:, :], scuma[:, cg], AF.Exp)
            K[(c, "dec")] = se.n

            @block.vector
            def _(v, c=c, cg=cg, pp=pp):
                ve.wait(se, K[(c, "dec")])
                ve.tensor_mul(sdd[:, :], sds[:, :], sdtv[:, cg])
                if c == 0:
                    ve.wait(ldxb[0], 16)
                else:
                    ve.wait(ldxb[pp], ldxb[pp].base + 16 * (c // 2 + 1))
                xin = sx[:, pp * HP:(pp + 1) * HP].rearrange("p (h q) -> p h q", h=64)
                ve.tensor_tensor(sxdt.rearrange("p (h q) -> p h q", h=64), xin,
                                 sdtv[:, cg].unsqueeze(2).broadcast_to((128, 64, 64)), ALU.mult)
                ve.tensor_tensor(sxdt2.rearrange("p (h q) -> p h q", h=64), xin,
                                 sdd[:, :].unsqueeze(2).broadcast_to((128, 64, 64)), ALU.mult)
            K[(c, "xdt")] = ve.n

            @block.scalar
            def _(s, c=c):
                se.wait(ve, K[(c, "xdt")])
                se.copy(sSb[:, :], sS[:, :])
            K[(c, "sbf")] = se.n

            @block.tensor
            def _(t, c=c):
                pe.wait(se, K[(c, "sbf")])
                if c == 0:
                    pe.wait(ve, cvt_done)
                if c > 0:
                    pe.wait(ve, K[(c - 1, "gtm")])
                i = pe.matmul(pGt[:, 128:256], sB_fm[:, c * 128:(c + 1) * 128],
                              sC_fm[:, c * 128:(c + 1) * 128], start=True, stop=True)
                K[(c, "gt")] = pe.done(i)

            @block.vector
            def _(v, c=c):
                ve.wait(pe, K[(c, "gt")])
                ve.tensor_mul(sGtm[:, :], pGt[:, 128:256], stril[:, :])
            K[(c, "gtm")] = ve.n

            for hg in range(8):
                idx = c * 8 + hg
                par = idx % 2
                hsl = slice(hg * 512, (hg + 1) * 512)

                @block.tensor
                def _(t, c=c, hg=hg):
                    if hg == 0:
                        pe.wait(ve, K[(c, "gtm")])
                        pe.wait(ldrow, ldrow.base + 16 * (c + 1))
                    prev = K.get((c, hg - 1, "ltsub") if hg else (c - 1, 7, "ltsub"))
                    if prev:
                        pe.wait(ve, prev)
                    for j in range(2):
                        i = pe.matmul(pLt[:, j * 512:(j + 1) * 512], sonec[:, :],
                                      srowp[0:1, hg * 1024 + j * 512:hg * 1024 + (j + 1) * 512],
                                      start=True, stop=True)
                    K[(c, hg, "lt")] = pe.done(i)

                @block.vector
                def _(v, c=c, hg=hg):
                    ve.wait(pe, K[(c, hg, "lt")])
                    ve.tensor_tensor(sLt.rearrange("p (h q) -> p h q", h=8),
                                     scuma[:, c * 64 + hg * 8:c * 64 + (hg + 1) * 8]
                                     .unsqueeze(2).broadcast_to((128, 8, 128)),
                                     pLt.rearrange("p (h q) -> p h q", h=8),
                                     ALU.subtract)
                K[(c, hg, "ltsub")] = ve.n

                @block.scalar
                def _(s, c=c, hg=hg):
                    se.wait(ve, K[(c, hg, "ltsub")])
                    if K.get((c, hg - 1, "mt") if hg else (c - 1, 7, "mt")):
                        se.wait(ve, K[(c, hg - 1, "mt") if hg else (c - 1, 7, "mt")])
                    se.activation(sLt[:, :], sLt[:, :], AF.Relu)
                    se.activation(sMt[:, :], sLt[:, :], AF.Exp, scale=-1.0)
                K[(c, hg, "srelu")] = se.n

                @block.vector
                def _(v, c=c, hg=hg):
                    ve.wait(se, K[(c, hg, "srelu")])
                    ve.tensor_tensor(sMt.rearrange("p (h q) -> p h q", h=8),
                                     sMt.rearrange("p (h q) -> p h q", h=8),
                                     sGtm[:, :].unsqueeze(1).broadcast_to((128, 8, 128)),
                                     ALU.mult)
                K[(c, hg, "mt")] = ve.n

                @block.tensor
                def _(t, c=c, hg=hg, hsl=hsl):
                    pe.wait(ve, K[(c, hg, "mt")])
                    prev_vy = K.get((c, hg - 1, "vy") if hg else (c - 1, 7, "vy"))
                    if prev_vy:
                        pe.wait(ve, prev_vy)
                    for h in range(8):
                        i = pe.matmul(pYd[:, h * 64:(h + 1) * 64], sMt[:, h * 128:(h + 1) * 128],
                                      sxdt[:, hg * 512 + h * 64:hg * 512 + (h + 1) * 64],
                                      start=True, stop=True)
                    pe.done(i)
                    i = pe.matmul(pYo[:, :], sC_fm[:, c * 128:(c + 1) * 128],
                                  sSb[:, hsl], start=True, stop=True)
                    K[(c, hg, "yo")] = pe.done(i)

                @block.gpsimd
                def _(g, c=c, hg=hg, pp=pp):
                    ge.wait(ve, K[(c, "xdt")])
                    prev_vy = K.get((c, hg - 1, "vy") if hg else (c - 1, 7, "vy"))
                    if prev_vy:
                        ge.wait(ve, prev_vy)
                    ge.tensor_tensor(sy[:, 1024:1536].rearrange("p (h q) -> p h q", h=8),
                                     sx[:, pp * HP + hg * 512:pp * HP + (hg + 1) * 512]
                                     .rearrange("p (h q) -> p h q", h=8),
                                     sDbc[:, hg * 8:(hg + 1) * 8].unsqueeze(2).broadcast_to((128, 8, 64)),
                                     ALU.mult)
                K[(c, hg, "gd")] = ge.n

                @block.vector
                def _(v, c=c, hg=hg, par=par, idx=idx):
                    ve.wait(pe, K[(c, hg, "yo")])
                    ve.wait(ge, K[(c, hg, "gd")])
                    if idx >= 2:
                        ve.wait(styp[par], styp[par].base + 16 * (idx // 2))
                    ve.tensor_tensor(syo.rearrange("p (h q) -> p h q", h=8),
                                     pYo[:, :].rearrange("p (h q) -> p h q", h=8),
                                     seA[:, hg * 8:(hg + 1) * 8].unsqueeze(2).broadcast_to((128, 8, 64)),
                                     ALU.mult)
                    ve.tensor_add(syo[:, :], syo[:, :], pYd[:, :])
                    ve.tensor_add(sy[:, par * 512:(par + 1) * 512], syo[:, :], sy[:, 1024:1536])
                K[(c, hg, "vy")] = ve.n

                @block.sync
                def _(s, c=c, hg=hg, par=par):
                    sq.wait(ve, K[(c, hg, "vy")])
                    sq.start(y_o[c * 128:(c + 1) * 128, hg * 512:(hg + 1) * 512],
                             sy[:, par * 512:(par + 1) * 512], sem=styp[par])

            for qt in range(4):
                hs = slice(qt * 1024, (qt + 1) * 1024)

                @block.tensor
                def _(t, c=c, qt=qt, hs=hs):
                    if qt == 0:
                        pe.wait(ve, K[(c, 7, "vy")])
                    else:
                        pe.wait(ve, K[(c, qt - 1, "supd")])
                    for j in range(2):
                        i = pe.matmul(pst[:, j * 512:(j + 1) * 512],
                                      sB_pm[:, c * 128:(c + 1) * 128],
                                      sxdt2[:, qt * 1024 + j * 512:qt * 1024 + (j + 1) * 512],
                                      start=True, stop=True)
                    K[(c, qt, "st")] = pe.done(i)

                @block.gpsimd
                def _(g, c=c, qt=qt, hs=hs):
                    ge.wait(se, K[(c, "sbf")])
                    if K.get((c, qt - 1, "supd") if qt else (c - 1, 3, "supd")):
                        ge.wait(ve, K[(c, qt - 1, "supd") if qt else (c - 1, 3, "supd")])
                    ge.tensor_tensor(stmp[:, 0:1024].rearrange("p (h q) -> p h q", h=16),
                                     sS[:, hs].rearrange("p (h q) -> p h q", h=16),
                                     sdec[:, qt * 16:(qt + 1) * 16].unsqueeze(2)
                                     .broadcast_to((128, 16, 64)), ALU.mult)
                K[(c, qt, "gs")] = ge.n

                @block.vector
                def _(v, c=c, qt=qt, hs=hs):
                    ve.wait(pe, K[(c, qt, "st")])
                    ve.wait(ge, K[(c, qt, "gs")])
                    ve.tensor_add(sS[:, hs], stmp[:, 0:1024], pst[:, :])
                K[(c, qt, "supd")] = ve.n

        @block.vector
        def _(v):
            ve.tensor_reduce(ssum[0:1, :], sculh[:, :].rearrange("p (h c) -> p h c", h=64),
                             mybir.AxisListType.X, ALU.add)
        v_sl = ve.n

        @block.sync
        def _(s):
            sq.wait(ve, v_sl)
            sq.wait(ve, K[(NCH - 1, 3, "supd")])
            sq.start(cc_S, sS[:, :], sem=stS)
            sq.start(cc_sl, ssum[0:1, :], sem=stS)

        @block.gpsimd
        def _(g):
            g.wait_ge(stS.h, stS.n)
            gq.collective("AllGather", ALU.bypass, [list(range(NCORES))],
                          [cc_S.opt()], [ag_S.opt()], sem=agc)
            gq.collective("AllGather", ALU.bypass, [list(range(NCORES))],
                          [cc_sl.opt()], [ag_sl.opt()], sem=agc)

        @block.sync
        def _(s):
            s.wait_ge(agc.h, agc.n)
            sq.start(sagsl[:, :], ag_sl, sem=agl)

        @block.vector
        def _(v):
            ve.wait(agl, agl.n)
            ve.memset(sSin[:, :], 0.0)
        v_sin0 = ve.n

        R = {}
        for j in range(NCORES):
            @block.tensor
            def _(t, j=j):
                if j == 0:
                    pe.wait(ve, v_sin0)
                else:
                    pe.wait(ve, R[(j - 1, "acc")])
                i = pe.matmul(pbcC[0:1, 0:64], sbm[:, j:j + 1], sagsl[:, :],
                              start=True, stop=True)
                R[(j, "log")] = pe.done(i)

            @block.vector
            def _(v, j=j):
                ve.wait(pe, R[(j, "log")])
                ve.tensor_copy(srow[0:1, :], pbcC[0:1, 0:64])
            R[(j, "vrow")] = ve.n

            @block.scalar
            def _(s, j=j):
                se.wait(ve, R[(j, "vrow")])
                se.activation(srow[0:1, :], srow[0:1, :], AF.Exp)
            R[(j, "sexp")] = se.n

            @block.vector
            def _(v, j=j):
                ve.wait(se, R[(j, "sexp")])
                ve.tensor_scalar_mul(srow[0:1, :], srow[0:1, :], spm[0:1, j:j + 1])
            R[(j, "vpm")] = ve.n

            @block.tensor
            def _(t, j=j):
                pe.wait(ve, R[(j, "vpm")])
                i = pe.matmul(pbcC[:, 64:128], sonec[:, :], srow[0:1, :], start=True, stop=True)
                R[(j, "bc")] = pe.done(i)

            @block.vector
            def _(v, j=j):
                ve.wait(pe, R[(j, "bc")])
                ve.tensor_copy(scexp[:, :], pbcC[:, 64:128])
            R[(j, "cexp")] = ve.n

            @block.sync
            def _(s, j=j):
                sq.wait(ve, R[(j, "cexp")])
                sq.start(sagS, ag_S[j * 128:(j + 1) * 128, :], sem=ags)
            R[(j, "lds")] = ags.n

            @block.vector
            def _(v, j=j):
                ve.wait(ags, R[(j, "lds")])
                ve.tensor_tensor(sagS.rearrange("p (h q) -> p h q", h=64),
                                 sagS.rearrange("p (h q) -> p h q", h=64),
                                 scexp[:, :].unsqueeze(2).broadcast_to((128, 64, 64)),
                                 ALU.mult)
                ve.tensor_add(sSin[:, :], sSin[:, :], sagS)
            R[(j, "acc")] = ve.n

        @block.scalar
        def _(s):
            se.wait(ve, R[(NCORES - 1, "acc")])
            se.copy(sSinb, sSin[:, :])
        s_sinb = se.n

        @block.vector
        def _(v):
            ve.wait(se, s_sinb)
            ve.memset(scoef[:, :], 1.0)
        v_coef0 = ve.n

        P = {}
        for c in range(NCH):
            @block.tensor
            def _(t, c=c):
                if c == 0:
                    pe.wait(ve, v_coef0)
                else:
                    pe.wait(ve, P[(c - 1, "coef")])
                    pe.wait(se, P[(c - 1, "dec")])
                i = pe.matmul(pbcC[:, 0:64], sonec[:, :],
                              sculc[:, c * 64:(c + 1) * 64], start=True, stop=True)
                P[(c, "bc")] = pe.done(i)

            @block.scalar
            def _(s, c=c):
                se.wait(pe, P[(c, "bc")])
                se.activation(sdec[:, :], pbcC[:, 0:64], AF.Exp)
                se.activation(seA[:, :], scuma[:, c * 64:(c + 1) * 64], AF.Exp)
            P[(c, "dec")] = se.n

            @block.vector
            def _(v, c=c):
                ve.wait(se, P[(c, "dec")])
                ve.tensor_mul(sfac[:, :], seA[:, :], scoef[:, :])
            P[(c, "fac")] = ve.n

            for hg in range(8):
                idx = c * 8 + hg
                par = idx % 2

                @block.tensor
                def _(t, c=c, hg=hg):
                    if hg == 0:
                        pe.wait(ve, P[(c, "fac")])
                    prev = P.get((c, hg - 1, "vy") if hg else (c - 1, 7, "vy"))
                    if prev:
                        pe.wait(ve, prev)
                    i = pe.matmul(pYo[:, :], sC_fm[:, c * 128:(c + 1) * 128],
                                  sSinb[:, hg * 512:(hg + 1) * 512], start=True, stop=True)
                    P[(c, hg, "mm")] = pe.done(i)

                @block.vector
                def _(v, c=c, hg=hg, par=par, idx=idx):
                    ve.wait(pe, P[(c, hg, "mm")])
                    if idx >= 2:
                        ve.wait(sty2[par], sty2[par].base + 16 * (idx // 2))
                    ve.tensor_tensor(sy[:, par * 512:(par + 1) * 512].rearrange("p (h q) -> p h q", h=8),
                                     pYo[:, :].rearrange("p (h q) -> p h q", h=8),
                                     sfac[:, hg * 8:(hg + 1) * 8].unsqueeze(2)
                                     .broadcast_to((128, 8, 64)), ALU.mult)
                P[(c, hg, "vy")] = ve.n

                @block.gpsimd
                def _(g, c=c, hg=hg, par=par):
                    ge.wait(ve, P[(c, hg, "vy")])
                    gq.start(y_o[c * 128:(c + 1) * 128, hg * 512:(hg + 1) * 512],
                             sy[:, par * 512:(par + 1) * 512], sem=sty2[par],
                             accum_op=ALU.add)

            @block.vector
            def _(v, c=c):
                ve.wait(se, P[(c, "dec")])
                ve.tensor_mul(scoef[:, :], scoef[:, :], sdec[:, :])
            P[(c, "coef")] = ve.n

        @block.sync
        def _(s):
            for sem in (styp[0], styp[1], stS, ldc, agl, ags):
                s.wait_ge(sem.h, sem.n)

        @block.gpsimd
        def _(g):
            for sem in (sty2[0], sty2[1], agc, ldxb[0], ldxb[1]):
                g.wait_ge(sem.h, sem.n)



def host_ssd(xc_all, bf_all, cf_all, dtv_all, cuma_all, Dvec):
    tril = (np.arange(128)[None, :] >= np.arange(128)[:, None]).astype(np.float32)
    onec = np.ones((1, 128), np.float32)
    drow = _f32(Dvec.reshape(1, 64))
    idnf = np.eye(128, dtype=np.float32)
    in_maps = []
    for c in range(NCORES):
        p = c % 4
        b0 = (c // 4) * 4
        bm = np.zeros((8, 8), np.float32)
        pm = np.zeros((1, 8), np.float32)
        for s in range(8):
            if b0 <= s < b0 + p:
                pm[0, s] = 1.0
                for k in range(s + 1, b0 + p):
                    bm[k, s] = 1.0
        in_maps.append({
            "xc": _bf(xc_all[c]),
            "Bf": _bf(bf_all[c]),
            "Cf": _bf(cf_all[c]),
            "dtv": _f32(dtv_all[c]),
            "cumA": _f32(cuma_all[c]),
            "idnf": idnf,
            "drow": drow,
            "trilm": tril, "onecol": onec,
            "bmask": bm, "pmask": pm,
        })
    return in_maps


def phase_mgate(P):
    nc = P.nc
    y_pm = P.dio("y_pm", [TC, D_INNER], BF16, "in")
    zs = P.dio("zs", [D_INNER, TC], BF16, "in")
    u_fm = P.dio("u_fm", [D, TC], F32, "in")
    woT = P.dio("woT", [D_INNER, D], F32, "in")
    normw = P.dio("normw", [128, 32], F32, "in")
    ones_bf = P.dio("ones_bf", [128, 128], BF16, "in")
    x1_o = P.dio("x1_fm", [D, TC], F32, "out")

    NG = 4
    GWD = 512
    NK = 32
    NO = 16

    with contextlib.ExitStack() as ctx:
        sg = ctx.enter_context(nc.sbuf_tensor(P.nm("sg"), [128, NK * GWD], BF16))
        sy = ctx.enter_context(nc.sbuf_tensor(P.nm("syb"), [128, 2 * GWD], BF16))
        sz = ctx.enter_context(nc.sbuf_tensor(P.nm("szb"), [128, 2 * GWD], BF16))
        sg2 = ctx.enter_context(nc.sbuf_tensor(P.nm("sg2"), [128, 2 * GWD], BF16))
        srms = ctx.enter_context(nc.sbuf_tensor(P.nm("srms"), [128, GWD], F32))
        snw = ctx.enter_context(nc.sbuf_tensor(P.nm("snw"), [128, 32], F32))
        sones = ctx.enter_context(nc.sbuf_tensor(P.nm("sones"), [128, 128], BF16))
        sw = ctx.enter_context(nc.sbuf_tensor(P.nm("sw"), [128, 2 * NK * 128], BF16))
        su = ctx.enter_context(nc.sbuf_tensor(P.nm("su"), [128, 2 * GWD], F32))
        sx1 = ctx.enter_context(nc.sbuf_tensor(P.nm("sx1"), [128, 2 * GWD], F32))

        pS = ctx.enter_context(nc.psum_tensor(P.nm("pS"), [128, 512], F32))
        pmm = ctx.enter_context(nc.psum_tensor(P.nm("pmm"), [128, 2 * 512], F32))

        E = P.engines()
        block = P.block
        ve, se, pe, gq, sq = E.ve, E.se, E.pe, E.gq, E.sq
        ldc = P.sem("ldc")
        ldyz = [P.sem("ldyz0"), P.sem("ldyz1")]
        ldw = [P.sem("ldw0"), P.sem("ldw1")]
        ldu = [P.sem("ldu0"), P.sem("ldu1")]
        stx = [P.sem("stx0"), P.sem("stx1")]

        @block.sync
        def _(s):
            sq.start(snw[:, :], normw, sem=ldc)
            sq.start(sones[:, :], ones_bf, sem=ldc)
        ldc_done = ldc.n

        K = {}
        nload = [0, 0]
        nwload = [0, 0]
        nuload = [0, 0]
        nxst = [0, 0]
        for grp in range(NG):
            go = grp * GWD
            for k in range(NK):
                pp = k % 2

                @block.sync
                def _(s, grp=grp, go=go, k=k, pp=pp):
                    if K.get(("g", grp, k - 2)) is not None:
                        sq.wait(ve, K[("g2", grp, k - 2)])
                    sq.transpose(sy[:, pp * GWD:(pp + 1) * GWD],
                                 y_pm[go:go + GWD, k * 128:(k + 1) * 128], sem=ldyz[pp])
                    sq.start(sz[:, pp * GWD:(pp + 1) * GWD],
                             zs[k * 128:(k + 1) * 128, go:go + GWD], sem=ldyz[pp])
                    nload[pp] += 32

                @block.vector
                def _(v, grp=grp, k=k, pp=pp):
                    ve.wait(ldyz[pp], ldyz[pp].base + 32 * ((grp * NK + k) // 2 + 1))
                    ve.tensor_mul(sg[:, k * GWD:(k + 1) * GWD],
                                  sy[:, pp * GWD:(pp + 1) * GWD],
                                  sz[:, pp * GWD:(pp + 1) * GWD])
                    K[("g", grp, k)] = ve.n
                    if k >= 2:
                        ve.wait(pe, K[("st", grp, k - 2)])
                    ve.tensor_mul(sg2[:, pp * GWD:(pp + 1) * GWD],
                                  sg[:, k * GWD:(k + 1) * GWD],
                                  sg[:, k * GWD:(k + 1) * GWD])
                    K[("g2", grp, k)] = ve.n

                @block.tensor
                def _(t, grp=grp, k=k, pp=pp):
                    pe.wait(ldc, ldc_done)
                    pe.wait(ve, K[("g2", grp, k)])
                    if grp > 0 and k == 0:
                        pe.wait(se, K[("rms", grp - 1)])
                    i = pe.matmul(pS[:, :], sones[:], sg2[:, pp * GWD:(pp + 1) * GWD],
                                  start=(k == 0), stop=(k == NK - 1))
                    K[("st", grp, k)] = pe.done(i)

            @block.vector
            def _(v, grp=grp):
                ve.wait(pe, K[("st", grp, NK - 1)])
                ve.tensor_scalar(srms[:, :], pS[:, :], 1.0 / D_INNER, LN_EPS,
                                 ALU.mult, ALU.add)
            K[("vrms", grp)] = ve.n

            @block.scalar
            def _(s, grp=grp):
                se.wait(ve, K[("vrms", grp)])
                se.activation(srms[:, :], srms[:, :], AF.Ln)
                se.activation(srms[:, :], srms[:, :], AF.Exp, scale=-0.5)
            K[("rms", grp)] = se.n

            for k in range(NK):
                @block.vector
                def _(v, grp=grp, k=k):
                    if k == 0:
                        ve.wait(se, K[("rms", grp)])
                    sl = sg[:, k * GWD:(k + 1) * GWD]
                    ve.tensor_scalar_mul(sl, sl, snw[:, k:k + 1])
                    ve.tensor_mul(sl, sl, srms[:, :])
                K[("gn", grp)] = ve.n

            for ot in range(NO):
                pp = ot % 2
                gi = grp * NO + ot

                @block.gpsimd
                def _(g, grp=grp, ot=ot, pp=pp, gi=gi):
                    if gi >= 2:
                        gq.wait(pe, K[("mm", gi - 2)])
                    gq.start(sw[:, pp * NK * 128:(pp + 1) * NK * 128],
                             woT[:, ot * 128:(ot + 1) * 128]
                             .rearrange("(k p) e -> p k e", p=128), sem=ldw[pp])
                    nwload[pp] += 16

                @block.sync
                def _(s, grp=grp, go=go, ot=ot, pp=pp, gi=gi):
                    if gi >= 2:
                        sq.wait(ve, K[("x1", gi - 2)])
                    sq.start(su[:, pp * GWD:(pp + 1) * GWD],
                             u_fm[ot * 128:(ot + 1) * 128, go:go + GWD], sem=ldu[pp])
                    nuload[pp] += 16

                @block.tensor
                def _(t, grp=grp, ot=ot, pp=pp, gi=gi):
                    pe.wait(ldw[pp], ldw[pp].base + 16 * (gi // 2 + 1))
                    if ot == 0:
                        pe.wait(ve, K[("gn", grp)])
                    if gi >= 2:
                        pe.wait(ve, K[("x1", gi - 2)])
                    for k in range(NK):
                        i = pe.matmul(pmm[:, pp * 512:pp * 512 + GWD],
                                      sw[:, pp * NK * 128 + k * 128:pp * NK * 128 + (k + 1) * 128],
                                      sg[:, k * GWD:(k + 1) * GWD],
                                      start=(k == 0), stop=(k == NK - 1))
                    K[("mm", gi)] = pe.done(i)

                @block.vector
                def _(v, grp=grp, ot=ot, pp=pp, gi=gi):
                    ve.wait(pe, K[("mm", gi)])
                    ve.wait(ldu[pp], ldu[pp].base + 16 * (gi // 2 + 1))
                    if gi >= 2:
                        ve.wait(stx[pp], stx[pp].base + 16 * (gi // 2))
                    ve.tensor_add(sx1[:, pp * GWD:(pp + 1) * GWD],
                                  pmm[:, pp * 512:pp * 512 + GWD],
                                  su[:, pp * GWD:(pp + 1) * GWD])
                K[("x1", gi)] = ve.n

                @block.sync
                def _(s, grp=grp, go=go, ot=ot, pp=pp, gi=gi):
                    sq.wait(ve, K[("x1", gi)])
                    sq.start(x1_o[ot * 128:(ot + 1) * 128, go:go + GWD],
                             sx1[:, pp * GWD:(pp + 1) * GWD], sem=stx[pp])
                    nxst[pp] += 16

        @block.sync
        def _(s):
            for sem in (stx[0], stx[1], ldc, ldu[0], ldu[1]):
                s.wait_ge(sem.h, sem.n)

        @block.gpsimd
        def _(g):
            for sem in (ldw[0], ldw[1]):
                g.wait_ge(sem.h, sem.n)



def host_mgate(y_all, zs_all, u_flat, out_proj_w, norm_w):
    woT = _f32(out_proj_w.T)
    nw = _f32(norm_w.reshape(32, 128).T)
    ones_bf = np.ones((128, 128), dtype=BF)
    in_maps = []
    for c in range(NCORES):
        s0 = c * TC
        in_maps.append({
            "y_pm": _bf(y_all[c]),
            "zs": zs_all[c] if zs_all[c].dtype == BF else _bf(zs_all[c]),
            "u_fm": _f32(u_flat[s0:s0 + TC].T),
            "woT": woT,
            "normw": nw,
            "ones_bf": ones_bf,
        })
    return in_maps

EMOE = 64
CAPM = 1024
TOPK = 6
ROWS_E = CAPM + 1


def phase_moe_route(P):
    nc = P.nc
    x1 = P.dio("x1", [D, TC], F32, "in")
    wg = P.dio("wg", [D, EMOE], F32, "in")
    wsu = P.dio("wsu", [D, 1024], F32, "in")
    wsd = P.dio("wsd", [128, 64 * 128], F32, "in")
    iot = P.dio("iot", [128, EMOE], F32, "in")
    trils = P.dio("trils", [128, 128], F32, "in")
    onek1 = P.dio("onek1", [1, 128], F32, "in")
    onecl = P.dio("onecl", [128, 1], F32, "in")
    rankm = P.dio("rankm", [8, 1], F32, "in")
    tokbase = P.dio("tokbase", [128, 1], F32, "in")
    ident_bf = P.dio("ident_bf", [128, 128], BF16, "in")
    ones_bf = P.dio("ones_bf", [128, 128], BF16, "in")

    xnf_o = P.dio("xnf", [D, TC], BF16, "out")
    xnt_o = P.dio("xnt", [TC, D], BF16, "out")
    sht_o = P.dio("sht", [TC, D], BF16, "out")
    slot_o = P.dio("slot", [TC, TOPK], F32, "out")
    wv_o = P.dio("wv", [TC, TOPK], F32, "out")
    cnt_o = P.dio("cnt", [1, EMOE], F32, "out")
    tbl_o = P.dio("tbl", [EMOE * ROWS_E, 1], F32, "out")

    cc_cnt = P.dio("cc_cnt", [1, EMOE], F32, "int")
    ag_cnt = P.dio("ag_cnt", [NCORES, EMOE], F32, "int")

    NG = 4
    GWD = 512
    NT = 16

    with contextlib.ExitStack() as ctx:
        sx = ctx.enter_context(nc.sbuf_tensor(P.nm("sx"), [128, 16 * GWD], F32))
        sxn = ctx.enter_context(nc.sbuf_tensor(P.nm("sxn"), [128, 16 * TC], BF16))
        sx2 = ctx.enter_context(nc.sbuf_tensor(P.nm("sx2"), [128, 2 * GWD], BF16))
        sm = ctx.enter_context(nc.sbuf_tensor(P.nm("smn"), [128, GWD], F32))
        srs = ctx.enter_context(nc.sbuf_tensor(P.nm("srs"), [128, GWD], F32))
        svar = ctx.enter_context(nc.sbuf_tensor(P.nm("svar"), [128, GWD], F32))
        swg = ctx.enter_context(nc.sbuf_tensor(P.nm("swg"), [128, 16 * EMOE], BF16))
        swgf = ctx.enter_context(nc.sbuf_tensor(P.nm("swgf"), [128, 16 * EMOE], F32))
        swsu = ctx.enter_context(nc.sbuf_tensor(P.nm("swsu"), [128, 2 * 16 * 128], BF16))
        swsd2 = ctx.enter_context(nc.sbuf_tensor(P.nm("swsd2"), [128, 64 * 128], BF16))
        ssc = ctx.enter_context(nc.sbuf_tensor(P.nm("ssc"), [128, NT * EMOE], F32))
        smax = ctx.enter_context(nc.sbuf_tensor(P.nm("smax"), [128, 8], F32))
        srem = ctx.enter_context(nc.sbuf_tensor(P.nm("srem"), [128, EMOE], F32))
        smask = ctx.enter_context(nc.sbuf_tensor(P.nm("smask"), [128, NT * EMOE], F32))
        spos = ctx.enter_context(nc.sbuf_tensor(P.nm("spos"), [128, NT * EMOE], F32))
        siota = ctx.enter_context(nc.sbuf_tensor(P.nm("siota"), [128, EMOE], F32))
        stril = ctx.enter_context(nc.sbuf_tensor(P.nm("strils"), [128, 128], F32))
        sone1 = ctx.enter_context(nc.sbuf_tensor(P.nm("sone1"), [1, 128], F32))
        sonec = ctx.enter_context(nc.sbuf_tensor(P.nm("sonecl"), [128, 1], F32))
        srkm = ctx.enter_context(nc.sbuf_tensor(P.nm("srkm"), [8, 1], F32))
        stkb = ctx.enter_context(nc.sbuf_tensor(P.nm("stkb"), [128, 1], F32))
        soff = ctx.enter_context(nc.sbuf_tensor(P.nm("soff"), [1, EMOE], F32))
        sgoff = ctx.enter_context(nc.sbuf_tensor(P.nm("sgoff"), [128, EMOE], F32))
        szer = ctx.enter_context(nc.sbuf_tensor(P.nm("szer"), [128, EMOE], F32))
        scum = ctx.enter_context(nc.sbuf_tensor(P.nm("scum"), [128, EMOE], F32))
        soh = ctx.enter_context(nc.sbuf_tensor(P.nm("soh"), [128, EMOE], F32))
        sd1 = ctx.enter_context(nc.sbuf_tensor(P.nm("sd1"), [128, EMOE], F32))
        scol = ctx.enter_context(nc.sbuf_tensor(P.nm("scol"), [128, 16], F32))
        sslot = ctx.enter_context(nc.sbuf_tensor(P.nm("sslot"), [128, TOPK], F32))
        swv = ctx.enter_context(nc.sbuf_tensor(P.nm("swv"), [128, TOPK], F32))
        svald = ctx.enter_context(nc.sbuf_tensor(P.nm("svald"), [128, TOPK], F32))
        sslo32 = ctx.enter_context(nc.sbuf_tensor(P.nm("sslo32"), [128, 1], I32))
        stok = ctx.enter_context(nc.sbuf_tensor(P.nm("stok"), [128, 1], F32))
        sidn = ctx.enter_context(nc.sbuf_tensor(P.nm("sidnb"), [128, 128], BF16))
        sonesb = ctx.enter_context(nc.sbuf_tensor(P.nm("sonesb"), [128, 128], BF16))
        sonef = ctx.enter_context(nc.sbuf_tensor(P.nm("sonef"), [128, 128], F32))
        ssu = ctx.enter_context(nc.sbuf_tensor(P.nm("ssu"), [128, 8 * GWD], BF16))
        sshf = ctx.enter_context(nc.sbuf_tensor(P.nm("sshf"), [128, 16 * GWD], BF16))
        sstg = ctx.enter_context(nc.sbuf_tensor(P.nm("sstg"), [128, 2 * 256], BF16))

        pS1 = ctx.enter_context(nc.psum_tensor(P.nm("pS1"), [128, 512], F32))
        pS2 = ctx.enter_context(nc.psum_tensor(P.nm("pS2"), [128, 512], F32))
        psc = ctx.enter_context(nc.psum_tensor(P.nm("psc"), [128, 512], F32))
        ppos = ctx.enter_context(nc.psum_tensor(P.nm("ppos"), [128, 512], F32))
        pmm = ctx.enter_context(nc.psum_tensor(P.nm("pmm"), [128, 2 * 512], F32))
        ptr = ctx.enter_context(nc.psum_tensor(P.nm("ptr"), [128, 512], BF16))

        E = P.engines()
        block = P.block
        ve, se, pe, gq, sq = E.ve, E.se, E.pe, E.gq, E.sq
        ldc = P.sem("ldc")
        ldx = P.sem("ldx")
        ldsu = [P.sem("ldsu0"), P.sem("ldsu1")]
        stf = P.sem("stf")
        agc = P.sem("agc")
        agl = P.sem("agl")
        sct = P.sem("sct")
        stsw = P.sem("stsw")
        stt_ = [P.sem("stt0"), P.sem("stt1")]

        @block.sync
        def _(s):
            sq.start(swgf[:, :].rearrange("p (k e) -> p k e", k=16),
                     wg.rearrange("(k p) e -> p k e", p=128), sem=ldc)
            sq.start(siota[:, :], iot, sem=ldc)
            sq.start(stril[:, :], trils, sem=ldc)
            sq.start(sone1[:, :], onek1, sem=ldc)
            sq.start(sonec[:, :], onecl, sem=ldc)
            sq.start(srkm[:, :], rankm, sem=ldc)
            sq.start(stkb[:, :], tokbase, sem=ldc)
            sq.start(sidn[:, :], ident_bf, sem=ldc)
            sq.start(sonesb[:, :], ones_bf, sem=ldc)
            pass
        ldc_done = ldc.n

        @block.gpsimd
        def _(g):
            gq.start(swsd2[:, :], wsd, sem=ldc)

        @block.vector
        def _(v):
            ve.wait(ldc, ldc_done)
            ve.tensor_copy(swg[:, :], swgf[:, :])
            ve.memset(szer[:, :], 0.0)
            ve.memset(soff[:, :], 0.0)
            ve.memset(sonef[:, :], 1.0)
        v_cst = ve.n

        K = {}
        for g in range(NG):
            go = g * GWD

            @block.gpsimd
            def _(gg, g=g, go=go):
                if g >= 1:
                    gq.wait(ve, K[("xn", g - 1)])
                for k in range(16):
                    gq.start(sx[:, k * GWD:(k + 1) * GWD],
                             x1[k * 128:(k + 1) * 128, go:go + GWD], sem=ldx)
            K[("ldx", g)] = ldx.n

            for k in range(16):
                pp = k % 2

                @block.vector
                def _(v, g=g, k=k, pp=pp):
                    if k == 0:
                        ve.wait(ldx, K[("ldx", g)])
                    if k >= 2:
                        ve.wait(pe, K[("st2", g, k - 2)])
                    ve.tensor_mul(sx2[:, pp * GWD:(pp + 1) * GWD],
                                  sx[:, k * GWD:(k + 1) * GWD],
                                  sx[:, k * GWD:(k + 1) * GWD])
                    K[("x2", g, k)] = ve.n

                @block.tensor
                def _(t, g=g, k=k, pp=pp):
                    if g > 0 and k == 0:
                        pe.wait(se, K[("rs", g - 1)])
                    pe.wait(ve, K[("x2", g, k)])
                    if k == 0:
                        pe.wait(ve, v_cst)
                    pe.matmul(pS1[:, :], sonef[:], sx[:, k * GWD:(k + 1) * GWD],
                              start=(k == 0), stop=(k == 15))
                    i = pe.matmul(pS2[:, :], sonesb[:], sx2[:, pp * GWD:(pp + 1) * GWD],
                                  start=(k == 0), stop=(k == 15))
                    K[("st2", g, k)] = pe.done(i)
                    if k == 15:
                        K[("stats", g)] = pe.n

            @block.vector
            def _(v, g=g):
                ve.wait(pe, K[("stats", g)])
                ve.tensor_scalar_mul(sm[:, :], pS1[:, :], 1.0 / D)
                ve.tensor_mul(svar[:, :], sm[:, :], sm[:, :])
                ve.scalar_tensor_tensor(svar[:, :], pS2[:, :], 1.0 / D, svar[:, :],
                                        ALU.mult, ALU.subtract)
                ve.tensor_scalar_add(svar[:, :], svar[:, :], LN_EPS)
            K[("var", g)] = ve.n

            @block.scalar
            def _(s, g=g):
                se.wait(ve, K[("var", g)])
                se.activation(srs[:, :], svar[:, :], AF.Ln)
                se.activation(srs[:, :], srs[:, :], AF.Exp, scale=-0.5)
            K[("rs", g)] = se.n

            @block.vector
            def _(v, g=g, go=go):
                ve.wait(se, K[("rs", g)])
                for k in range(16):
                    sl = sxn[:, k * TC + go:k * TC + go + GWD]
                    ve.tensor_sub(sx[:, k * GWD:(k + 1) * GWD], sx[:, k * GWD:(k + 1) * GWD],
                                  sm[:, :])
                    ve.tensor_mul(sl, sx[:, k * GWD:(k + 1) * GWD], srs[:, :])
            K[("xn", g)] = ve.n

        @block.sync
        def _(s):
            sq.wait(ve, K[("xn", NG - 1)])
            for k in range(16):
                sq.start(xnf_o[k * 128:(k + 1) * 128, :], sxn[:, k * TC:(k + 1) * TC], sem=stf)
        K["xnf_done"] = stf.n

        for tt in range(NT):
            @block.tensor
            def _(t, tt=tt):
                pe.wait(ve, K[("xn", NG - 1)])
                if tt >= 1:
                    pe.wait(se, K[("sig", tt - 1)])
                for k in range(16):
                    i = pe.matmul(psc[:, 0:EMOE], sxn[:, k * TC + tt * 128:k * TC + (tt + 1) * 128],
                                  swg[:, k * EMOE:(k + 1) * EMOE],
                                  start=(k == 0), stop=(k == 15))
                K[("sc", tt)] = pe.done(i)

            @block.scalar
            def _(s, tt=tt):
                se.wait(pe, K[("sc", tt)])
                se.activation(ssc[:, tt * EMOE:(tt + 1) * EMOE], psc[:, 0:EMOE], AF.Sigmoid)
            K[("sig", tt)] = se.n

        for tt in range(NT):
            @block.vector
            def _(v, tt=tt):
                ve.wait(se, K[("sig", tt)])
                ve.wait(ldc, ldc_done)
                sc = ssc[:, tt * EMOE:(tt + 1) * EMOE]
                ve.max(out=smax[:, :], in_=sc)
                ve.memset(smax[:, 6:8], -1e30)
                ve.match_replace(out=srem[:, :], in_to_replace=smax[:, :], in_values=sc,
                                 imm_value=-1e30)
                ve.tensor_tensor(smask[:, tt * EMOE:(tt + 1) * EMOE], sc, srem[:, :],
                                 ALU.not_equal)
            K[("msk", tt)] = ve.n

            @block.tensor
            def _(t, tt=tt):
                pe.wait(ve, K[("msk", tt)])
                if tt > 0:
                    pe.wait(ve, K[("pos", tt - 1)])
                pe.matmul(ppos[:, 0:EMOE], stril[:, :], smask[:, tt * EMOE:(tt + 1) * EMOE],
                          start=True, stop=False)
                i = pe.matmul(ppos[:, 0:EMOE], sone1[:, :], soff[:, :],
                              start=False, stop=True)
                K[("posm", tt)] = pe.done(i)
                i = pe.matmul(ppos[0:1, 256:256 + EMOE], sonec[:, :],
                              smask[:, tt * EMOE:(tt + 1) * EMOE], start=True, stop=True)
                K[("cnt", tt)] = pe.done(i)

            @block.vector
            def _(v, tt=tt):
                ve.wait(pe, K[("cnt", tt)])
                ve.tensor_copy(spos[:, tt * EMOE:(tt + 1) * EMOE], ppos[:, 0:EMOE])
                ve.tensor_add(soff[:, :], soff[:, :], ppos[0:1, 256:256 + EMOE])
            K[("pos", tt)] = ve.n

        @block.sync
        def _(s):
            sq.wait(ve, K[("pos", NT - 1)])
            sq.start(cnt_o, soff[:, :], sem=stf)
            sq.start(cc_cnt, soff[:, :], sem=sct)

        @block.gpsimd
        def _(g):
            g.wait_ge(sct.h, sct.n)
            gq.collective("AllGather", ALU.bypass, [list(range(NCORES))],
                          [cc_cnt.opt()], [ag_cnt.opt()], sem=agc)

        @block.sync
        def _(s):
            s.wait_ge(agc.h, agc.n)
            sq.start(srem[0:8, :], ag_cnt, sem=agl)
        K["agl"] = agl.n

        @block.tensor
        def _(t):
            pe.wait(agl, K["agl"])
            i = pe.matmul(ppos[0:1, 0:EMOE], srkm[:, :], srem[0:8, :], start=True, stop=True)
            pe.done(i)
            K["goffmm"] = pe.n

        @block.vector
        def _(v):
            ve.wait(pe, K["goffmm"])
            ve.tensor_copy(soff[:, :], ppos[0:1, 0:EMOE])
        K["goffrow"] = ve.n

        @block.tensor
        def _(t):
            pe.wait(ve, K["goffrow"])
            i = pe.matmul(ppos[:, 256:256 + EMOE], sone1[:, :], soff[:, :], start=True, stop=True)
            K["goffbc"] = pe.done(i)

        @block.vector
        def _(v):
            ve.wait(pe, K["goffbc"])
            ve.tensor_copy(sgoff[:, :], ppos[:, 256:256 + EMOE])
        K["goff"] = ve.n

        for tt in range(NT):
            @block.vector
            def _(v, tt=tt):
                if tt > 0:
                    ve.wait(sct, K[("sctt", tt - 1, TOPK - 1)])
                    ve.wait(stsw, stsw.base + 32 * tt)
                msk = smask[:, tt * EMOE:(tt + 1) * EMOE]
                sc = ssc[:, tt * EMOE:(tt + 1) * EMOE]
                ve.tensor_tensor_scan(scum[:, :], msk, szer[:, :], 0.0, ALU.add, ALU.add)
                ve.tensor_add(sd1[:, :], spos[:, tt * EMOE:(tt + 1) * EMOE], sgoff[:, :])
                for k in range(TOPK):
                    ve.tensor_scalar(soh[:, :], scum[:, :], float(k + 1), None, ALU.is_equal)
                    ve.tensor_mul(soh[:, :], soh[:, :], msk)
                    ve.tensor_mul(srem[:, :], soh[:, :], siota[:, :])
                    ve.tensor_reduce(scol[:, 0:1], srem[:, :], mybir.AxisListType.X, ALU.add)
                    ve.tensor_mul(srem[:, :], soh[:, :], sd1[:, :])
                    ve.tensor_reduce(scol[:, 1:2], srem[:, :], mybir.AxisListType.X, ALU.add)
                    ve.tensor_mul(srem[:, :], soh[:, :], sc)
                    ve.tensor_reduce(swv[:, k:k + 1], srem[:, :], mybir.AxisListType.X, ALU.add)
                    ve.tensor_scalar(svald[:, k:k + 1], scol[:, 1:2], 1024.0, None, ALU.is_lt)
                    ve.tensor_scalar_min(scol[:, 1:2], scol[:, 1:2], 1024.0)
                    ve.scalar_tensor_tensor(sslot[:, k:k + 1], scol[:, 0:1], 1025.0,
                                            scol[:, 1:2], ALU.mult, ALU.add)
                ve.tensor_reduce(scol[:, 2:3], swv[:, :], mybir.AxisListType.X, ALU.add)
                ve.reciprocal(scol[:, 3:4], scol[:, 2:3])
                ve.tensor_scalar_mul(swv[:, :], swv[:, :], scol[:, 3:4])
                ve.tensor_mul(swv[:, :], swv[:, :], svald[:, :])
                ve.tensor_scalar_add(stok[:, :], stkb[:, :], float(tt * 128))
            K[("ext", tt)] = ve.n

            @block.sync
            def _(s, tt=tt):
                sq.wait(ve, K[("ext", tt)])
                sq.start(slot_o[tt * 128:(tt + 1) * 128, :], sslot[:, :], sem=stsw)
                sq.start(wv_o[tt * 128:(tt + 1) * 128, :], swv[:, :], sem=stsw)
            K[("exts", tt)] = stsw.n

            for k in range(TOPK):
                @block.vector
                def _(v, tt=tt, k=k):
                    if (tt, k) != (0, 0):
                        gq_prev = K.get(("sctt", tt, k - 1) if k else ("sctt", tt - 1, TOPK - 1))
                        ve.wait(sct, gq_prev)
                    ve.tensor_copy(sslo32[:, :], sslot[:, k:k + 1])
                K[("i32", tt, k)] = ve.n

                @block.gpsimd
                def _(g, tt=tt, k=k):
                    gq.wait(ve, K[("i32", tt, k)])
                    gq.indirect(out=tbl_o, out_offset=IndirectOffsetOnAxis(sslo32[:, :], axis=0),
                                in_=stok[:, :], in_offset=None, sem=sct)
                K[("sctt", tt, k)] = sct.n

        for grp in range(NG):
            go = grp * GWD
            for mt in range(8):
                pp = mt % 2
                gi = grp * 8 + mt

                @block.gpsimd
                def _(g, grp=grp, mt=mt, pp=pp, gi=gi):
                    if gi >= 2:
                        gq.wait(pe, K[("su", gi - 2)])
                    gq.start(swsu[:, pp * 16 * 128:(pp + 1) * 16 * 128],
                             wsu[:, mt * 128:(mt + 1) * 128].rearrange("(k p) m -> p k m", p=128),
                             sem=ldsu[pp])
                K[("ldsu", gi)] = ldsu[pp].base + 16 * (gi // 2 + 1)

                @block.tensor
                def _(t, grp=grp, go=go, mt=mt, pp=pp, gi=gi):
                    pe.wait(ldsu[pp], K[("ldsu", gi)])
                    if gi >= 2:
                        pe.wait(ve, K[("hh", gi - 2)] if (mt % 2 == 0) else K[("hh", gi - 2)])
                    for k in range(16):
                        i = pe.matmul(pmm[:, pp * 512:pp * 512 + GWD],
                                      swsu[:, pp * 16 * 128 + k * 128:pp * 16 * 128 + (k + 1) * 128],
                                      sxn[:, k * TC + go:k * TC + go + GWD],
                                      start=(k == 0), stop=(k == 15))
                    K[("su", gi)] = pe.done(i)

                @block.scalar
                def _(s, grp=grp, mt=mt, pp=pp, gi=gi):
                    se.wait(pe, K[("su", gi)])
                    if mt < 4:
                        se.copy(ssu[:, mt * GWD:(mt + 1) * GWD], pmm[:, pp * 512:pp * 512 + GWD])
                    else:
                        se.activation(ssu[:, (mt % 4 + 4) * GWD:(mt % 4 + 5) * GWD],
                                      pmm[:, pp * 512:pp * 512 + GWD], AF.Silu)
                K[("hh", gi)] = se.n

            @block.vector
            def _(v, grp=grp):
                ve.wait(se, K[("hh", grp * 8 + 7)])
                for j in range(4):
                    ve.tensor_mul(ssu[:, j * GWD:(j + 1) * GWD],
                                  ssu[:, j * GWD:(j + 1) * GWD],
                                  ssu[:, (j + 4) * GWD:(j + 5) * GWD])
            K[("hmul", grp)] = ve.n

            for ot in range(16):
                @block.tensor
                def _(t, grp=grp, ot=ot):
                    pe.wait(ve, K[("hmul", grp)])
                    if ot >= 1:
                        pe.wait(se, K[("shev", grp, ot - 1)])
                    for k in range(4):
                        i = pe.matmul(pmm[:, 0:GWD],
                                      swsd2[:, (k * 16 + ot) * 128:(k * 16 + ot + 1) * 128],
                                      ssu[:, k * GWD:(k + 1) * GWD],
                                      start=(k == 0), stop=(k == 3))
                    K[("shmm", grp, ot)] = pe.done(i)

                @block.scalar
                def _(s, grp=grp, ot=ot):
                    se.wait(pe, K[("shmm", grp, ot)])
                    se.copy(sshf[:, ot * GWD:(ot + 1) * GWD], pmm[:, 0:GWD])
                K[("shev", grp, ot)] = se.n

            for sub in range(4):
                tt = grp * 4 + sub
                for dt_ in range(16):
                    idx = (tt * 16 + dt_)
                    par = idx % 2

                    @block.tensor
                    def _(t, grp=grp, go=go, sub=sub, tt=tt, dt_=dt_, par=par, idx=idx):
                        if dt_ == 0 and sub == 0:
                            pe.wait(se, K[("shev", grp, 15)])
                        prev = K.get(("trev", idx - 1))
                        if prev:
                            pe.wait(ve, prev)
                        i = pe.transpose(ptr[:, 0:128], sxn[:, dt_ * TC + tt * 128:dt_ * TC + (tt + 1) * 128], sidn[:, :])
                        pe.done(i)
                        i = pe.transpose(ptr[:, 128:256], sshf[:, dt_ * GWD + sub * 128:dt_ * GWD + (sub + 1) * 128], sidn[:, :])
                        K[("tr", idx)] = pe.done(i)

                    @block.vector
                    def _(v, tt=tt, dt_=dt_, par=par, idx=idx):
                        ve.wait(pe, K[("tr", idx)])
                        if idx >= 2:
                            ve.wait(stt_[par], stt_[par].base + 32 * (idx // 2))
                        ve.tensor_copy(sstg[:, par * 256:par * 256 + 128], ptr[:, 0:128])
                        ve.tensor_copy(sstg[:, par * 256 + 128:par * 256 + 256], ptr[:, 128:256])
                    K[("trev", idx)] = ve.n

                    @block.sync
                    def _(s, tt=tt, dt_=dt_, par=par, idx=idx):
                        sq.wait(ve, K[("trev", idx)])
                        sq.start(xnt_o[tt * 128:(tt + 1) * 128, dt_ * 128:(dt_ + 1) * 128],
                                 sstg[:, par * 256:par * 256 + 128], sem=stt_[par])
                        sq.start(sht_o[tt * 128:(tt + 1) * 128, dt_ * 128:(dt_ + 1) * 128],
                                 sstg[:, par * 256 + 128:par * 256 + 256], sem=stt_[par])

        @block.sync
        def _(s):
            for sem in (stf, stt_[0], stt_[1], ldc, agl):
                s.wait_ge(sem.h, sem.n)

        @block.gpsimd
        def _(g):
            for sem in (sct, agc, ldx, ldsu[0], ldsu[1]):
                g.wait_ge(sem.h, sem.n)



def host_moe_route(x1_all, w_gate, w_shared_up, w_shared_down):
    wsd_t = np.zeros((128, 64 * 128), np.float32)
    for k in range(4):
        for mt in range(16):
            wsd_t[:, (k * 16 + mt) * 128:(k * 16 + mt + 1) * 128] = \
                w_shared_down[k * 128:(k + 1) * 128, mt * 128:(mt + 1) * 128]
    iot = np.broadcast_to(np.arange(EMOE, dtype=np.float32), (128, EMOE)).copy()
    trils = (np.arange(128)[None, :] < np.arange(128)[:, None]).astype(np.float32)
    onek1 = np.ones((1, 128), np.float32)
    onecl = np.ones((128, 1), np.float32)
    ident_bf = np.eye(128, dtype=np.float32).astype(BF)
    ones_bf = np.ones((128, 128), dtype=BF)
    in_maps = []
    for c in range(NCORES):
        rk = np.zeros((8, 1), np.float32)
        rk[:c] = 1.0
        tb = (np.arange(128, dtype=np.float32) + c * TC).reshape(128, 1)
        in_maps.append({
            "x1": _f32(x1_all[c]),
            "wg": _f32(w_gate),
            "wsu": _f32(w_shared_up),
            "wsd": wsd_t,
            "iot": iot, "trils": trils, "onek1": onek1, "onecl": onecl,
            "rankm": rk, "tokbase": tb,
            "ident_bf": ident_bf, "ones_bf": ones_bf,
        })
    return in_maps



def phase_moe_expert(P):
    nc = P.nc
    xg = P.dio("xg", [NTOK, D], BF16, "in")
    tbls = P.dio("tbls", [NCORES, EMOE * ROWS_E], F32, "in")
    sel64 = P.dio("sel64", [64, 8], F32, "in")
    cntT = P.dio("cntT", [8, 8], F32, "in")
    wup = P.dio("wup", [8 * D, 512], F32, "in")
    wdn = P.dio("wdn", [8 * 256, D], F32, "in")
    islot = P.dio("islot", [8, ROWS_E], F32, "in")
    idn = P.dio("idn", [128, 128], F32, "in")

    y_o = P.dio("Yout", [8 * ROWS_E, D], BF16, "out")
    xe_d = P.dio("xe_d", [8 * CAPM, D], BF16, "int")
    ye_d = P.dio("ye_d", [D, CAPM], BF16, "int")

    with contextlib.ExitStack() as ctx:
        stbl = ctx.enter_context(nc.sbuf_tensor(P.nm("stbl"), [8, ROWS_E], F32))
        stbl64 = ctx.enter_context(nc.sbuf_tensor(P.nm("stbl64"), [64, ROWS_E], F32))
        ssel = ctx.enter_context(nc.sbuf_tensor(P.nm("sselE"), [64, 8], F32))
        smrg = ctx.enter_context(nc.sbuf_tensor(P.nm("smrg"), [8, ROWS_E], F32))
        scnt = ctx.enter_context(nc.sbuf_tensor(P.nm("scnt"), [8, 8], F32))
        sgo = ctx.enter_context(nc.sbuf_tensor(P.nm("sgo"), [8, 8], F32))
        send = ctx.enter_context(nc.sbuf_tensor(P.nm("send"), [8, 8], F32))
        szer8 = ctx.enter_context(nc.sbuf_tensor(P.nm("szer8"), [8, 8], F32))
        sio = ctx.enter_context(nc.sbuf_tensor(P.nm("sio"), [8, ROWS_E], F32))
        si1 = ctx.enter_context(nc.sbuf_tensor(P.nm("si1"), [8, ROWS_E], F32))
        si2 = ctx.enter_context(nc.sbuf_tensor(P.nm("si2"), [8, ROWS_E], F32))
        sidn = ctx.enter_context(nc.sbuf_tensor(P.nm("sidn"), [128, 128], F32))
        stc = ctx.enter_context(nc.sbuf_tensor(P.nm("stc"), [128, 64], F32))
        sti = ctx.enter_context(nc.sbuf_tensor(P.nm("sti"), [128, 64], I32))
        sxe = ctx.enter_context(nc.sbuf_tensor(P.nm("sxe"), [128, 2 * 16 * 1024], BF16))
        swu = ctx.enter_context(nc.sbuf_tensor(P.nm("swu"), [128, 2 * 64 * 128], BF16))
        swd = ctx.enter_context(nc.sbuf_tensor(P.nm("swd"), [128, 2 * 32 * 128], BF16))
        sh = ctx.enter_context(nc.sbuf_tensor(P.nm("sh"), [128, 4 * 512], BF16))
        sg_ = ctx.enter_context(nc.sbuf_tensor(P.nm("sg_"), [128, 1024], BF16))
        syst = ctx.enter_context(nc.sbuf_tensor(P.nm("syst"), [128, 2 * 512], BF16))
        sytm = ctx.enter_context(nc.sbuf_tensor(P.nm("sytm"), [128, 2 * D], BF16))
        sgst = ctx.enter_context(nc.sbuf_tensor(P.nm("sgst"), [128, 2 * D], BF16))
        szero = ctx.enter_context(nc.sbuf_tensor(P.nm("szero"), [1, D], BF16))

        pmm = ctx.enter_context(nc.psum_tensor(P.nm("pmm"), [128, 4 * 512], F32))
        ptr = ctx.enter_context(nc.psum_tensor(P.nm("ptr"), [128, 512], F32))

        E = P.engines()
        block = P.block
        ve, se, pe, gq, sq = E.ve, E.se, E.pe, E.gq, E.sq
        ldc = P.sem("ldc")
        ldt = P.sem("ldt")
        gat = P.sem("gat")
        ldxe = [P.sem("ldxe0"), P.sem("ldxe1")]
        ldwu = [P.sem("ldwu0"), P.sem("ldwu1")]
        ldwd = [P.sem("ldwd0"), P.sem("ldwd1")]
        stye = P.sem("stye")
        ldyt = P.sem("ldyt")
        sty = [P.sem("sty0"), P.sem("sty1")]
        sttr = P.sem("sttr")

        @block.sync
        def _(s):
            sq.start(scnt[:, :], cntT, sem=ldc)
            sq.start(sio[:, :], islot, sem=ldc)
            sq.start(sidn[:, :], idn, sem=ldc)
            sq.start(ssel[:, :], sel64, sem=ldc)
        ldc_done = ldc.n

        @block.vector
        def _(v):
            ve.wait(ldc, ldc_done)
            ve.memset(szer8[:, :], 0.0)
            ve.memset(szero[:, :], 0.0)
            ve.tensor_tensor_scan(sgo[:, :], scnt[:, :], szer8[:, :], 0.0, ALU.add, ALU.add)
            ve.tensor_copy(send[:, 1:8], sgo[:, 0:7])
            ve.memset(send[:, 0:1], 0.0)
            ve.tensor_add(sgo[:, :], send[:, :], scnt[:, :])
            ve.memset(smrg[:, :], 0.0)
        v_go = ve.n

        K = {}
        for r in range(NCORES):
            @block.sync
            def _(s, r=r):
                if r == 0:
                    sq.wait(ve, v_go)
                else:
                    sq.wait(ve, K[("mrg", r - 1)])
                sq.start(stbl64[:, :], tbls[r:r + 1, :].rearrange("o (e s) -> (o e) s", e=EMOE),
                         sem=ldt)
            ldt_r = ldt.n

            for nchk, o in ((512, 0), (512, 512), (1, 1024)):
                @block.tensor
                def _(t, r=r, nchk=nchk, o=o, ldt_r=ldt_r):
                    if o == 0:
                        pe.wait(ldt, ldt_r)
                        pe.wait(ldc, ldc_done)
                        if r > 0:
                            pe.wait(ve, K[("mrg", r - 1)])
                    i = pe.matmul(pmm[0:8, 0:nchk] if o == 0 else
                                  (pmm[0:8, 512:512 + nchk] if o == 512 else pmm[0:8, 1024:1025]),
                                  ssel[:, :], stbl64[:, o:o + nchk], start=True, stop=True)
                    K[("sel", r, o)] = pe.done(i)

            @block.vector
            def _(v, r=r):
                ve.wait(pe, K[("sel", r, 1024)])
                ve.tensor_copy(stbl[:, 0:512], pmm[0:8, 0:512])
                ve.tensor_copy(stbl[:, 512:1024], pmm[0:8, 512:1024])
                ve.tensor_copy(stbl[:, 1024:1025], pmm[0:8, 1024:1025])
                ve.tensor_scalar(si1[:, :], sio[:, :], send[:, r:r + 1], None, ALU.is_ge)
                ve.tensor_scalar(si2[:, :], sio[:, :], sgo[:, r:r + 1], None, ALU.is_lt)
                ve.tensor_mul(si1[:, :], si1[:, :], si2[:, :])
                ve.tensor_mul(si1[:, :], si1[:, :], stbl[:, :])
                ve.tensor_add(smrg[:, :], smrg[:, :], si1[:, :])
            K[("mrg", r)] = ve.n

        for st in range(8):
            @block.tensor
            def _(t, st=st):
                if st == 0:
                    pe.wait(ve, K[("mrg", NCORES - 1)])
                else:
                    pe.wait(ve, K[("trc", st - 1)])
                i = pe.transpose(ptr[:, 0:8], smrg[:, st * 128:(st + 1) * 128], sidn[0:8, 0:8])
                K[("tr", st)] = pe.done(i)

            @block.vector
            def _(v, st=st):
                ve.wait(pe, K[("tr", st)])
                ve.tensor_copy(stc[:, st * 8:(st + 1) * 8], ptr[:, 0:8])
            K[("trc", st)] = ve.n

        @block.vector
        def _(v):
            ve.tensor_copy(sti[:, :], stc[:, :])
        v_idx = ve.n

        gstg = [P.sem("gstg0"), P.sem("gstg1")]
        gi = 0
        for el in range(8):
            for st in range(8):
                par = gi % 2

                @block.gpsimd
                def _(g, el=el, st=st, par=par, gi=gi):
                    if gi == 0:
                        gq.wait(ve, v_idx)
                    if gi >= 2:
                        g.wait_ge(gstg[par].h, gstg[par].base + 16 * (gi // 2))
                    gq.indirect(out=sgst[:, par * D:(par + 1) * D],
                                out_offset=None, in_=xg,
                                in_offset=IndirectOffsetOnAxis(sti[:, st * 8 + el:st * 8 + el + 1], axis=0),
                                sem=gat)
                K[("gath", gi)] = gat.n

                @block.sync
                def _(s, el=el, st=st, par=par, gi=gi):
                    s.wait_ge(gat.h, K[("gath", gi)])
                    sq.start(xe_d[el * CAPM + st * 128:el * CAPM + (st + 1) * 128, :],
                             sgst[:, par * D:(par + 1) * D], sem=gstg[par])
                gi += 1
        gat_all = (gstg[0].n, gstg[1].n)

        for el in range(8):
            pp = el % 2

            @block.gpsimd
            def _(g, el=el, pp=pp):
                if el >= 2:
                    gq.wait(pe, K[("upmm", el - 2)])
                gq.start(swu[:, pp * 8192:(pp + 1) * 8192],
                         wup[el * D:(el + 1) * D, :]
                         .rearrange("(k p) (j m) -> p j k m", p=128, m=128), sem=ldwu[pp])
                if el >= 2:
                    gq.wait(pe, K[("dnmm", el - 2)])
                gq.start(swd[:, pp * 4096:(pp + 1) * 4096],
                         wdn[el * 256:(el + 1) * 256, :]
                         .rearrange("(k p) (t m) -> p t k m", p=128, m=128), sem=ldwd[pp])
            K[("ldw", el)] = (ldwu[pp].n, ldwd[pp].n)

            @block.sync
            def _(s, el=el, pp=pp):
                if el == 0:
                    s.wait_ge(gstg[0].h, gat_all[0])
                    s.wait_ge(gstg[1].h, gat_all[1])
                if el >= 2:
                    sq.wait(pe, K[("upmm", el - 2)])
                for dt_ in range(16):
                    sq.transpose(sxe[:, pp * 16384 + dt_ * 1024:pp * 16384 + (dt_ + 1) * 1024],
                                 xe_d[el * CAPM:(el + 1) * CAPM, dt_ * 128:(dt_ + 1) * 128],
                                 sem=ldxe[pp])
            K[("ldxe", el)] = ldxe[pp].n

            for n2 in range(2):
                @block.tensor
                def _(t, el=el, pp=pp, n2=n2):
                    if n2 == 0:
                        pe.wait(ldxe[pp], K[("ldxe", el)])
                        pe.wait(ldwu[pp], K[("ldw", el)][0])
                        if el >= 1:
                            pe.wait(ve, K[("h", el - 1, 1)])
                            pe.wait(se, K[("ydev", el - 1, 15)])
                    else:
                        pe.wait(ve, K[("h", el, 0)])
                    for jt in range(4):
                        for k in range(16):
                            i = pe.matmul(pmm[:, jt * 512:(jt + 1) * 512],
                                          swu[:, pp * 8192 + (jt * 16 + k) * 128:
                                              pp * 8192 + (jt * 16 + k + 1) * 128],
                                          sxe[:, pp * 16384 + k * 1024 + n2 * 512:
                                              pp * 16384 + k * 1024 + (n2 + 1) * 512],
                                          start=(k == 0), stop=(k == 15))
                    K[("upmm", el, n2)] = pe.done(i)
                    if n2 == 1:
                        K[("upmm", el)] = pe.n

                @block.scalar
                def _(s, el=el, pp=pp, n2=n2):
                    se.wait(pe, K[("upmm", el, n2)])
                    se.activation(sg_[:, 0:512], pmm[:, 0:512], AF.Silu)
                    se.activation(sg_[:, 512:1024], pmm[:, 512:1024], AF.Silu)
                K[("silu", el, n2)] = se.n

                @block.vector
                def _(v, el=el, pp=pp, n2=n2):
                    ve.wait(se, K[("silu", el, n2)])
                    ve.tensor_mul(sh[:, n2 * 512:(n2 + 1) * 512], sg_[:, 0:512],
                                  pmm[:, 1024:1536])
                    ve.tensor_mul(sh[:, 1024 + n2 * 512:1024 + (n2 + 1) * 512],
                                  sg_[:, 512:1024], pmm[:, 1536:2048])
                K[("h", el, n2)] = ve.n

            for mt in range(16):
                spp = mt % 2

                @block.tensor
                def _(t, el=el, pp=pp, mt=mt, spp=spp):
                    if mt == 0:
                        pe.wait(ve, K[("h", el, 1)])
                        pe.wait(ldwd[pp], K[("ldw", el)][1])
                    if mt >= 2:
                        pe.wait(se, K[("ydev", el, mt - 2)])
                    for n2 in range(2):
                        for kh in range(2):
                            i = pe.matmul(pmm[:, (spp * 2 + n2) * 512:(spp * 2 + n2 + 1) * 512],
                                          swd[:, pp * 4096 + (mt * 2 + kh) * 128:
                                              pp * 4096 + (mt * 2 + kh + 1) * 128],
                                          sh[:, kh * 1024 + n2 * 512:kh * 1024 + (n2 + 1) * 512],
                                          start=(kh == 0), stop=(kh == 1))
                    K[("dnmm", el, mt)] = pe.done(i)

                @block.scalar
                def _(s, el=el, mt=mt, spp=spp):
                    se.wait(pe, K[("dnmm", el, mt)])
                    if (el, mt) != (0, 0):
                        pel, pmt = (el, mt - 1) if mt else (el - 1, 15)
                        se.wait(stye, K[("yst", pel, pmt)])
                    se.copy(syst[:, 0:512], pmm[:, spp * 1024:spp * 1024 + 512])
                    se.copy(syst[:, 512:1024], pmm[:, spp * 1024 + 512:spp * 1024 + 1024])
                K[("ydev", el, mt)] = se.n

                @block.sync
                def _(s, el=el, mt=mt):
                    sq.wait(se, K[("ydev", el, mt)])
                    if el >= 1 and mt == 0:
                        s.wait_ge(ldyt.h, K[("ytr", el - 1, 7)])
                    sq.start(ye_d[mt * 128:(mt + 1) * 128, 0:512], syst[:, 0:512], sem=stye)
                    sq.start(ye_d[mt * 128:(mt + 1) * 128, 512:1024], syst[:, 512:1024], sem=stye)
                K[("yst", el, mt)] = stye.n
            K[("dnmm", el)] = pe.n

            for st in range(8):
                tpp = st % 2

                @block.sync
                def _(s, el=el, st=st, tpp=tpp):
                    if st == 0:
                        s.wait_ge(stye.h, K[("yst", el, 15)])
                    gi = el * 8 + st
                    if gi >= 2:
                        s.wait_ge(sty[tpp].h, sty[tpp].base + 16 * (gi // 2))
                    sq.transpose(sytm[:, tpp * D:(tpp + 1) * D],
                                 ye_d[:, st * 128:(st + 1) * 128], sem=ldyt)
                K[("ytr", el, st)] = ldyt.n

                @block.sync
                def _(s, el=el, st=st, tpp=tpp):
                    s.wait_ge(ldyt.h, K[("ytr", el, st)])
                    sq.start(y_o[el * ROWS_E + st * 128:el * ROWS_E + (st + 1) * 128, :],
                             sytm[:, tpp * D:(tpp + 1) * D], sem=sty[tpp])

            @block.sync
            def _(s, el=el):
                sq.wait(ve, v_go)
                sq.start(y_o[el * ROWS_E + CAPM:el * ROWS_E + CAPM + 1, :], szero[:, :],
                         sem=sttr)

        @block.sync
        def _(s):
            for sem in (sty[0], sty[1], sttr, stye, ldt, ldc, ldyt, ldxe[0], ldxe[1]):
                s.wait_ge(sem.h, sem.n)

        @block.gpsimd
        def _(g):
            for sem in (gat, ldwu[0], ldwu[1], ldwd[0], ldwd[1], gstg[0], gstg[1]):
                g.wait_ge(sem.h, sem.n)



def host_moe_expert(xnt_all, tbl_all, cnt_all, w_up, w_down):
    xg = np.concatenate([np.asarray(x) for x in xnt_all], 0)
    tbls = np.stack([np.asarray(t).reshape(-1) for t in tbl_all]).astype(np.float32)
    islot = np.broadcast_to(np.arange(ROWS_E, dtype=np.float32), (8, ROWS_E)).copy()
    idn = np.eye(128, dtype=np.float32)
    cnt_mat = np.stack([np.asarray(c).reshape(-1) for c in cnt_all])
    in_maps = []
    for c in range(NCORES):
        cntT = np.ascontiguousarray(cnt_mat[:, 8 * c:8 * c + 8].T)
        sel = np.zeros((64, 8), np.float32)
        for j in range(8):
            sel[8 * c + j, j] = 1.0
        in_maps.append({
            "xg": xg,
            "tbls": tbls,
            "sel64": sel,
            "cntT": _f32(cntT),
            "wup": _f32(w_up[8 * c:8 * c + 8].reshape(8 * D, 512)),
            "wdn": _f32(w_down[8 * c:8 * c + 8].reshape(8 * 256, D)),
            "islot": islot,
            "idn": idn,
        })
    return in_maps



def phase_moe_combine(P):
    nc = P.nc
    yall = P.dio("yall", [EMOE * ROWS_E, D], BF16, "in")
    slot = P.dio("slot", [TC, TOPK], F32, "in")
    wv = P.dio("wv", [TC, TOPK], F32, "in")
    sht = P.dio("sht", [TC, D], BF16, "in")
    x1t = P.dio("x1t", [TC, D], F32, "in")
    x2_o = P.dio("x2t", [TC, D], F32, "out")

    NT = 16
    with contextlib.ExitStack() as ctx:
        sslot = ctx.enter_context(nc.sbuf_tensor(P.nm("sslot"), [128, TOPK], F32))
        sslo32 = ctx.enter_context(nc.sbuf_tensor(P.nm("sslo32"), [128, TOPK], I32))
        swv = ctx.enter_context(nc.sbuf_tensor(P.nm("swv"), [128, TOPK], F32))
        sga = ctx.enter_context(nc.sbuf_tensor(P.nm("sga"), [128, 2 * D], BF16))
        ssh = ctx.enter_context(nc.sbuf_tensor(P.nm("ssh"), [128, D], BF16))
        sx1 = ctx.enter_context(nc.sbuf_tensor(P.nm("sx1"), [128, D], F32))
        sacc = ctx.enter_context(nc.sbuf_tensor(P.nm("sacc"), [128, 2 * D], F32))

        E = P.engines()
        block = P.block
        ve, gq, sq = E.ve, E.gq, E.sq
        lds = P.sem("lds")
        gat = [P.sem("gat0"), P.sem("gat1")]
        stx = [P.sem("stx0"), P.sem("stx1")]

        K = {}
        for tt in range(NT):
            par = tt % 2

            @block.sync
            def _(s, tt=tt, par=par):
                if tt >= 1:
                    sq.wait(ve, K[("acc", tt - 1)])
                sq.start(sslot[:, :], slot[tt * 128:(tt + 1) * 128, :], sem=lds)
                sq.start(swv[:, :], wv[tt * 128:(tt + 1) * 128, :], sem=lds)
                sq.start(ssh[:, :], sht[tt * 128:(tt + 1) * 128, :], sem=lds)
                sq.start(sx1[:, :], x1t[tt * 128:(tt + 1) * 128, :], sem=lds)
            K[("lds", tt)] = lds.n

            @block.vector
            def _(v, tt=tt, par=par):
                ve.wait(lds, K[("lds", tt)])
                if tt >= 1:
                    ve.wait(gat[(tt - 1) % 2], K[("ga", tt - 1, TOPK - 1)])
                if tt >= 2:
                    ve.wait(stx[par], stx[par].base + 16 * (tt // 2))
                ve.tensor_copy(sslo32[:, :], sslot[:, :])
                ve.tensor_add(sacc[:, par * D:(par + 1) * D], sx1[:, :], ssh[:, :])
            K[("i32", tt)] = ve.n

            for k in range(TOPK):
                @block.gpsimd
                def _(g, tt=tt, k=k, par=par):
                    if k == 0:
                        gq.wait(ve, K[("i32", tt)])
                    else:
                        gq.wait(ve, K[("k", tt, k - 1)])
                    gq.indirect(out=sga[:, par * D:(par + 1) * D], out_offset=None,
                                in_=yall,
                                in_offset=IndirectOffsetOnAxis(sslo32[:, k:k + 1], axis=0),
                                sem=gat[par])
                K[("ga", tt, k)] = gat[par].n

                @block.vector
                def _(v, tt=tt, k=k, par=par):
                    ve.wait(gat[par], K[("ga", tt, k)])
                    ve.scalar_tensor_tensor(sacc[:, par * D:(par + 1) * D],
                                            sga[:, par * D:(par + 1) * D],
                                            swv[:, k:k + 1],
                                            sacc[:, par * D:(par + 1) * D],
                                            ALU.mult, ALU.add)
                K[("k", tt, k)] = ve.n
            K[("acc", tt)] = ve.n

            @block.sync
            def _(s, tt=tt, par=par):
                sq.wait(ve, K[("acc", tt)])
                sq.start(x2_o[tt * 128:(tt + 1) * 128, :], sacc[:, par * D:(par + 1) * D],
                         sem=stx[par])

        @block.sync
        def _(s):
            for sem in (stx[0], stx[1], lds):
                s.wait_ge(sem.h, sem.n)

        @block.gpsimd
        def _(g):
            for sem in (gat[0], gat[1]):
                g.wait_ge(sem.h, sem.n)



def host_moe_combine(y_all_cores, slot_all, wv_all, sht_all, x1_all):
    yall = np.concatenate([np.asarray(y) for y in y_all_cores], 0)
    in_maps = []
    for c in range(NCORES):
        in_maps.append({
            "yall": yall,
            "slot": _f32(slot_all[c]),
            "wv": _f32(wv_all[c]),
            "sht": sht_all[c] if np.asarray(sht_all[c]).dtype == BF else _bf(sht_all[c]),
            "x1t": _f32(x1_all[c]),
        })
    return in_maps
N_HEAD = 16
N_KV = 4
HD = 128
ROPE_EPS = 1e-6


def phase_qkv(P):
    nc = P.nc
    x2 = P.dio("x2", [D, TC], F32, "in")
    wq = P.dio("wq", [D, D], F32, "in")
    wk = P.dio("wk", [D, 512], F32, "in")
    wv = P.dio("wv", [D, 512], F32, "in")
    cosr = P.dio("cosr", [128, TC], F32, "in")
    sinr = P.dio("sinr", [128, TC], F32, "in")
    ones_bf = P.dio("ones_bf", [128, 128], BF16, "in")

    qh_o = P.dio("qh", [D, TC], BF16, "out")
    kh_o = P.dio("kh", [512, TC], BF16, "out")
    v_o = P.dio("vv", [512, TC], BF16, "out")
    xn_o = P.dio("xn2", [D, TC], BF16, "out")

    NG = 4
    GWD = 512
    with contextlib.ExitStack() as ctx:
        sx = ctx.enter_context(nc.sbuf_tensor(P.nm("sx"), [128, 16 * GWD], F32))
        sxn = ctx.enter_context(nc.sbuf_tensor(P.nm("sxn"), [128, 16 * TC], BF16))
        sx2b = ctx.enter_context(nc.sbuf_tensor(P.nm("sx2b"), [128, 2 * GWD], BF16))
        sm = ctx.enter_context(nc.sbuf_tensor(P.nm("sm"), [128, GWD], F32))
        srs = ctx.enter_context(nc.sbuf_tensor(P.nm("srs"), [128, GWD], F32))
        svar = ctx.enter_context(nc.sbuf_tensor(P.nm("svar"), [128, GWD], F32))
        sonesb = ctx.enter_context(nc.sbuf_tensor(P.nm("sonesb"), [128, 128], BF16))
        sonef = ctx.enter_context(nc.sbuf_tensor(P.nm("sonef"), [128, 128], F32))
        sw = ctx.enter_context(nc.sbuf_tensor(P.nm("sw"), [128, 2 * 16 * 128], BF16))
        shead = ctx.enter_context(nc.sbuf_tensor(P.nm("shead"), [128, 2 * TC], BF16))
        sqo = ctx.enter_context(nc.sbuf_tensor(P.nm("sqo"), [128, TC], BF16))
        scos = ctx.enter_context(nc.sbuf_tensor(P.nm("scos"), [128, TC], F32))
        ssin = ctx.enter_context(nc.sbuf_tensor(P.nm("ssin"), [128, TC], F32))
        st1 = ctx.enter_context(nc.sbuf_tensor(P.nm("st1"), [128, TC], F32))
        st2 = ctx.enter_context(nc.sbuf_tensor(P.nm("st2"), [128, TC], F32))
        sq2 = ctx.enter_context(nc.sbuf_tensor(P.nm("sq2"), [128, TC], BF16))
        srms = ctx.enter_context(nc.sbuf_tensor(P.nm("srms"), [128, TC], F32))

        pS1 = ctx.enter_context(nc.psum_tensor(P.nm("pS1"), [128, 512], F32))
        pS2 = ctx.enter_context(nc.psum_tensor(P.nm("pS2"), [128, 512], F32))
        pmm = ctx.enter_context(nc.psum_tensor(P.nm("pmm"), [128, 2 * 512], F32))
        pn = ctx.enter_context(nc.psum_tensor(P.nm("pn"), [128, 2 * 512], F32))

        E = P.engines()
        block = P.block
        ve, se, pe, gq, sq = E.ve, E.se, E.pe, E.gq, E.sq
        ldc = P.sem("ldc")
        ldx = P.sem("ldx")
        ldw = [P.sem("ldw0"), P.sem("ldw1")]
        sto = P.sem("sto")

        @block.sync
        def _(s):
            sq.start(sonesb[:, :], ones_bf, sem=ldc)
            sq.start(scos[:, :], cosr, sem=ldc)
            sq.start(ssin[:, :], sinr, sem=ldc)
        ldc_done = ldc.n

        K = {}
        for g in range(NG):
            go = g * GWD

            @block.gpsimd
            def _(gg, g=g, go=go):
                if g >= 1:
                    gq.wait(ve, K[("xn", g - 1)])
                for k in range(16):
                    gq.start(sx[:, k * GWD:(k + 1) * GWD],
                             x2[k * 128:(k + 1) * 128, go:go + GWD], sem=ldx)
            K[("ldx", g)] = ldx.n

            for k in range(16):
                pp = k % 2

                @block.vector
                def _(v, g=g, k=k, pp=pp):
                    if k == 0:
                        ve.wait(ldx, K[("ldx", g)])
                        if g == 0:
                            ve.wait(ldc, ldc_done)
                            ve.memset(sonef[:, :], 1.0)
                    if k >= 2:
                        ve.wait(pe, K[("st2", g, k - 2)])
                    ve.tensor_mul(sx2b[:, pp * GWD:(pp + 1) * GWD],
                                  sx[:, k * GWD:(k + 1) * GWD],
                                  sx[:, k * GWD:(k + 1) * GWD])
                    K[("x2", g, k)] = ve.n

                @block.tensor
                def _(t, g=g, k=k, pp=pp):
                    if g > 0 and k == 0:
                        pe.wait(se, K[("rs", g - 1)])
                    pe.wait(ve, K[("x2", g, k)])
                    pe.matmul(pS1[:, :], sonef[:], sx[:, k * GWD:(k + 1) * GWD],
                              start=(k == 0), stop=(k == 15))
                    i = pe.matmul(pS2[:, :], sonesb[:], sx2b[:, pp * GWD:(pp + 1) * GWD],
                                  start=(k == 0), stop=(k == 15))
                    K[("st2", g, k)] = pe.done(i)

            @block.vector
            def _(v, g=g):
                ve.wait(pe, K[("st2", g, 15)])
                ve.tensor_scalar_mul(sm[:, :], pS1[:, :], 1.0 / D)
                ve.tensor_mul(svar[:, :], sm[:, :], sm[:, :])
                ve.scalar_tensor_tensor(svar[:, :], pS2[:, :], 1.0 / D, svar[:, :],
                                        ALU.mult, ALU.subtract)
                ve.tensor_scalar_add(svar[:, :], svar[:, :], LN_EPS)
            K[("var", g)] = ve.n

            @block.scalar
            def _(s, g=g):
                se.wait(ve, K[("var", g)])
                se.activation(srs[:, 0:GWD], svar[:, :], AF.Ln)
                se.activation(srs[:, 0:GWD], srs[:, 0:GWD], AF.Exp, scale=-0.5)
            K[("rs", g)] = se.n

            @block.vector
            def _(v, g=g, go=go):
                ve.wait(se, K[("rs", g)])
                for k in range(16):
                    ve.tensor_sub(sx[:, k * GWD:(k + 1) * GWD], sx[:, k * GWD:(k + 1) * GWD],
                                  sm[:, :])
                    ve.tensor_mul(sxn[:, k * TC + go:k * TC + go + GWD],
                                  sx[:, k * GWD:(k + 1) * GWD], srs[:, 0:GWD])
            K[("xn", g)] = ve.n

        @block.sync
        def _(s):
            sq.wait(ve, K[("xn", NG - 1)])
            for k in range(16):
                sq.start(xn_o[k * 128:(k + 1) * 128, :], sxn[:, k * TC:(k + 1) * TC], sem=sto)

        for ot in range(24):
            pp = ot % 2
            if ot < 16:
                wsrc, off = wq, ot * 128
                outdst = qh_o[ot * 128:(ot + 1) * 128, :]
            elif ot < 20:
                wsrc, off = wk, (ot - 16) * 128
                outdst = kh_o[(ot - 16) * 128:(ot - 15) * 128, :]
            else:
                wsrc, off = wv, (ot - 20) * 128
                outdst = v_o[(ot - 20) * 128:(ot - 19) * 128, :]

            @block.gpsimd
            def _(g, ot=ot, pp=pp, wsrc=wsrc, off=off):
                if ot >= 2:
                    gq.wait(pe, K[("mm", ot - 2, NG - 1)])
                gq.start(sw[:, pp * 2048:(pp + 1) * 2048],
                         wsrc[:, off:off + 128].rearrange("(k p) e -> p k e", p=128),
                         sem=ldw[pp])
            K[("ldw", ot)] = ldw[pp].base + 16 * (ot // 2 + 1)

            for g in range(NG):
                idx = ot * NG + g
                bank = idx % 2

                @block.tensor
                def _(t, ot=ot, pp=pp, g=g, idx=idx, bank=bank):
                    if g == 0:
                        pe.wait(ldw[pp], K[("ldw", ot)])
                        if ot == 0:
                            pe.wait(ve, K[("xn", NG - 1)])
                    if idx >= 2:
                        pidx = idx - 2
                        pe.wait(se, K[("ev", pidx // NG, pidx % NG)])
                    for k in range(16):
                        i = pe.matmul(pmm[:, bank * 512:(bank + 1) * 512],
                                      sw[:, pp * 2048 + k * 128:pp * 2048 + (k + 1) * 128],
                                      sxn[:, k * TC + g * GWD:k * TC + (g + 1) * GWD],
                                      start=(k == 0), stop=(k == 15))
                    K[("mm", ot, g)] = pe.done(i)

                @block.scalar
                def _(s, ot=ot, pp=pp, g=g, idx=idx, bank=bank):
                    if g == 0:
                        if ot >= 2 and ot - 2 < 20:
                            se.wait(ve, K[("rmul", ot - 2, NG - 1)])
                        elif ot >= 2:
                            se.wait(sto, sto.n)
                    se.wait(pe, K[("mm", ot, g)])
                    se.copy(shead[:, pp * TC + g * GWD:pp * TC + (g + 1) * GWD],
                            pmm[:, bank * 512:(bank + 1) * 512])
                    K[("ev", ot, g)] = se.n

            if ot >= 20:
                @block.sync
                def _(s, ot=ot, pp=pp, outdst=outdst):
                    sq.wait(se, K[("ev", ot, NG - 1)])
                    sq.start(outdst, shead[:, pp * TC:(pp + 1) * TC], sem=sto)
            else:
                @block.vector
                def _(v, ot=ot, pp=pp):
                    ve.wait(se, K[("ev", ot, NG - 1)])
                    if ot >= 1:
                        ve.wait(sto, sto.n)
                    x1 = shead[:, pp * TC:(pp + 1) * TC]
                    ve.tensor_copy(st2[0:64, 0:TC], x1[64:128, :])
                    ve.tensor_copy(st2[64:128, 0:TC], x1[0:64, :])
                    ve.tensor_tensor(st1[0:64, 0:TC], x1[0:64, :], scos[0:64, :], ALU.mult)
                    ve.tensor_tensor(st1[64:128, 0:TC], x1[64:128, :], scos[64:128, :], ALU.mult)
                    ve.tensor_tensor(st2[0:64, 0:TC], st2[0:64, 0:TC], ssin[0:64, :], ALU.mult)
                    ve.tensor_tensor(st2[64:128, 0:TC], st2[64:128, 0:TC], ssin[64:128, :], ALU.mult)
                    ve.tensor_add(st1[0:64, 0:TC], st1[0:64, 0:TC], st2[0:64, 0:TC])
                    ve.tensor_sub(st1[64:128, 0:TC], st1[64:128, 0:TC], st2[64:128, 0:TC])
                    ve.tensor_mul(st2[:, 0:TC], st1[:, 0:TC], st1[:, 0:TC])
                    ve.tensor_copy(sq2[:, 0:TC], st2[:, 0:TC])
                K[("rope", ot)] = ve.n

                for g in range(NG):
                    @block.tensor
                    def _(t, ot=ot, g=g):
                        pe.wait(ve, K[("rope", ot)])
                        prev = K.get(("rv", ot, g - 2) if g >= 2 else ("rv", ot - 1, g + 2))
                        if prev:
                            pe.wait(ve, prev)
                        i = pe.matmul(pn[:, (g % 2) * 512:(g % 2) * 512 + 512], sonesb[:, :],
                                      sq2[:, g * 512:(g + 1) * 512], start=True, stop=True)
                        K[("rn", ot, g)] = pe.done(i)

                    @block.vector
                    def _(v, ot=ot, g=g):
                        ve.wait(pe, K[("rn", ot, g)])
                        ve.tensor_scalar(srms[:, g * 512:(g + 1) * 512],
                                         pn[:, (g % 2) * 512:(g % 2) * 512 + 512],
                                         1.0 / HD, ROPE_EPS, ALU.mult, ALU.add)
                    K[("rv", ot, g)] = ve.n

                    @block.scalar
                    def _(s, ot=ot, g=g):
                        se.wait(ve, K[("rv", ot, g)])
                        se.activation(srms[:, g * 512:(g + 1) * 512],
                                      srms[:, g * 512:(g + 1) * 512], AF.Ln)
                        se.activation(srms[:, g * 512:(g + 1) * 512],
                                      srms[:, g * 512:(g + 1) * 512], AF.Exp, scale=-0.5)
                    K[("srm", ot, g)] = se.n

                    @block.vector
                    def _(v, ot=ot, g=g):
                        ve.wait(se, K[("srm", ot, g)])
                        ve.tensor_mul(sqo[:, g * GWD:(g + 1) * GWD],
                                      st1[:, g * GWD:(g + 1) * GWD],
                                      srms[:, g * 512:(g + 1) * 512])
                    K[("rmul", ot, g)] = ve.n

                @block.sync
                def _(s, ot=ot, outdst=outdst):
                    for g in range(NG):
                        sq.wait(ve, K[("rmul", ot, g)])
                    sq.start(outdst, sqo[:, :], sem=sto)

        @block.sync
        def _(s):
            for sem in (sto, ldc):
                s.wait_ge(sem.h, sem.n)

        @block.gpsimd
        def _(g):
            for sem in (ldx, ldw[0], ldw[1]):
                g.wait_ge(sem.h, sem.n)



def host_qkv(x2_all, wq, wk, wv):
    ones_bf = np.ones((128, 128), dtype=BF)
    half = HD // 2
    inv_freq = 1.0 / (100000.0 ** (np.arange(half, dtype=np.float32) / half))
    in_maps = []
    for c in range(NCORES):
        s0 = c * TC
        t0 = s0 % T
        pos = np.arange(t0, t0 + TC, dtype=np.float32)
        fr = pos[None, :] * inv_freq[:, None]
        in_maps.append({
            "x2": _f32(np.asarray(x2_all[c % len(x2_all)]).T),
            "wq": _f32(wq), "wk": _f32(wk), "wv": _f32(wv),
            "cosr": _f32(np.concatenate([np.cos(fr), np.cos(fr)], 0)),
            "sinr": _f32(np.concatenate([np.sin(fr), np.sin(fr)], 0)),
            "ones_bf": ones_bf,
        })
    return in_maps



def phase_attn(P):
    nc = P.nc
    NU = 4
    qh = P.dio("qh", [NU * 128, T], BF16, "in")
    kh = P.dio("kh", [NU * 128, T], BF16, "in")
    vp = P.dio("vp", [NU * T, 128], BF16, "in")
    dmask = P.dio("dmask", [128, 4 * 512], F32, "in")
    ones_bf = P.dio("ones_bf", [128, 128], BF16, "in")
    y_out = P.dio("yat", [NU * 128, T], BF16, "out")

    NQT = T // 512
    SCALE = 1.0 / np.sqrt(128.0)
    G8 = 8

    with contextlib.ExitStack() as ctx:
        sq_ = ctx.enter_context(nc.sbuf_tensor(P.nm("sq_"), [128, 2 * T], BF16))
        sk = ctx.enter_context(nc.sbuf_tensor(P.nm("sk"), [128, 2 * T], BF16))
        sv = ctx.enter_context(nc.sbuf_tensor(P.nm("sv"), [128, 2 * T], BF16))
        sms = ctx.enter_context(nc.sbuf_tensor(P.nm("sms"), [128, 2 * G8 * 512], F32))
        sp = ctx.enter_context(nc.sbuf_tensor(P.nm("sp"), [128, 2 * G8 * 512], BF16))
        smk = ctx.enter_context(nc.sbuf_tensor(P.nm("smk"), [128, 4 * 512], F32))
        sonesb = ctx.enter_context(nc.sbuf_tensor(P.nm("sonesb"), [128, 128], BF16))
        sli = ctx.enter_context(nc.sbuf_tensor(P.nm("sli"), [128, 512], F32))
        sy = ctx.enter_context(nc.sbuf_tensor(P.nm("sy"), [128, 2 * 512], BF16))

        pS = ctx.enter_context(nc.psum_tensor(P.nm("pS"), [128, 4 * 512], F32))
        pL = ctx.enter_context(nc.psum_tensor(P.nm("pL"), [128, 2 * 512], F32))
        pY = ctx.enter_context(nc.psum_tensor(P.nm("pY"), [128, 2 * 512], F32))

        E = P.engines()
        block = P.block
        ve, se, pe, gq, sq = E.ve, E.se, E.pe, E.gq, E.sq
        ldc = P.sem("ldc")
        ldq = [P.sem("ldq0"), P.sem("ldq1")]
        ldk = [P.sem("ldk0"), P.sem("ldk1")]
        ldv = [P.sem("ldv0"), P.sem("ldv1")]
        sty = [P.sem("sty0"), P.sem("sty1")]

        @block.sync
        def _(s):
            sq.start(sonesb[:, :], ones_bf, sem=ldc)
            sq.start(smk[:, :], dmask, sem=ldc)
        ldc_done = ldc.n

        K = {}
        gc = 0
        for u in range(4):
            upp = u % 2

            @block.gpsimd
            def _(g, u=u, upp=upp):
                if u >= 2:
                    gq.wait(pe, K[("ulast", u - 2)])
                gq.start(sq_[:, upp * T:(upp + 1) * T], qh[u * 128:(u + 1) * 128, :],
                         sem=ldq[upp])
                gq.start(sk[:, upp * T:(upp + 1) * T], kh[u * 128:(u + 1) * 128, :],
                         sem=ldk[upp])
                gq.start(sv[:, upp * T:(upp + 1) * T],
                         vp[u * T:(u + 1) * T, :].rearrange("(n p) d -> p n d", p=128),
                         sem=ldv[upp])
            K[("ld", u)] = (ldq[upp].base + 16 * (u // 2 + 1))

            for qt in range(NQT):
                nkt = 4 * qt + 4
                NG = (nkt + G8 - 1) // G8
                gqi = u * NQT + qt

                def avl_batch(g_, gc_, u=u, upp=upp, qt=qt, nkt=nkt, gqi=gqi):
                    @block.tensor
                    def _(t):
                        pe.wait(se, K[("exp", gc_)])
                        if g_ == 0 and gqi >= 2:
                            pe.wait(ve, K[("ep", gqi - 2)])
                        lo = g_ * G8
                        hi = min(lo + G8, nkt)
                        i = None
                        for kt in range(lo, hi):
                            sl = sp[:, (gc_ % 2) * G8 * 512 + (kt - lo) * 512:
                                    (gc_ % 2) * G8 * 512 + (kt - lo + 1) * 512]
                            i = pe.matmul(pL[:, (qt % 2) * 512:(qt % 2 + 1) * 512],
                                          sonesb[:, :], sl,
                                          start=(kt == 0), stop=(kt == nkt - 1))
                            i = pe.matmul(pY[:, (qt % 2) * 512:(qt % 2 + 1) * 512],
                                          sv[:, upp * T + kt * 128:upp * T + (kt + 1) * 128],
                                          sl, start=(kt == 0), stop=(kt == nkt - 1))
                        K[("avg", gc_)] = pe.done(i)

                for g in range(NG):
                    lo = g * G8
                    hi = min(lo + G8, nkt)
                    for half in range(2):
                        h0 = lo + half * 4
                        h1 = min(h0 + 4, hi)
                        if h0 >= h1:
                            continue

                        @block.tensor
                        def _(t, u=u, upp=upp, qt=qt, h0=h0, h1=h1, gqi=gqi):
                            for kt in range(h0, h1):
                                if (u, qt, kt) == (0, 0, 0):
                                    pe.wait(ldc, ldc_done)
                                if qt == 0 and kt == 0:
                                    rel = K[("ld", u)] - ldq[upp].base
                                    pe.wait(ldq[upp], ldq[upp].base + rel)
                                    pe.wait(ldk[upp], ldk[upp].base + rel)
                                    pe.wait(ldv[upp], ldv[upp].base + rel)
                                if kt >= 4:
                                    pe.wait(ve, K[("cp", u, qt, kt - 4)])
                                elif kt == 0 and gqi > 0:
                                    pe.wait(ve, K[("cpall", gqi - 1)])
                                i = pe.matmul(pS[:, (kt % 4) * 512:(kt % 4 + 1) * 512],
                                              sk[:, upp * T + kt * 128:upp * T + (kt + 1) * 128],
                                              sq_[:, upp * T + qt * 512:upp * T + (qt + 1) * 512],
                                              start=True, stop=True)
                                K[("smm", u, qt, kt)] = pe.done(i)

                        @block.vector
                        def _(v, u=u, qt=qt, g=g, gc=gc, lo=lo, h0=h0, h1=h1):
                            for kt in range(h0, h1):
                                ve.wait(pe, K[("smm", u, qt, kt)])
                                if kt == lo and gc >= 2:
                                    ve.wait(se, K[("exp", gc - 2)])
                                dst = sms[:, (gc % 2) * G8 * 512 + (kt - lo) * 512:
                                          (gc % 2) * G8 * 512 + (kt - lo + 1) * 512]
                                src = pS[:, (kt % 4) * 512:(kt % 4 + 1) * 512]
                                if kt >= 4 * qt:
                                    mi = kt - 4 * qt
                                    ve.tensor_tensor(dst, src,
                                                     smk[:, mi * 512:(mi + 1) * 512],
                                                     ALU.min)
                                else:
                                    ve.tensor_copy(dst, src)
                                K[("cp", u, qt, kt)] = ve.n

                    @block.scalar
                    def _(s, gc=gc, lo=lo, hi=hi, qt=qt, u=u):
                        se.wait(ve, K[("cp", u, qt, hi - 1)])
                        if ("avg", gc - 2) in K:
                            se.wait(pe, K[("avg", gc - 2)])
                        w = (hi - lo) * 512
                        se.activation(sp[:, (gc % 2) * G8 * 512:(gc % 2) * G8 * 512 + w],
                                      sms[:, (gc % 2) * G8 * 512:(gc % 2) * G8 * 512 + w],
                                      AF.Exp, scale=SCALE)
                    K[("exp", gc)] = se.n
                    K[("cpall", gqi)] = ve.n

                    if g >= 1:
                        avl_batch(g - 1, gc - 1)
                    gc += 1

                avl_batch(NG - 1, gc - 1)

                @block.vector
                def _(v, u=u, qt=qt, gqi=gqi, gc=gc):
                    ve.wait(pe, K[("avg", gc - 1)])
                    if gqi >= 2:
                        ve.wait(sty[gqi % 2], sty[gqi % 2].base + 16 * (gqi // 2))
                    ve.reciprocal(sli[:, :], pL[:, (qt % 2) * 512:(qt % 2 + 1) * 512])
                    ve.tensor_tensor(sy[:, (gqi % 2) * 512:(gqi % 2 + 1) * 512],
                                     pY[:, (qt % 2) * 512:(qt % 2 + 1) * 512],
                                     sli[:, :], ALU.mult)
                K[("ep", gqi)] = ve.n

                @block.sync
                def _(s, u=u, qt=qt, gqi=gqi):
                    sq.wait(ve, K[("ep", gqi)])
                    sq.start(y_out[u * 128:(u + 1) * 128, qt * 512:(qt + 1) * 512],
                             sy[:, (gqi % 2) * 512:(gqi % 2 + 1) * 512], sem=sty[gqi % 2])
            K[("ulast", u)] = pe.n

        @block.sync
        def _(s):
            for sem in (sty[0], sty[1], ldc):
                s.wait_ge(sem.h, sem.n)

        @block.gpsimd
        def _(g):
            for sem in (ldq[0], ldq[1], ldk[0], ldk[1], ldv[0], ldv[1]):
                g.wait_ge(sem.h, sem.n)



def host_attn(qh_all, kh_all, v_all):
    qh_full = np.zeros((2, N_HEAD * 128, T), dtype=BF)
    kh_full = np.zeros((2, N_KV * 128, T), dtype=BF)
    v_full = np.zeros((2, N_KV * 128, T), dtype=BF)
    for c in range(NCORES):
        s0 = c * TC
        b0, t0 = s0 // T, s0 % T
        qh_full[b0, :, t0:t0 + TC] = np.asarray(qh_all[c])
        kh_full[b0, :, t0:t0 + TC] = np.asarray(kh_all[c])
        v_full[b0, :, t0:t0 + TC] = np.asarray(v_all[c])
    MASKV = -10000.0 * np.sqrt(128.0)
    dmask = np.zeros((128, 4 * 512), np.float32)
    for mi in range(4):
        q_idx = np.arange(512)[None, :]
        k_idx = (np.arange(128) + mi * 128)[:, None]
        dmask[:, mi * 512:(mi + 1) * 512] = np.where(q_idx >= k_idx, 3e38, MASKV)
    ones_bf = np.ones((128, 128), dtype=BF)
    in_maps = []
    for c in range(NCORES):
        qs, ks, vs, = [], [], []
        for un in range(4):
            uid = c * 4 + un
            b, h = uid // 16, uid % 16
            kvh = h // 4
            qs.append(qh_full[b, h * 128:(h + 1) * 128, :])
            ks.append(kh_full[b, kvh * 128:(kvh + 1) * 128, :])
            vs.append(np.ascontiguousarray(v_full[b, kvh * 128:(kvh + 1) * 128, :].T))
        in_maps.append({
            "qh": np.concatenate(qs, 0), "kh": np.concatenate(ks, 0),
            "vp": np.concatenate(vs, 0),
            "dmask": dmask, "ones_bf": ones_bf,
        })
    return in_maps



def phase_attnout(P):
    nc = P.nc
    ya = P.dio("ya", [D, TC], BF16, "in")
    wo = P.dio("wo", [D, D], F32, "in")
    x2 = P.dio("x2", [D, TC], F32, "in")
    x3_o = P.dio("x3", [D, TC], F32, "out")

    NG = 4
    GWD = 512
    with contextlib.ExitStack() as ctx:
        sy = ctx.enter_context(nc.sbuf_tensor(P.nm("sy"), [128, 16 * TC], BF16))
        sw = ctx.enter_context(nc.sbuf_tensor(P.nm("sw"), [128, 2 * 2048], BF16))
        sr = ctx.enter_context(nc.sbuf_tensor(P.nm("sr"), [128, 2 * GWD], F32))
        so = ctx.enter_context(nc.sbuf_tensor(P.nm("so"), [128, 2 * GWD], F32))
        pmm = ctx.enter_context(nc.psum_tensor(P.nm("pmm"), [128, 2 * 512], F32))

        E = P.engines()
        block = P.block
        ve, pe, gq, sq = E.ve, E.pe, E.gq, E.sq
        ldy = P.sem("ldy")
        ldw = [P.sem("ldw0"), P.sem("ldw1")]
        ldr = [P.sem("ldr0"), P.sem("ldr1")]
        stx = [P.sem("stx0"), P.sem("stx1")]

        @block.sync
        def _(s):
            for k in range(16):
                sq.start(sy[:, k * TC:(k + 1) * TC], ya[k * 128:(k + 1) * 128, :], sem=ldy)
        ldy_done = ldy.n

        K = {}
        for gi in range(64):
            ot, grp = gi // 4, gi % 4
            pp = ot % 2
            spp = gi % 2
            go = grp * GWD

            if grp == 0:
                @block.gpsimd
                def _(g, ot=ot, pp=pp):
                    if ot >= 2:
                        gq.wait(pe, K[("mm", (ot - 1) * 4 + 3)])
                    gq.start(sw[:, pp * 2048:(pp + 1) * 2048],
                             wo[:, ot * 128:(ot + 1) * 128].rearrange("(k p) e -> p k e", p=128),
                             sem=ldw[pp])

            @block.sync
            def _(s, gi=gi, ot=ot, go=go, spp=spp):
                if gi >= 2:
                    sq.wait(ve, K[("o", gi - 2)])
                sq.start(sr[:, spp * GWD:(spp + 1) * GWD],
                         x2[ot * 128:(ot + 1) * 128, go:go + GWD], sem=ldr[spp])
            K[("ldr", gi)] = ldr[spp].n

            @block.tensor
            def _(t, gi=gi, ot=ot, grp=grp, pp=pp, spp=spp, go=go):
                if grp == 0:
                    pe.wait(ldw[pp], ldw[pp].base + 16 * (ot // 2 + 1))
                    if ot == 0:
                        pe.wait(ldy, ldy_done)
                if gi >= 2:
                    pe.wait(ve, K[("o", gi - 2)])
                for k in range(16):
                    i = pe.matmul(pmm[:, spp * 512:(spp + 1) * 512],
                                  sw[:, pp * 2048 + k * 128:pp * 2048 + (k + 1) * 128],
                                  sy[:, k * TC + go:k * TC + go + GWD],
                                  start=(k == 0), stop=(k == 15))
                K[("mm", gi)] = pe.done(i)

            @block.vector
            def _(v, gi=gi, spp=spp):
                ve.wait(pe, K[("mm", gi)])
                ve.wait(ldr[spp], K[("ldr", gi)])
                if gi >= 2:
                    ve.wait(stx[spp], stx[spp].base + 16 * (gi // 2))
                ve.tensor_add(so[:, spp * GWD:(spp + 1) * GWD],
                              pmm[:, spp * 512:(spp + 1) * 512],
                              sr[:, spp * GWD:(spp + 1) * GWD])
            K[("o", gi)] = ve.n

            @block.sync
            def _(s, gi=gi, ot=ot, go=go, spp=spp):
                sq.wait(ve, K[("o", gi)])
                sq.start(x3_o[ot * 128:(ot + 1) * 128, go:go + GWD],
                         so[:, spp * GWD:(spp + 1) * GWD], sem=stx[spp])

        @block.sync
        def _(s):
            for sem in (stx[0], stx[1], ldy, ldr[0], ldr[1]):
                s.wait_ge(sem.h, sem.n)

        @block.gpsimd
        def _(g):
            for sem in (ldw[0], ldw[1]):
                g.wait_ge(sem.h, sem.n)



def host_attnout(yat_all, wo, x2_all):
    y_full = np.zeros((2, N_HEAD * 128, T), dtype=BF)
    for c in range(NCORES):
        for un in range(4):
            uid = c * 4 + un
            b, h = uid // 16, uid % 16
            y_full[b, h * 128:(h + 1) * 128, :] = np.asarray(yat_all[c])[un * 128:(un + 1) * 128, :]
    in_maps = []
    for c in range(NCORES):
        s0 = c * TC
        b0, t0 = s0 // T, s0 % T
        in_maps.append({
            "ya": np.ascontiguousarray(y_full[b0, :, t0:t0 + TC]),
            "wo": _f32(wo),
            "x2": _f32(x2_all[c]),
        })
    return in_maps


INTER = 5632
NIT = 44


def phase_ffn(P):
    nc = P.nc
    x3 = P.dio("x3", [D, TC], F32, "in")
    gwT = P.dio("gwT", [D, INTER], F32, "in")
    uwT = P.dio("uwT", [D, INTER], F32, "in")
    dwT = P.dio("dwT", [INTER, D], F32, "in")
    gs = P.dio("gs", [128, NIT], F32, "in")
    us = P.dio("us", [128, NIT], F32, "in")
    ds = P.dio("ds", [128, 16], F32, "in")
    ones_bf = P.dio("ones_bf", [128, 128], BF16, "in")
    x4_o = P.dio("x4", [D, TC], F32, "out")
    h_d = P.dio("h_d", [INTER, TC], BF16, "int")

    NG = 4
    GWD = 512
    with contextlib.ExitStack() as ctx:
        sx = ctx.enter_context(nc.sbuf_tensor(P.nm("sx"), [128, 16 * GWD], F32))
        sxn = ctx.enter_context(nc.sbuf_tensor(P.nm("sxn"), [128, 16 * TC], BF16))
        sx2b = ctx.enter_context(nc.sbuf_tensor(P.nm("sx2b"), [128, 2 * GWD], BF16))
        sm = ctx.enter_context(nc.sbuf_tensor(P.nm("sm"), [128, GWD], F32))
        srs = ctx.enter_context(nc.sbuf_tensor(P.nm("srs"), [128, GWD], F32))
        svar = ctx.enter_context(nc.sbuf_tensor(P.nm("svar"), [128, GWD], F32))
        sonesb = ctx.enter_context(nc.sbuf_tensor(P.nm("sonesb"), [128, 128], BF16))
        sonef = ctx.enter_context(nc.sbuf_tensor(P.nm("sonef"), [128, 128], F32))
        sgw = ctx.enter_context(nc.sbuf_tensor(P.nm("sgw"), [128, 2 * 2048], BF16))
        suw = ctx.enter_context(nc.sbuf_tensor(P.nm("suw"), [128, 2 * 2048], BF16))
        sdw = ctx.enter_context(nc.sbuf_tensor(P.nm("sdw"), [128, 2 * NIT * 128], BF16))
        sgs = ctx.enter_context(nc.sbuf_tensor(P.nm("sgs"), [128, NIT], F32))
        sus = ctx.enter_context(nc.sbuf_tensor(P.nm("sus"), [128, NIT], F32))
        sds = ctx.enter_context(nc.sbuf_tensor(P.nm("sds"), [128, 16], F32))
        sg = ctx.enter_context(nc.sbuf_tensor(P.nm("sg"), [128, TC], BF16))
        sh = ctx.enter_context(nc.sbuf_tensor(P.nm("sh"), [128, TC], BF16))
        shl = ctx.enter_context(nc.sbuf_tensor(P.nm("shl"), [128, NIT * GWD], BF16))
        sr = ctx.enter_context(nc.sbuf_tensor(P.nm("sr"), [128, 2 * GWD], F32))
        so = ctx.enter_context(nc.sbuf_tensor(P.nm("so"), [128, 2 * GWD], F32))

        pS1 = ctx.enter_context(nc.psum_tensor(P.nm("pS1"), [128, 512], F32))
        pS2 = ctx.enter_context(nc.psum_tensor(P.nm("pS2"), [128, 512], F32))
        pg = ctx.enter_context(nc.psum_tensor(P.nm("pg"), [128, 2 * 512], F32))
        pu = ctx.enter_context(nc.psum_tensor(P.nm("pu"), [128, 2 * 512], F32))
        pd = ctx.enter_context(nc.psum_tensor(P.nm("pd"), [128, 2 * 512], F32))

        E = P.engines()
        block = P.block
        ve, se, pe, gq, sq = E.ve, E.se, E.pe, E.gq, E.sq
        ldc = P.sem("ldc")
        ldx = P.sem("ldx")
        ldg = [P.sem("ldg0"), P.sem("ldg1")]
        ldu = [P.sem("ldu0"), P.sem("ldu1")]
        ldd = [P.sem("ldd0"), P.sem("ldd1")]
        ldh = [P.sem("ldh0"), P.sem("ldh1")]
        ldr = [P.sem("ldr0"), P.sem("ldr1")]
        sth = [P.sem("sth0"), P.sem("sth1")]
        stx = [P.sem("stx0"), P.sem("stx1")]

        @block.sync
        def _(s):
            sq.start(sonesb[:, :], ones_bf, sem=ldc)
            sq.start(sgs[:, :], gs, sem=ldc)
            sq.start(sus[:, :], us, sem=ldc)
            sq.start(sds[:, :], ds, sem=ldc)
        ldc_done = ldc.n

        K = {}
        for g in range(NG):
            go = g * GWD

            @block.gpsimd
            def _(gg, g=g, go=go):
                if g >= 1:
                    gq.wait(ve, K[("xn", g - 1)])
                for k in range(16):
                    gq.start(sx[:, k * GWD:(k + 1) * GWD],
                             x3[k * 128:(k + 1) * 128, go:go + GWD], sem=ldx)
            K[("ldx", g)] = ldx.n

            for k in range(16):
                pp = k % 2

                @block.vector
                def _(v, g=g, k=k, pp=pp):
                    if k == 0:
                        ve.wait(ldx, K[("ldx", g)])
                        if g == 0:
                            ve.wait(ldc, ldc_done)
                            ve.memset(sonef[:, :], 1.0)
                    if k >= 2:
                        ve.wait(pe, K[("st2", g, k - 2)])
                    ve.tensor_mul(sx2b[:, pp * GWD:(pp + 1) * GWD],
                                  sx[:, k * GWD:(k + 1) * GWD],
                                  sx[:, k * GWD:(k + 1) * GWD])
                    K[("x2", g, k)] = ve.n

                @block.tensor
                def _(t, g=g, k=k, pp=pp):
                    if g > 0 and k == 0:
                        pe.wait(se, K[("rs", g - 1)])
                    pe.wait(ve, K[("x2", g, k)])
                    pe.matmul(pS1[:, :], sonef[:], sx[:, k * GWD:(k + 1) * GWD],
                              start=(k == 0), stop=(k == 15))
                    i = pe.matmul(pS2[:, :], sonesb[:], sx2b[:, pp * GWD:(pp + 1) * GWD],
                                  start=(k == 0), stop=(k == 15))
                    K[("st2", g, k)] = pe.done(i)

            @block.vector
            def _(v, g=g):
                ve.wait(pe, K[("st2", g, 15)])
                ve.tensor_scalar_mul(sm[:, :], pS1[:, :], 1.0 / D)
                ve.tensor_mul(svar[:, :], sm[:, :], sm[:, :])
                ve.scalar_tensor_tensor(svar[:, :], pS2[:, :], 1.0 / D, svar[:, :],
                                        ALU.mult, ALU.subtract)
                ve.tensor_scalar_add(svar[:, :], svar[:, :], LN_EPS)
            K[("var", g)] = ve.n

            @block.scalar
            def _(s, g=g):
                se.wait(ve, K[("var", g)])
                se.activation(srs[:, 0:GWD], svar[:, :], AF.Ln)
                se.activation(srs[:, 0:GWD], srs[:, 0:GWD], AF.Exp, scale=-0.5)
            K[("rs", g)] = se.n

            @block.vector
            def _(v, g=g, go=go):
                ve.wait(se, K[("rs", g)])
                for k in range(16):
                    ve.tensor_sub(sx[:, k * GWD:(k + 1) * GWD], sx[:, k * GWD:(k + 1) * GWD],
                                  sm[:, :])
                    ve.tensor_mul(sxn[:, k * TC + go:k * TC + go + GWD],
                                  sx[:, k * GWD:(k + 1) * GWD], srs[:, 0:GWD])
            K[("xn", g)] = ve.n

        for it in range(NIT):
            pp = it % 2

            @block.gpsimd
            def _(g, it=it, pp=pp):
                if it >= 2:
                    gq.wait(pe, K[("gum", it - 2)])
                gq.start(sgw[:, pp * 2048:(pp + 1) * 2048],
                         gwT[:, it * 128:(it + 1) * 128].rearrange("(k p) e -> p k e", p=128),
                         sem=ldg[pp])
                gq.start(suw[:, pp * 2048:(pp + 1) * 2048],
                         uwT[:, it * 128:(it + 1) * 128].rearrange("(k p) e -> p k e", p=128),
                         sem=ldu[pp])

            for grp in range(NG):
                @block.tensor
                def _(t, it=it, pp=pp, grp=grp):
                    if grp == 0:
                        pe.wait(ldg[pp], ldg[pp].base + 16 * (it // 2 + 1))
                        pe.wait(ldu[pp], ldu[pp].base + 16 * (it // 2 + 1))
                        if it == 0:
                            pe.wait(ve, K[("xn", NG - 1)])
                    prev = K.get(("h", it, grp - 2) if grp >= 2 else ((it - 1, grp + 2)
                                 and ("h", it - 1, grp + 2)))
                    if prev:
                        pe.wait(ve, prev)
                    for k in range(16):
                        pe.matmul(pg[:, (grp % 2) * 512:(grp % 2 + 1) * 512],
                                  sgw[:, pp * 2048 + k * 128:pp * 2048 + (k + 1) * 128],
                                  sxn[:, k * TC + grp * GWD:k * TC + (grp + 1) * GWD],
                                  start=(k == 0), stop=(k == 15))
                        i = pe.matmul(pu[:, (grp % 2) * 512:(grp % 2 + 1) * 512],
                                      suw[:, pp * 2048 + k * 128:pp * 2048 + (k + 1) * 128],
                                      sxn[:, k * TC + grp * GWD:k * TC + (grp + 1) * GWD],
                                      start=(k == 0), stop=(k == 15))
                    K[("gum", it, grp)] = pe.done(i)
                    if grp == NG - 1:
                        K[("gum", it)] = pe.n

                @block.scalar
                def _(s, it=it, grp=grp):
                    se.wait(pe, K[("gum", it, grp)])
                    se.activation(sg[:, grp * GWD:(grp + 1) * GWD],
                                  pg[:, (grp % 2) * 512:(grp % 2 + 1) * 512], AF.Silu,
                                  scale=sgs[:, it:it + 1])
                    K[("sg", it, grp)] = se.n

                @block.vector
                def _(v, it=it, grp=grp):
                    ve.wait(se, K[("sg", it, grp)])
                    if grp == 0 and it >= 1:
                        ve.wait(sth[0], K[("sth", it - 1)])
                    ve.scalar_tensor_tensor(sh[:, grp * GWD:(grp + 1) * GWD],
                                            pu[:, (grp % 2) * 512:(grp % 2 + 1) * 512],
                                            sus[:, it:it + 1],
                                            sg[:, grp * GWD:(grp + 1) * GWD],
                                            ALU.mult, ALU.mult)
                K[("h", it, grp)] = ve.n

            @block.sync
            def _(s, it=it):
                sq.wait(ve, K[("h", it, NG - 1)])
                sq.start(h_d[it * 128:(it + 1) * 128, :], sh[:, 0:TC], sem=sth[0])
            K[("sth", it)] = sth[0].n

        for grp in range(NG):
            go = grp * GWD

            @block.sync
            def _(s, grp=grp, go=go):
                if grp == 0:
                    s.wait_ge(sth[0].h, sth[0].n)
                else:
                    sq.wait(pe, K[("dmm", grp - 1, 15)])
                for it in range(NIT):
                    sq.start(shl[:, it * GWD:(it + 1) * GWD],
                             h_d[it * 128:(it + 1) * 128, go:go + GWD], sem=ldh[grp % 2])
            K[("ldh", grp)] = ldh[grp % 2].n

            for ot in range(16):
                gi = grp * 16 + ot
                pp = gi % 2

                @block.gpsimd
                def _(g, grp=grp, ot=ot, gi=gi, pp=pp):
                    if gi >= 2:
                        gq.wait(pe, K[("dmm", (gi - 2) // 16, (gi - 2) % 16)])
                    gq.start(sdw[:, pp * NIT * 128:(pp + 1) * NIT * 128],
                             dwT[:, ot * 128:(ot + 1) * 128]
                             .rearrange("(k p) e -> p k e", p=128),
                             sem=ldd[pp])

                @block.sync
                def _(s, grp=grp, go=go, ot=ot, gi=gi, pp=pp):
                    if gi >= 2:
                        sq.wait(ve, K[("o", gi - 2)])
                    sq.start(sr[:, (gi % 2) * GWD:(gi % 2 + 1) * GWD],
                             x3[ot * 128:(ot + 1) * 128, go:go + GWD], sem=ldr[gi % 2])
                K[("ldr", gi)] = ldr[gi % 2].n

                @block.tensor
                def _(t, grp=grp, ot=ot, gi=gi, pp=pp):
                    pe.wait(ldd[pp], ldd[pp].base + 16 * (gi // 2 + 1))
                    if ot == 0:
                        pe.wait(ldh[grp % 2], K[("ldh", grp)])
                    if gi >= 2:
                        pe.wait(ve, K[("o", gi - 2)])
                    for k in range(NIT):
                        i = pe.matmul(pd[:, (gi % 2) * 512:(gi % 2 + 1) * 512],
                                      sdw[:, pp * NIT * 128 + k * 128:pp * NIT * 128 + (k + 1) * 128],
                                      shl[:, k * GWD:(k + 1) * GWD],
                                      start=(k == 0), stop=(k == NIT - 1))
                    K[("dmm", grp, ot)] = pe.done(i)

                @block.vector
                def _(v, grp=grp, ot=ot, gi=gi):
                    ve.wait(pe, K[("dmm", grp, ot)])
                    ve.wait(ldr[gi % 2], K[("ldr", gi)])
                    if gi >= 2:
                        ve.wait(stx[gi % 2], stx[gi % 2].base + 16 * (gi // 2))
                    ve.scalar_tensor_tensor(so[:, (gi % 2) * GWD:(gi % 2 + 1) * GWD],
                                            pd[:, (gi % 2) * 512:(gi % 2 + 1) * 512],
                                            sds[:, ot:ot + 1],
                                            sr[:, (gi % 2) * GWD:(gi % 2 + 1) * GWD],
                                            ALU.mult, ALU.add)
                K[("o", gi)] = ve.n

                @block.sync
                def _(s, grp=grp, go=go, ot=ot, gi=gi):
                    sq.wait(ve, K[("o", gi)])
                    sq.start(x4_o[ot * 128:(ot + 1) * 128, go:go + GWD],
                             so[:, (gi % 2) * GWD:(gi % 2 + 1) * GWD], sem=stx[gi % 2])

        @block.sync
        def _(s):
            for sem in (stx[0], stx[1], sth[0], ldc, ldr[0], ldr[1], ldh[0], ldh[1]):
                s.wait_ge(sem.h, sem.n)

        @block.gpsimd
        def _(g):
            for sem in (ldx, ldg[0], ldg[1], ldu[0], ldu[1], ldd[0], ldd[1]):
                g.wait_ge(sem.h, sem.n)



def host_ffn(x3_all, gate_w, up_w, down_w, gate_s, up_s, down_s):
    gwT = _f32(gate_w.T)
    uwT = _f32(up_w.T)
    dwT = _f32(down_w.T)
    gs_t = _f32(gate_s.reshape(NIT, 128).T)
    us_t = _f32(up_s.reshape(NIT, 128).T)
    ds_t = _f32(down_s.reshape(16, 128).T)
    ones_bf = np.ones((128, 128), dtype=BF)
    in_maps = []
    for c in range(NCORES):
        in_maps.append({
            "x3": _f32(x3_all[c]),
            "gwT": gwT, "uwT": uwT, "dwT": dwT,
            "gs": gs_t, "us": us_t, "ds": ds_t,
            "ones_bf": ones_bf,
        })
    return in_maps


def _standalone(phase_fn, prefix):
    nc = bass.Bass("TRN2", target_bir_lowering=False)
    ctx = contextlib.ExitStack()
    with ctx:
        P = Phase(nc, ctx, prefix)
        phase_fn(P)
    return nc


def build_mamba_front():
    return _standalone(phase_mamba_front, "m1_")


def build_ssd():
    return _standalone(phase_ssd, "m2_")


def build_mgate():
    return _standalone(phase_mgate, "m3_")


def build_moe_route():
    return _standalone(phase_moe_route, "r4_")


def build_moe_expert():
    return _standalone(phase_moe_expert, "r5_")


def build_moe_combine():
    return _standalone(phase_moe_combine, "r6_")


def build_qkv():
    return _standalone(phase_qkv, "q7_")


def build_attn():
    return _standalone(phase_attn, "a8_")


def build_attnout():
    return _standalone(phase_attnout, "o9_")


def build_ffn():
    return _standalone(phase_ffn, "f10_")



_NC_CACHE = {}


def _get(name, builder):
    if name not in _NC_CACHE:
        _NC_CACHE[name] = builder()
    return _NC_CACHE[name]


def kernel(**inputs):
    return kernel_fused(**inputs)


def kernel_unfused(**inputs):
    I = {k: np.asarray(v) for k, v in inputs.items()}
    u_flat = _f32(I["L0_attn_u"].reshape(NTOK, D))
    R = list(range(NCORES))

    r1 = run_bass_kernel_spmd(_get("l1", build_mamba_front),
                              host_mamba_front(u_flat, I["L0_attn_in_proj_w"],
                                               I["L0_attn_conv_w"], I["L0_attn_conv_b"],
                                               I["L0_attn_dt_bias"], I["L0_attn_A_log"]),
                              core_ids=R).results
    r2 = run_bass_kernel_spmd(_get("l2", build_ssd),
                              host_ssd([r1[c]["xc"] for c in R], [r1[c]["Bf"] for c in R],
                                       [r1[c]["Cf"] for c in R], [r1[c]["dtv"] for c in R],
                                       [r1[c]["cumA"] for c in R], I["L0_attn_D"]),
                              core_ids=R).results
    r3 = run_bass_kernel_spmd(_get("l3", build_mgate),
                              host_mgate([r2[c]["y_pm"] for c in R], [r1[c]["zs"] for c in R],
                                         u_flat, I["L0_attn_out_proj_w"], I["L0_attn_norm_w"]),
                              core_ids=R).results
    r4a = run_bass_kernel_spmd(_get("l4a", build_moe_route),
                               host_moe_route([r3[c]["x1_fm"] for c in R], I["L0_ffn_w_gate"],
                                              I["L0_ffn_w_shared_up"], I["L0_ffn_w_shared_down"]),
                               core_ids=R).results
    r4b = run_bass_kernel_spmd(_get("l4b", build_moe_expert),
                               host_moe_expert([r4a[c]["xnt"] for c in R],
                                               [r4a[c]["tbl"] for c in R],
                                               [r4a[c]["cnt"] for c in R],
                                               I["L0_ffn_w_up"], I["L0_ffn_w_down"]),
                               core_ids=R).results
    r4c = run_bass_kernel_spmd(_get("l4c", build_moe_combine),
                               host_moe_combine([r4b[c]["Yout"] for c in R],
                                                [r4a[c]["slot"] for c in R],
                                                [r4a[c]["wv"] for c in R],
                                                [r4a[c]["sht"] for c in R],
                                                [np.asarray(r3[c]["x1_fm"]).T for c in R]),
                               core_ids=R).results
    r5 = run_bass_kernel_spmd(_get("l5", build_qkv),
                              host_qkv([r4c[c]["x2t"] for c in R], I["L1_attn_wq"],
                                       I["L1_attn_wk"], I["L1_attn_wv"]),
                              core_ids=R).results
    r6 = run_bass_kernel_spmd(_get("l6", build_attn),
                              host_attn([r5[c]["qh"] for c in R], [r5[c]["kh"] for c in R],
                                        [r5[c]["vv"] for c in R]),
                              core_ids=R).results
    r7 = run_bass_kernel_spmd(_get("l7", build_attnout),
                              host_attnout([r6[c]["yat"] for c in R], I["L1_attn_wo"],
                                           [np.asarray(r4c[c]["x2t"]).T for c in R]),
                              core_ids=R).results
    r8 = run_bass_kernel_spmd(_get("l8", build_ffn),
                              host_ffn([r7[c]["x3"] for c in R], I["L1_ffn_gate_w"],
                                       I["L1_ffn_up_w"], I["L1_ffn_down_w"],
                                       I["L1_ffn_gate_s"], I["L1_ffn_up_s"],
                                       I["L1_ffn_down_s"]),
                              core_ids=R).results
    out = np.concatenate([np.asarray(r8[c]["x4"], np.float32).T for c in R], 0)
    return out.reshape(B, T, D)



def _emit_f32_transpose(P, src_ap, dst_ap, idn_ap, tag):
    nc = P.nc
    E = P.engines()
    block = P.block
    ve, pe, sq = E.ve, E.pe, E.sq
    R, C = src_ap.shape
    with contextlib.ExitStack() as ctx:
        sb = ctx.enter_context(nc.sbuf_tensor(P.nm(tag + "_sb"), [128, 2 * C], F32))
        st = ctx.enter_context(nc.sbuf_tensor(P.nm(tag + "_st"), [128, 2 * R], F32))
        si = ctx.enter_context(nc.sbuf_tensor(P.nm(tag + "_si"), [128, 128], F32))
        pt = ctx.enter_context(nc.psum_tensor(P.nm(tag + "_pt"), [128, 1024], F32))
        lds = [P.sem("tldA0"), P.sem("tldA1")]
        sts = [P.sem("tstA0"), P.sem("tstA1")]
        ldi = P.sem("tldI")
        K = {}

        @block.sync
        def _(s):
            sq.start(si[:, :], idn_ap, sem=ldi)
        ldi_n = ldi.n

        for rt in range(R // 128):
            pp = rt % 2

            @block.sync
            def _(s, rt=rt, pp=pp):
                if rt >= 2:
                    sq.wait(pe, K[("tr", rt - 2, C // 128 - 1)])
                sq.start(sb[:, pp * C:(pp + 1) * C], src_ap[rt * 128:(rt + 1) * 128, :],
                         sem=lds[pp])
            K[("ld", rt)] = lds[pp].n

            for ct in range(C // 128):
                @block.tensor
                def _(t, rt=rt, ct=ct, pp=pp):
                    if ct == 0:
                        pe.wait(ldi, ldi_n)
                        pe.wait(lds[pp], K[("ld", rt)])
                    prev = K.get(("ev", rt, ct - 1) if ct else (rt - 1, "evl"))
                    if ct == 0:
                        prev = K.get((rt - 1, "evl"))
                    else:
                        prev = K.get(("ev", rt, ct - 1))
                    if ct >= 2:
                        pe.wait(ve, K[("ev", rt, ct - 2)])
                    elif rt >= 1 and ct < 2:
                        pe.wait(ve, K[("ev", rt - 1, C // 128 - 2 + ct)])
                    i = pe.transpose(pt[:, 0:128] if ct % 2 == 0 else pt[:, 512:640],
                                     sb[:, pp * C + ct * 128:pp * C + (ct + 1) * 128],
                                     si[:, :])
                    K[("tr", rt, ct)] = pe.done(i)

                @block.vector
                def _(v, rt=rt, ct=ct, pp=pp):
                    ve.wait(pe, K[("tr", rt, ct)])
                    if ct == 0 and rt >= 2:
                        ve.wait(sts[pp], K[("st", rt - 2)])
                    ve.tensor_copy(st[:, pp * R + ct * 128:pp * R + (ct + 1) * 128],
                                   pt[:, 0:128] if ct % 2 == 0 else pt[:, 512:640])
                K[("ev", rt, ct)] = ve.n
            K[(rt, "evl")] = ve.n

            @block.sync
            def _(s, rt=rt, pp=pp):
                sq.wait(ve, K[(rt, "evl")])
                sq.start(dst_ap[:, rt * 128:(rt + 1) * 128].rearrange("(ct p) q -> p ct q", p=128),
                         st[:, pp * R:(pp + 1) * R].rearrange("p (ct q) -> p ct q", q=128),
                         sem=sts[pp])
            K[("st", rt)] = sts[pp].n

        @block.sync
        def _(s):
            for sem in (sts[0], sts[1], lds[0], lds[1], ldi):
                s.wait_ge(sem.h, sem.n)


def phase_moe_adapt(P):
    nc = P.nc
    xnt = P.dio("xnt", [TC, D], BF16, "int")
    tbl = P.dio("tbl", [EMOE * ROWS_E, 1], F32, "int")
    ag_cnt = P.dio("ag_cnt", [NCORES, EMOE], F32, "int")
    sel64 = P.dio("sel64", [64, 8], F32, "in")
    idnf = P.dio("idnf", [128, 128], F32, "in")
    ag_xnt = P.dio("ag_xnt", [NTOK, D], BF16, "int")
    ag_tbl = P.dio("ag_tbl", [NCORES, EMOE * ROWS_E], F32, "int")
    cntT = P.dio("cntT", [8, 8], F32, "int")
    E = P.engines()
    block = P.block
    ve, pe, gq, sq = E.ve, E.pe, E.gq, E.sq
    with contextlib.ExitStack() as ctx:
        sag = ctx.enter_context(nc.sbuf_tensor(P.nm("sag"), [8, 64], F32))
        sagT = ctx.enter_context(nc.sbuf_tensor(P.nm("sagT"), [64, 8], F32))
        ssel = ctx.enter_context(nc.sbuf_tensor(P.nm("ssel"), [64, 8], F32))
        scT = ctx.enter_context(nc.sbuf_tensor(P.nm("scT"), [8, 8], F32))
        sidn = ctx.enter_context(nc.sbuf_tensor(P.nm("sidn8"), [128, 128], F32))
        pt = ctx.enter_context(nc.psum_tensor(P.nm("ptA"), [128, 512], F32))
        agc = P.sem("agc")
        lds = P.sem("ldsA")
        sts = P.sem("stsA")

        @block.gpsimd
        def _(g):
            gq.collective("AllGather", ALU.bypass, [list(range(NCORES))],
                          [xnt.opt()], [ag_xnt.opt()], sem=agc)
            gq.collective("AllGather", ALU.bypass, [list(range(NCORES))],
                          [tbl.rearrange("r o -> o r").opt()], [ag_tbl.opt()], sem=agc)
        agc_n = agc.n

        @block.sync
        def _(s):
            sq.start(sag[:, :], ag_cnt, sem=lds)
            sq.start(ssel[:, :], sel64, sem=lds)
            sq.start(sidn[:, :], idnf, sem=lds)
        lds_n = lds.n

        @block.tensor
        def _(t):
            pe.wait(lds, lds_n)
            i = pe.transpose(pt[0:64, 0:8], sag[:, :], sidn[0:8, 0:8])
            pe.done(i)
        pe_t = pe.n

        @block.vector
        def _(v):
            ve.wait(pe, pe_t)
            ve.tensor_copy(sagT[:, :], pt[0:64, 0:8])
        ve_t = ve.n

        @block.tensor
        def _(t):
            pe.wait(ve, ve_t)
            i = pe.matmul(pt[0:8, 128:136], ssel[:, :], sagT[:, :], start=True, stop=True)
            pe.done(i)
        pe_m = pe.n

        @block.vector
        def _(v):
            ve.wait(pe, pe_m)
            ve.tensor_copy(scT[:, :], pt[0:8, 128:136])
        ve_m = ve.n

        @block.sync
        def _(s):
            sq.wait(ve, ve_m)
            sq.start(cntT, scT[:, :], sem=sts)
            s.wait_ge(sts.h, sts.n)
            s.wait_ge(agc.h, agc_n)


def phase_attn_adapt(P):
    nc = P.nc
    qh = P.dio("qh", [D, TC], BF16, "int")
    kh = P.dio("kh", [512, TC], BF16, "int")
    vv = P.dio("vv", [512, TC], BF16, "int")
    qgi = P.dio("qgi", [128, 16], I32, "in")
    kgi = P.dio("kgi", [128, 16], I32, "in")
    vgi = P.dio("vgi", [128, 16], I32, "in")
    ident_bf = P.dio("ident_bf", [128, 128], BF16, "in")
    ag_q = P.dio("ag_q", [NCORES * D, TC], BF16, "int")
    ag_k = P.dio("ag_k", [NCORES * 512, TC], BF16, "int")
    ag_v = P.dio("ag_v", [NCORES * 512, TC], BF16, "int")
    qh6 = P.dio("qh6", [512, T], BF16, "int")
    kh6 = P.dio("kh6", [512, T], BF16, "int")
    vp6 = P.dio("vp6", [4 * T, 128], BF16, "int")
    E = P.engines()
    block = P.block
    ve, pe, gq, sq = E.ve, E.pe, E.gq, E.sq
    with contextlib.ExitStack() as ctx:
        sqi = ctx.enter_context(nc.sbuf_tensor(P.nm("sqi"), [128, 16], I32))
        ski = ctx.enter_context(nc.sbuf_tensor(P.nm("ski"), [128, 16], I32))
        svi = ctx.enter_context(nc.sbuf_tensor(P.nm("svi"), [128, 16], I32))
        sga = ctx.enter_context(nc.sbuf_tensor(P.nm("sgaA"), [128, 2 * TC], BF16))
        svt = ctx.enter_context(nc.sbuf_tensor(P.nm("svtA"), [128, 2 * TC], BF16))
        sidb = ctx.enter_context(nc.sbuf_tensor(P.nm("sidbA"), [128, 128], BF16))
        ptb = ctx.enter_context(nc.psum_tensor(P.nm("ptbA"), [128, 2048], BF16))
        agc = P.sem("agc")
        lds = P.sem("ldsB")
        gts = [P.sem("gtsB0"), P.sem("gtsB1")]
        sts = [P.sem("stsB0"), P.sem("stsB1")]
        K = {}

        @block.gpsimd
        def _(g):
            gq.collective("AllGather", ALU.bypass, [list(range(NCORES))],
                          [qh.opt()], [ag_q.opt()], sem=agc)
            gq.collective("AllGather", ALU.bypass, [list(range(NCORES))],
                          [kh.opt()], [ag_k.opt()], sem=agc)
            gq.collective("AllGather", ALU.bypass, [list(range(NCORES))],
                          [vv.opt()], [ag_v.opt()], sem=agc)
        agc_n = agc.n

        @block.sync
        def _(s):
            sq.start(sqi[:, :], qgi, sem=lds)
            sq.start(ski[:, :], kgi, sem=lds)
            sq.start(svi[:, :], vgi, sem=lds)
            sq.start(sidb[:, :], ident_bf, sem=lds)
        lds_n = lds.n

        gi = 0
        for name, idxb, agsrc, dst in (("q", 0, 0, 0), ("k", 1, 1, 1)):
            srcap = [ag_q, ag_k][idxb]
            dstap = [qh6, kh6][idxb]
            ib = [sqi, ski][idxb]
            for j in range(16):
                par = gi % 2

                @block.gpsimd
                def _(g, gi=gi, j=j, par=par, srcap=srcap, ib=ib):
                    if gi == 0:
                        g.wait_ge(agc.h, agc_n)
                        g.wait_ge(lds.h, lds_n)
                    if gi >= 2:
                        g.wait_ge(sts[par].h, sts[par].base + 16 * (gi // 2))
                    gq.indirect(out=sga[:, par * TC:(par + 1) * TC], out_offset=None,
                                in_=srcap,
                                in_offset=IndirectOffsetOnAxis(ib[:, j:j + 1], axis=0),
                                sem=gts[par])
                K[("g", gi)] = gts[par].n

                @block.sync
                def _(s, gi=gi, j=j, par=par, dstap=dstap):
                    s.wait_ge(gts[par].h, K[("g", gi)])
                    un, rq = j // 4, j % 4
                    sq.start(dstap[un * 128:(un + 1) * 128, rq * TC:(rq + 1) * TC],
                             sga[:, par * TC:(par + 1) * TC], sem=sts[par])
                gi += 1

        for j in range(16):
            par = j % 2

            @block.gpsimd
            def _(g, j=j, par=par):
                if j == 0:
                    g.wait_ge(agc.h, agc_n)
                    g.wait_ge(lds.h, lds_n)
                g.wait_ge(sts[par].h, sts[par].base + 16 * (16 + j // 2))
                gq.indirect(out=sga[:, par * TC:(par + 1) * TC], out_offset=None,
                            in_=ag_v,
                            in_offset=IndirectOffsetOnAxis(svi[:, j:j + 1], axis=0),
                            sem=gts[par])
            K[("vg", j)] = gts[par].n

            for st_ in range(16):
                @block.tensor
                def _(t, j=j, par=par, st_=st_):
                    if j == 0 and st_ == 0:
                        pe.wait(lds, lds_n)
                    if st_ == 0:
                        pe.wait(gts[par], K[("vg", j)])
                    prev = K.get(("vev", j, st_ - 2) if st_ >= 2 else
                                 ("vev", j - 1, 14 + st_))
                    if prev:
                        pe.wait(ve, prev)
                    i = pe.transpose(ptb[:, (st_ % 2) * 1024:(st_ % 2) * 1024 + 128],
                                     sga[:, par * TC + st_ * 128:par * TC + (st_ + 1) * 128],
                                     sidb[:, :])
                    K[("vtr", j, st_)] = pe.done(i)

                @block.vector
                def _(v, j=j, par=par, st_=st_):
                    ve.wait(pe, K[("vtr", j, st_)])
                    if st_ == 0 and j >= 2:
                        ve.wait(sts[par], K[("vst", j - 2)])
                    ve.tensor_copy(svt[:, par * TC + st_ * 128:par * TC + (st_ + 1) * 128],
                                   ptb[:, (st_ % 2) * 1024:(st_ % 2) * 1024 + 128])
                K[("vev", j, st_)] = ve.n

            @block.sync
            def _(s, j=j, par=par):
                sq.wait(ve, K[("vev", j, 15)])
                un, rq = j // 4, j % 4
                sq.start(vp6[un * T + rq * TC:un * T + (rq + 1) * TC, :]
                         .rearrange("(st p) d -> p st d", p=128),
                         svt[:, par * TC:(par + 1) * TC].rearrange("p (st d) -> p st d", d=128),
                         sem=sts[par])
            K[("vst", j)] = sts[par].n

        @block.sync
        def _(s):
            for sem in (sts[0], sts[1]):
                s.wait_ge(sem.h, sem.n)


def phase_y_adapt(P):
    nc = P.nc
    yat = P.dio("yat", [512, T], BF16, "int")
    ygi = P.dio("ygi", [128, 16], I32, "in")
    ag_y = P.dio("ag_y", [NCORES * 512, T], BF16, "int")
    ya = P.dio("ya", [D, TC], BF16, "int")
    E = P.engines()
    block = P.block
    ve, gq, sq = E.ve, E.gq, E.sq
    with contextlib.ExitStack() as ctx:
        syi = ctx.enter_context(nc.sbuf_tensor(P.nm("syi"), [128, 16], I32))
        sga = ctx.enter_context(nc.sbuf_tensor(P.nm("sgaY"), [128, 2 * TC], BF16))
        agc = P.sem("agc")
        lds = P.sem("ldsC")
        gts = [P.sem("gtsC0"), P.sem("gtsC1")]
        sts = [P.sem("stsC0"), P.sem("stsC1")]
        K = {}

        @block.gpsimd
        def _(g):
            gq.collective("AllGather", ALU.bypass, [list(range(NCORES))],
                          [yat.opt()], [ag_y.opt()], sem=agc)
        agc_n = agc.n

        @block.sync
        def _(s):
            sq.start(syi[:, :], ygi, sem=lds)
        lds_n = lds.n

        agyv = ag_y.rearrange("r (s t) -> (r s) t", s=4)
        for j in range(16):
            par = j % 2

            @block.gpsimd
            def _(g, j=j, par=par):
                if j == 0:
                    g.wait_ge(agc.h, agc_n)
                    g.wait_ge(lds.h, lds_n)
                if j >= 2:
                    g.wait_ge(sts[par].h, sts[par].base + 16 * (j // 2))
                gq.indirect(out=sga[:, par * TC:(par + 1) * TC], out_offset=None,
                            in_=agyv,
                            in_offset=IndirectOffsetOnAxis(syi[:, j:j + 1], axis=0),
                            sem=gts[par])
            K[("g", j)] = gts[par].n

            @block.sync
            def _(s, j=j, par=par):
                s.wait_ge(gts[par].h, K[("g", j)])
                sq.start(ya[j * 128:(j + 1) * 128, :], sga[:, par * TC:(par + 1) * TC],
                         sem=sts[par])


def phase_yout_ag(P):
    nc = P.nc
    yout = P.dio("Yout", [8 * ROWS_E, D], BF16, "int")
    ag_Y = P.dio("ag_Y", [EMOE * ROWS_E, D], BF16, "int")
    E = P.engines()
    block = P.block
    gq = E.gq
    agc = P.sem("agc")

    @block.gpsimd
    def _(g):
        gq.collective("AllGather", ALU.bypass, [list(range(NCORES))],
                      [yout.opt()], [ag_Y.opt()], sem=agc)
    n = agc.n

    @block.gpsimd
    def _(g):
        g.wait_ge(agc.h, n)



def build_fused(upto=99):
    nc = bass.Bass("TRN2", target_bir_lowering=False)
    TE = TC + 3
    with contextlib.ExitStack() as octx:
        P = Phase(nc, octx, "")

        def mi(name, shape, dt):
            return nc.dram_tensor(name, list(shape), dt, kind="ExternalInput").ap()

        def ms(name, shape, dt, kind="Internal"):
            return nc.dram_tensor(name, list(shape), dt, kind=kind).ap()

        xt = mi("xt", [D, TE], F32)
        wT = mi("wT", [D, D_IN_PROJ], F32)
        cw = mi("cw", [128, 136], F32)
        cb = mi("cb", [128, 34], F32)
        dtb = mi("dtb", [64, 1], F32)
        alog = mi("alog", [64, 1], F32)
        ones_bf = mi("ones_bf", [128, 128], BF16)
        idnf = mi("idnf", [128, 128], F32)
        drow = mi("drow", [1, 64], F32)
        trilm = mi("trilm", [128, 128], F32)
        onecol = mi("onecol", [1, 128], F32)
        bmask = mi("bmask", [8, 8], F32)
        pmask = mi("pmask", [1, 8], F32)
        woT = mi("woT", [D_INNER, D], F32)
        normw = mi("normw", [128, 32], F32)
        wg = mi("wg", [D, EMOE], F32)
        wsu = mi("wsu", [D, 1024], F32)
        wsd = mi("wsd", [128, 64 * 128], F32)
        iot = mi("iot", [128, EMOE], F32)
        trils = mi("trils", [128, 128], F32)
        onek1 = mi("onek1", [1, 128], F32)
        onecl = mi("onecl", [128, 1], F32)
        rankm = mi("rankm", [8, 1], F32)
        tokbase = mi("tokbase", [128, 1], F32)
        ident_bf = mi("ident_bf", [128, 128], BF16)
        sel64 = mi("sel64", [64, 8], F32)
        wup = mi("wup", [8 * D, 512], F32)
        wdn = mi("wdn", [8 * 256, D], F32)
        islot = mi("islot", [8, ROWS_E], F32)
        idn = mi("idn", [128, 128], F32)
        wq = mi("wq", [D, D], F32)
        wk = mi("wk", [D, 512], F32)
        wv_w = mi("wv", [D, 512], F32)
        cosr = mi("cosr", [128, TC], F32)
        sinr = mi("sinr", [128, TC], F32)
        dmask = mi("dmask", [128, 4 * 512], F32)
        qgi = mi("qgi", [128, 16], I32)
        kgi = mi("kgi", [128, 16], I32)
        vgi = mi("vgi", [128, 16], I32)
        ygi = mi("ygi", [128, 16], I32)
        wo = mi("wo", [D, D], F32)
        gwT = mi("gwT", [D, INTER], F32)
        uwT = mi("uwT", [D, INTER], F32)
        dwT = mi("dwT", [INTER, D], F32)
        gs = mi("gs", [128, NIT], F32)
        us = mi("us", [128, NIT], F32)
        ds = mi("ds", [128, 16], F32)

        zs = ms("zs_s", [D_INNER, TC], BF16)
        xc = ms("xc_s", [D_INNER, TC], BF16)
        Bf = ms("Bf_s", [D_STATE, TC], BF16)
        Cf = ms("Cf_s", [D_STATE, TC], BF16)
        dtv = ms("dtv_s", [64, TC], F32)
        cumA = ms("cumA_s", [64, TC], F32)
        xnj = ms("xn_j", [D, TE], BF16)
        y_pm = ms("ypm_s", [TC, D_INNER], BF16)
        x1 = ms("x1_s", [D, TC], F32)
        xnfj = ms("xnf_j", [D, TC], BF16)
        xnt = ms("xnt_s", [TC, D], BF16)
        sht = ms("sht_s", [TC, D], BF16)
        slot = ms("slot_s", [TC, TOPK], F32)
        wv_s = ms("wv_s", [TC, TOPK], F32)
        cntj = ms("cnt_j", [1, EMOE], F32)
        tbl = ms("tbl_s", [EMOE * ROWS_E, 1], F32)
        ag_cnt = ms("agcnt_s", [NCORES, EMOE], F32)
        ag_xnt = ms("agxnt_s", [NTOK, D], BF16)
        ag_tbl = ms("agtbl_s", [NCORES, EMOE * ROWS_E], F32)
        cntT = ms("cntT_s", [8, 8], F32)
        Yout = ms("Yout_s", [8 * ROWS_E, D], BF16)
        ag_Y = ms("agY_s", [EMOE * ROWS_E, D], BF16)
        x1t = ms("x1t_s", [TC, D], F32)
        x2t = ms("x2t_s", [TC, D], F32)
        x2f = ms("x2f_s", [D, TC], F32)
        qh = ms("qh_s", [D, TC], BF16)
        kh = ms("kh_s", [512, TC], BF16)
        vv = ms("vv_s", [512, TC], BF16)
        xn2j = ms("xn2_j", [D, TC], BF16)
        ag_q = ms("agq_s", [NCORES * D, TC], BF16)
        ag_k = ms("agk_s", [NCORES * 512, TC], BF16)
        ag_v = ms("agv_s", [NCORES * 512, TC], BF16)
        qh6 = ms("qh6_s", [512, T], BF16)
        kh6 = ms("kh6_s", [512, T], BF16)
        vp6 = ms("vp6_s", [4 * T, 128], BF16)
        yat = ms("yat_s", [512, T], BF16)
        ag_y = ms("agy_s", [NCORES * 512, T], BF16)
        ya = ms("ya_s", [D, TC], BF16)
        x3 = ms("x3_s", [D, TC], F32)
        x4 = ms("x4", [D, TC], F32, kind="ExternalOutput")

        stage = [0]

        def run(prefix, fn, mapping):
            stage[0] += 1
            if stage[0] > upto:
                return
            P.prefix = prefix
            P.dram = dict(mapping)
            fn(P)
            P.barrier()

        run("m1_", phase_mamba_front,
            dict(xt=xt, wT=wT, cw=cw, cb=cb, dtb=dtb, alog=alog, ones_bf=ones_bf,
                 zs=zs, xc=xc, Bf=Bf, Cf=Cf, dtv=dtv, cumA=cumA, xn=xnj))
        run("m2_", phase_ssd,
            dict(xc=xc, Bf=Bf, Cf=Cf, dtv=dtv, cumA=cumA, idnf=idnf, drow=drow,
                 trilm=trilm, onecol=onecol, bmask=bmask, pmask=pmask, y_pm=y_pm))
        run("m3_", phase_mgate,
            dict(y_pm=y_pm, zs=zs, u_fm=xt[:, 3:3 + TC], woT=woT, normw=normw,
                 ones_bf=ones_bf, x1_fm=x1))
        run("r4_", phase_moe_route,
            dict(x1=x1, wg=wg, wsu=wsu, wsd=wsd, iot=iot, trils=trils, onek1=onek1,
                 onecl=onecl, rankm=rankm, tokbase=tokbase, ident_bf=ident_bf,
                 ones_bf=ones_bf, xnf=xnfj, xnt=xnt, sht=sht, slot=slot, wv=wv_s,
                 cnt=cntj, tbl=tbl, ag_cnt=ag_cnt))
        run("a4_", phase_moe_adapt,
            dict(xnt=xnt, tbl=tbl, ag_cnt=ag_cnt, sel64=sel64, idnf=idnf,
                 ag_xnt=ag_xnt, ag_tbl=ag_tbl, cntT=cntT))
        run("r5_", phase_moe_expert,
            dict(xg=ag_xnt, tbls=ag_tbl, cntT=cntT, wup=wup, wdn=wdn, islot=islot,
                 idn=idn, sel64=sel64, Yout=Yout))
        run("a5_", phase_yout_ag, dict(Yout=Yout, ag_Y=ag_Y))
        stage[0] += 1
        if stage[0] <= upto:
            P.prefix = "a5b_"
            P.dram = {}
            _emit_f32_transpose(P, x1, x1t, idnf, "x1t")
            P.barrier()
        run("r6_", phase_moe_combine,
            dict(yall=ag_Y, slot=slot, wv=wv_s, sht=sht, x1t=x1t, x2t=x2t))
        stage[0] += 1
        if stage[0] <= upto:
            P.prefix = "a6_"
            P.dram = {}
            _emit_f32_transpose(P, x2t, x2f, idnf, "x2f")
            P.barrier()
        run("q7_", phase_qkv,
            dict(x2=x2f, wq=wq, wk=wk, wv=wv_w, cosr=cosr, sinr=sinr, ones_bf=ones_bf,
                 qh=qh, kh=kh, vv=vv, xn2=xn2j))
        run("a7_", phase_attn_adapt,
            dict(qh=qh, kh=kh, vv=vv, qgi=qgi, kgi=kgi, vgi=vgi, ident_bf=ident_bf,
                 ag_q=ag_q, ag_k=ag_k, ag_v=ag_v, qh6=qh6, kh6=kh6, vp6=vp6))
        run("a8_", phase_attn,
            dict(qh=qh6, kh=kh6, vp=vp6, dmask=dmask, ones_bf=ones_bf, yat=yat))
        run("a9_", phase_y_adapt, dict(yat=yat, ygi=ygi, ag_y=ag_y, ya=ya))
        run("o9_", phase_attnout, dict(ya=ya, wo=wo, x2=x2f, x3=x3))
        run("f10_", phase_ffn,
            dict(x3=x3, gwT=gwT, uwT=uwT, dwT=dwT, gs=gs, us=us, ds=ds,
                 ones_bf=ones_bf, x4=x4))
    return nc


def host_fused(I):
    u_flat = _f32(np.asarray(I["L0_attn_u"]).reshape(NTOK, D))
    m1 = host_mamba_front(u_flat, I["L0_attn_in_proj_w"], I["L0_attn_conv_w"],
                          I["L0_attn_conv_b"], I["L0_attn_dt_bias"], I["L0_attn_A_log"])
    tril_ge = (np.arange(128)[None, :] >= np.arange(128)[:, None]).astype(np.float32)
    tril_lt = (np.arange(128)[None, :] < np.arange(128)[:, None]).astype(np.float32)
    onec = np.ones((1, 128), np.float32)
    onecl_ = np.ones((128, 1), np.float32)
    idnf = np.eye(128, dtype=np.float32)
    ident_b = idnf.astype(BF)
    woT = _f32(np.asarray(I["L0_attn_out_proj_w"]).T)
    normw = _f32(np.asarray(I["L0_attn_norm_w"]).reshape(32, 128).T)
    wsd_t = np.zeros((128, 64 * 128), np.float32)
    wsdm = np.asarray(I["L0_ffn_w_shared_down"], np.float32)
    for k in range(4):
        for mt in range(16):
            wsd_t[:, (k * 16 + mt) * 128:(k * 16 + mt + 1) * 128] = \
                wsdm[k * 128:(k + 1) * 128, mt * 128:(mt + 1) * 128]
    iotm = np.broadcast_to(np.arange(EMOE, dtype=np.float32), (128, EMOE)).copy()
    islot_ = np.broadcast_to(np.arange(ROWS_E, dtype=np.float32), (8, ROWS_E)).copy()
    half = HD // 2
    inv_freq = 1.0 / (100000.0 ** (np.arange(half, dtype=np.float32) / half))
    MASKV = -10000.0 * np.sqrt(128.0)
    dmask = np.zeros((128, 4 * 512), np.float32)
    for mi_ in range(4):
        q_idx = np.arange(512)[None, :]
        k_idx = (np.arange(128) + mi_ * 128)[:, None]
        dmask[:, mi_ * 512:(mi_ + 1) * 512] = np.where(q_idx >= k_idx, 3e38, MASKV)
    gs_t = _f32(np.asarray(I["L1_ffn_gate_s"]).reshape(NIT, 128).T)
    us_t = _f32(np.asarray(I["L1_ffn_up_s"]).reshape(NIT, 128).T)
    ds_t = _f32(np.asarray(I["L1_ffn_down_s"]).reshape(16, 128).T)
    gwT = _f32(np.asarray(I["L1_ffn_gate_w"]).T)
    uwT = _f32(np.asarray(I["L1_ffn_up_w"]).T)
    dwT = _f32(np.asarray(I["L1_ffn_down_w"]).T)
    w_up = np.asarray(I["L0_ffn_w_up"], np.float32)
    w_dn = np.asarray(I["L0_ffn_w_down"], np.float32)
    pvec = np.arange(128, dtype=np.int64)

    in_maps = []
    for c in range(NCORES):
        p = c % 4
        b0c = (c // 4) * 4
        bm = np.zeros((8, 8), np.float32)
        pm = np.zeros((1, 8), np.float32)
        for s in range(8):
            if b0c <= s < b0c + p:
                pm[0, s] = 1.0
                for k in range(s + 1, b0c + p):
                    bm[k, s] = 1.0
        rk = np.zeros((8, 1), np.float32)
        rk[:c] = 1.0
        tb = (np.arange(128, dtype=np.float32) + c * TC).reshape(128, 1)
        sel = np.zeros((64, 8), np.float32)
        for j in range(8):
            sel[8 * c + j, j] = 1.0
        s0 = c * TC
        t0 = s0 % T
        pos = np.arange(t0, t0 + TC, dtype=np.float32)
        fr = pos[None, :] * inv_freq[:, None]
        qgi = np.zeros((128, 16), np.int64)
        kgi = np.zeros((128, 16), np.int64)
        vgi = np.zeros((128, 16), np.int64)
        ygi = np.zeros((128, 16), np.int64)
        for j in range(16):
            un, rq = j // 4, j % 4
            uid = 4 * c + un
            b, h = uid // 16, uid % 16
            kvh = h // 4
            qgi[:, j] = (4 * b + rq) * D + h * 128 + pvec
            kgi[:, j] = (4 * b + rq) * 512 + kvh * 128 + pvec
            vgi[:, j] = (4 * b + rq) * 512 + kvh * 128 + pvec
        b0 = s0 // T
        for hj in range(16):
            uid = b0 * 16 + hj
            ygi[:, hj] = (uid * 128 + pvec) * 4 + (c % 4)
        in_maps.append({
            **m1[c],
            "idnf": idnf, "drow": _f32(np.asarray(I["L0_attn_D"]).reshape(1, 64)),
            "trilm": tril_ge, "onecol": onec, "bmask": bm, "pmask": pm,
            "woT": woT, "normw": normw,
            "wg": _f32(I["L0_ffn_w_gate"]), "wsu": _f32(I["L0_ffn_w_shared_up"]),
            "wsd": wsd_t, "iot": iotm, "trils": tril_lt, "onek1": onec,
            "onecl": onecl_, "rankm": rk, "tokbase": tb, "ident_bf": ident_b,
            "sel64": sel,
            "wup": _f32(w_up[8 * c:8 * c + 8].reshape(8 * D, 512)),
            "wdn": _f32(w_dn[8 * c:8 * c + 8].reshape(8 * 256, D)),
            "islot": islot_, "idn": idnf,
            "wq": _f32(I["L1_attn_wq"]), "wk": _f32(I["L1_attn_wk"]),
            "wv": _f32(I["L1_attn_wv"]),
            "cosr": _f32(np.concatenate([np.cos(fr), np.cos(fr)], 0)),
            "sinr": _f32(np.concatenate([np.sin(fr), np.sin(fr)], 0)),
            "dmask": dmask,
            "qgi": qgi.astype(np.int32), "kgi": kgi.astype(np.int32),
            "vgi": vgi.astype(np.int32), "ygi": ygi.astype(np.int32),
            "wo": _f32(I["L1_attn_wo"]),
            "gwT": gwT, "uwT": uwT, "dwT": dwT, "gs": gs_t, "us": us_t, "ds": ds_t,
        })
    return in_maps


def kernel_fused(**inputs):
    I = {k: np.asarray(v) for k, v in inputs.items()}
    nc = _get("fused", build_fused)
    res = run_bass_kernel_spmd(nc, host_fused(I), core_ids=list(range(NCORES)))
    out = np.concatenate([np.asarray(res.results[c]["x4"], np.float32).T
                          for c in range(NCORES)], 0)
    return out.reshape(B, T, D)
